# version 01
# baseline (speedup 1.0000x reference)
import math
from contextlib import ExitStack

import numpy as np
import ml_dtypes
import concourse.bass as bass
import concourse.mybir as mybir
from concourse.bass_utils import run_bass_kernel_spmd

F32 = mybir.dt.float32
BF16 = mybir.dt.bfloat16
I32 = mybir.dt.int32
AF = mybir.ActivationFunctionType
ALU = mybir.AluOpType

SCALE = 1.0 / math.sqrt(128.0)
SHIFT = 12.0
EPS = 1e-5
FDMAX = 5632


class Cfg:
    def __init__(self, H, NKV, S, IC, B=4):
        self.H = H
        self.NKV = NKV
        self.G = H // NKV
        self.D = H * 128
        self.DC = self.D // 128
        self.S = S
        self.B = B
        self.TOK = B * S // 8
        assert self.TOK == S // 2
        self.KC = S // 128
        self.KCh = self.TOK // 128
        self.IC = IC
        self.INTER = IC * 128
        self.KVD = NKV * 128
        self.NT = min(512, self.TOK)
        self.TN = self.TOK // self.NT
        self.FT = min(512, self.TOK)
        self.FTN = self.TOK // self.FT
        self.RG = self.D // 8
        self.RDC = self.D // 8
        self.SEC = self.D * self.INTER // 8


FULL = Cfg(H=16, NKV=4, S=2048, IC=43)
SMALL = Cfg(H=2, NKV=1, S=512, IC=2)


def build(cfg, verbose=False, stages=('a', 'f1', 'b', 'f2')):
    H, D, DC, TOK, KC, KCh = cfg.H, cfg.D, cfg.DC, cfg.TOK, cfg.KC, cfg.KCh
    IC, KVD, NT, TN = cfg.IC, cfg.KVD, cfg.NT, cfg.TN
    S, G, NKV, INTER, RG, RDC, SEC = (cfg.S, cfg.G, cfg.NKV, cfg.INTER,
                                      cfg.RG, cfg.RDC, cfg.SEC)
    FT, FTN = cfg.FT, cfg.FTN

    nc = bass.Bass("TRN2", target_bir_lowering=False)

    xin = nc.dram_tensor("xin", [D, TOK], F32, kind="ExternalInput")
    wqkvT = nc.dram_tensor("wqkvT", [D, 3 * D], F32, kind="ExternalInput")
    woT = nc.dram_tensor("woT", [D, D], F32, kind="ExternalInput")
    wqb = nc.dram_tensor("wqb", [D, D], F32, kind="ExternalInput")
    wkb = nc.dram_tensor("wkb", [D, KVD], F32, kind="ExternalInput")
    wvb = nc.dram_tensor("wvb", [D, KVD], F32, kind="ExternalInput")
    wob = nc.dram_tensor("wob", [D, D], F32, kind="ExternalInput")
    cosq = nc.dram_tensor("cosq", [128, TOK], F32, kind="ExternalInput")
    sinq = nc.dram_tensor("sinq", [128, TOK], F32, kind="ExternalInput")
    maskb = nc.dram_tensor("maskb", [S, TOK], BF16, kind="ExternalInput")
    lutin = nc.dram_tensor("lutin", [128, 16], F32, kind="ExternalInput")
    gsl = nc.dram_tensor("gsl", [RG, INTER], I32, kind="ExternalInput")
    usl = nc.dram_tensor("usl", [RG, INTER], I32, kind="ExternalInput")
    dsl = nc.dram_tensor("dsl", [INTER, RDC], I32, kind="ExternalInput")
    y = nc.dram_tensor("y", [D, TOK], F32, kind="ExternalOutput")

    L1 = 2 * D * TOK
    cag1_in = nc.dram_tensor("cag1_in", [L1], BF16)
    cag1_out = nc.dram_tensor("cag1_out", [8, L1], BF16)
    L2 = 3 * SEC
    cag2_in = nc.dram_tensor("cag2_in", [L2], BF16)
    cag2_out = nc.dram_tensor("cag2_out", [8, L2], BF16)
    L3 = 2 * KVD * TOK
    cag3_in = nc.dram_tensor("cag3_in", [L3], BF16)
    cag3_out = nc.dram_tensor("cag3_out", [8, L3], BF16)
    kvloc1 = nc.dram_tensor("kvloc1", [2, L1], BF16)
    kvloc3 = nc.dram_tensor("kvloc3", [2, L3], BF16)
    VOFF1 = D * TOK
    VOFF3 = KVD * TOK

    class Ctr:
        def __init__(self, name):
            self.h = nc.alloc_semaphore(name)
            self.n = 0

        def inc(self, instr, k=1):
            instr.then_inc(self.h, k)
            self.n += k
            return self.n

    sW = Ctr("sW")
    gp = [Ctr("gp0"), Ctr("gp1")]
    sy = [Ctr("sy0"), Ctr("sy1")]
    exL = [Ctr("exL0"), Ctr("exL1")]
    exS = [Ctr("exS0"), Ctr("exS1")]
    sMK = Ctr("sMK")
    sC1 = Ctr("sC1")
    sC2 = Ctr("sC2")
    sC3 = Ctr("sC3")
    sTE = Ctr("sTE")
    sSC = Ctr("sSC")
    sVE = Ctr("sVE")

    _uid = [0]

    def uid(s):
        _uid[0] += 1
        return f"{s}{_uid[0]}"

    def nsl(n):
        return slice(n * NT, (n + 1) * NT)

    def colslice(t, j0, w):
        return t.ap().rearrange("(c p) o -> p c o", p=128)[:, :, j0:j0 + w]

    st = ExitStack()
    with st:
        sb = lambda name, shape, dt: st.enter_context(
            nc.sbuf_tensor(name, shape, dt))
        xsb = sb("xsb", [128, DC, TOK], F32)
        xn = sb("xnb", [128, DC, TOK], BF16)
        qbuf = sb("qbuf", [128, H, TOK], BF16)
        luts = sb("luts", [128, 16], F32)
        ones_col_bf = sb("ones_col_bf", [128, 1], BF16)
        ones_row_f = sb("ones_row_f", [1, 128], F32)
        ones_row_bf = sb("ones_row_bf", [1, 128], BF16)
        negshift = sb("negshift", [128, 1], F32)
        rvec = sb("rvec", [1, TOK], F32)

        with nc.Block() as blk:
            def f_sy(sy):
                sy.dma_start(out=xsb[:], in_=xin.ap().rearrange(
                    "(c p) t -> p c t", p=128)).then_inc(sW.h, 16)
                sW.n += 16
                sy.dma_start(out=luts[:], in_=lutin.ap()).then_inc(sW.h, 16)
                sW.n += 16
            blk.sync(f_sy)

            def f_ve(ve):
                ve.memset(ones_col_bf[:], 1.0)
                ve.memset(ones_row_f[:], 1.0)
                ve.memset(ones_row_bf[:], 1.0)
                ve.memset(negshift[:], -SHIFT)
            blk.vector(f_ve)
        p0_w = sW.n

        def emit_ln(first=False):
            with (
                nc.sbuf_tensor(uid("ln_mu"), [1, TOK], F32) as mu,
                nc.sbuf_tensor(uid("ln_es"), [1, TOK], F32) as es,
                nc.sbuf_tensor(uid("ln_rs"), [1, TOK], F32) as rsd,
                nc.sbuf_tensor(uid("ln_mr"), [1, TOK], F32) as mr,
                nc.sbuf_tensor(uid("ln_sq"), [128, 2, TOK], BF16) as sqb,
                nc.sbuf_tensor(uid("ln_nrm"), [128, 2, TOK], F32) as nrm,
                nc.psum_tensor(uid("ln_sum"), [1, TOK], F32) as sumps,
                nc.psum_tensor(uid("ln_sqp"), [1, TOK], F32) as sqps,
                nc.psum_tensor(uid("ln_br"), [128, TOK], F32) as brps,
                nc.psum_tensor(uid("ln_bm"), [128, TOK], F32) as bmps,
                nc.Block() as blk,
            ):
                ve0, sc0, te0 = sVE.n, sSC.n, sTE.n
                copy_m = lambda c: ve0 + c + 1
                sq_m = lambda c: sc0 + c + 1
                sum_m = lambda c: te0 + c + 1
                stats1_m = ve0 + DC + 1
                sqrt_m = sc0 + DC + 1
                stats2_m = ve0 + DC + 2
                bcast_m = te0 + DC + 1

                def f_ve(ve):
                    if first:
                        ve.wait_ge(sW.h, p0_w)
                    for c in range(DC):
                        i = ve.tensor_copy(xn[:, c], xsb[:, c])
                        sVE.inc(i)
                    ve.wait_ge(sTE.h, sum_m(DC - 1))
                    ve.tensor_scalar(mu[:], sumps[:], 1.0 / D, None, ALU.mult)
                    ve.tensor_scalar(es[:], sqps[:], 1.0 / D, None, ALU.mult)
                    ve.drain()
                    ve.tensor_mul(rsd[:], mu[:], mu[:])
                    ve.drain()
                    ve.tensor_sub(es[:], es[:], rsd[:])
                    ve.drain()
                    i = ve.tensor_scalar(es[:], es[:], EPS, None, ALU.add)
                    sVE.inc(i)
                    ve.wait_ge(sSC.h, sqrt_m)
                    ve.reciprocal(rsd[:], es[:])
                    ve.drain()
                    i = ve.tensor_mul(mr[:], mu[:], rsd[:])
                    sVE.inc(i)
                    for c in range(DC):
                        ve.wait_ge(sTE.h, bcast_m)
                        ve.tensor_mul(nrm[:, c % 2], xsb[:, c], brps[:])
                        ve.drain()
                        i = ve.tensor_sub(xn[:, c], nrm[:, c % 2], bmps[:])
                        sVE.inc(i)
                blk.vector(f_ve)

                def f_sc(se):
                    for c in range(DC):
                        se.wait_ge(sVE.h, copy_m(c))
                        if c >= 2:
                            se.wait_ge(sTE.h, sum_m(c - 2))
                        i = se.activation(sqb[:, c % 2], xn[:, c], AF.Square)
                        sSC.inc(i)
                    se.wait_ge(sVE.h, stats1_m)
                    i = se.activation(es[:], es[:], AF.Sqrt)
                    sSC.inc(i)
                blk.scalar(f_sc)

                def f_te(te):
                    for c in range(DC):
                        te.wait_ge(sVE.h, copy_m(c))
                        te.wait_ge(sSC.h, sq_m(c))
                        for n in range(TN):
                            te.matmul(sumps[:, nsl(n)], ones_col_bf[:],
                                      xn[:, c, nsl(n)],
                                      start=(c == 0), stop=(c == DC - 1))
                        last = None
                        for n in range(TN):
                            last = te.matmul(sqps[:, nsl(n)], ones_col_bf[:],
                                             sqb[:, c % 2, nsl(n)],
                                             start=(c == 0), stop=(c == DC - 1))
                        sTE.inc(last)
                    te.wait_ge(sVE.h, stats2_m)
                    for n in range(TN):
                        te.matmul(brps[:, nsl(n)], ones_row_f[:],
                                  rsd[:, nsl(n)], start=True, stop=True)
                    last = None
                    for n in range(TN):
                        last = te.matmul(bmps[:, nsl(n)], ones_row_f[:],
                                         mr[:, nsl(n)], start=True, stop=True)
                    sTE.inc(last)
                blk.tensor(f_te)

        def emit_proj(wt, jobs, kind, cag_in_t=None, koff=0):
            rope = kind.startswith("rope")
            to_cag = kind.endswith("k")
            with (
                nc.sbuf_tensor(uid("pj_w"), [128, 2, DC, 128], BF16) as wqk,
                nc.sbuf_tensor(uid("pj_kst"), [128, 2, TOK], BF16) as kst,
                nc.sbuf_tensor(uid("pj_rt"), [128, 2, TOK], F32) as rtp,
                nc.sbuf_tensor(uid("pj_ut"), [128, 2, TOK], F32) as utp,
                nc.sbuf_tensor(uid("pj_cos"), [128, TOK], F32) as cosb,
                nc.sbuf_tensor(uid("pj_sin"), [128, TOK], F32) as sinb,
                nc.psum_tensor(uid("pj_ps"), [128, 2, TOK], F32) as pps,
                nc.Block() as blk,
            ):
                n_j = len(jobs)
                ve0, sc0, te0, w0 = sVE.n, sSC.n, sTE.n, sW.n
                g0 = [gp[0].n, gp[1].n]
                s0 = [sy[0].n, sy[1].n]
                tm = lambda j: te0 + j + 1
                em0 = ve0 if rope else sc0
                em = lambda j: em0 + j + 1
                gm = lambda j: g0[j % 2] + 16 * (j // 2 + 1)
                kdma_m = lambda j: s0[j % 2] + 16 * (j // 2 + 1)

                def f_sy(syn):
                    if rope:
                        syn.dma_start(out=cosb[:], in_=cosq.ap()).then_inc(
                            sW.h, 16)
                        sW.n += 16
                        syn.dma_start(out=sinb[:], in_=sinq.ap()).then_inc(
                            sW.h, 16)
                        sW.n += 16
                    if to_cag:
                        eh = sVE.h if rope else sSC.h
                        for j, (oc, hidx) in enumerate(jobs):
                            syn.wait_ge(eh, em(j))
                            dst = cag_in_t.ap()[
                                (koff + hidx * 128) * TOK:
                                (koff + (hidx + 1) * 128) * TOK]
                            dst = dst.rearrange("(p t) -> p t", p=128)
                            i = syn.dma_start(out=dst, in_=kst[:, j % 2])
                            sy[j % 2].inc(i, 16)
                        syn.wait_ge(sy[0].h, sy[0].n)
                        syn.wait_ge(sy[1].h, sy[1].n)
                blk.sync(f_sy)

                def f_gp(g):
                    for j, (oc, hidx) in enumerate(jobs):
                        if j >= 2:
                            g.wait_ge(sTE.h, tm(j - 2))
                        i = g.dma_start(out=wqk[:, j % 2],
                                        in_=colslice(wt, oc * 128, 128))
                        gp[j % 2].inc(i, 16)
                    g.wait_ge(gp[0].h, gp[0].n)
                    g.wait_ge(gp[1].h, gp[1].n)
                blk.gpsimd(f_gp)

                def f_te(te):
                    for j, (oc, hidx) in enumerate(jobs):
                        te.wait_ge(gp[j % 2].h, gm(j))
                        if j >= 2:
                            te.wait_ge(sVE.h if rope else sSC.h, em(j - 2))
                        last = None
                        for dc in range(DC):
                            for n in range(TN):
                                last = te.matmul(
                                    pps[:, j % 2, nsl(n)], wqk[:, j % 2, dc],
                                    xn[:, dc, nsl(n)],
                                    start=(dc == 0), stop=(dc == DC - 1))
                        sTE.inc(last)
                blk.tensor(f_te)

                if rope:
                    def f_ve(ve):
                        if rope:
                            ve.wait_ge(sW.h, sW.n)
                        for j, (oc, hidx) in enumerate(jobs):
                            ve.wait_ge(sTE.h, tm(j))
                            ps = pps[:, j % 2]
                            if to_cag:
                                if j >= 2:
                                    ve.wait_ge(sy[j % 2].h, kdma_m(j - 2))
                                dst = kst[:, j % 2]
                            else:
                                dst = qbuf[:, hidx]
                            t = rtp[:, j % 2]
                            u = utp[:, j % 2]
                            ve.tensor_mul(t, ps, cosb[:])
                            ve.tensor_mul(u[0:64], ps[64:128], sinb[0:64])
                            ve.tensor_mul(u[64:128], ps[0:64], sinb[64:128])
                            ve.drain()
                            ve.tensor_sub(dst[0:64], t[0:64], u[0:64])
                            i = ve.tensor_add(dst[64:128], t[64:128],
                                              u[64:128])
                            sVE.inc(i)
                    blk.vector(f_ve)
                else:
                    def f_sc(se):
                        for j, (oc, hidx) in enumerate(jobs):
                            se.wait_ge(sTE.h, tm(j))
                            if to_cag:
                                if j >= 2:
                                    se.wait_ge(sy[j % 2].h, kdma_m(j - 2))
                                dst = kst[:, j % 2]
                            else:
                                dst = qbuf[:, hidx]
                            i = se.activation(dst, pps[:, j % 2], AF.Copy)
                            sSC.inc(i)
                    blk.scalar(f_sc)

        def emit_vproj(wsrc_fn, out_cols, voff, cag_in_t, Dv):
            NTv = min(512, out_cols)
            NOS = out_cols // NTv
            TC = TOK // 128
            with (
                nc.sbuf_tensor(uid("vp_w"), [128, 2, DC, NTv], BF16) as wv4,
                nc.sbuf_tensor(uid("vp_st"), [128, 2, NTv], BF16) as vst,
                nc.psum_tensor(uid("vp_ps"), [128, 2, NTv], F32) as vps,
                nc.Block() as blk,
            ):
                jobs = [(o, tc) for o in range(NOS) for tc in range(TC)]
                sc0, te0 = sSC.n, sTE.n
                g0 = [gp[0].n, gp[1].n]
                s0 = [sy[0].n, sy[1].n]
                tm = lambda j: te0 + j + 1
                scm = lambda j: sc0 + j + 1
                gm = lambda o: g0[o % 2] + 16 * (o // 2 + 1)
                vdma_m = lambda j: s0[j % 2] + 16 * (j // 2 + 1)

                def f_gp(g):
                    for o in range(NOS):
                        if o >= 2:
                            g.wait_ge(sTE.h, te0 + (o - 1) * TC)
                        i = g.dma_start(out=wv4[:, o % 2],
                                        in_=wsrc_fn(o * NTv, NTv))
                        gp[o % 2].inc(i, 16)
                    g.wait_ge(gp[0].h, gp[0].n)
                    g.wait_ge(gp[1].h, gp[1].n)
                blk.gpsimd(f_gp)

                def f_te(te):
                    for j, (o, tc) in enumerate(jobs):
                        if tc == 0:
                            te.wait_ge(gp[o % 2].h, gm(o))
                        if j >= 2:
                            te.wait_ge(sSC.h, scm(j - 2))
                        last = None
                        for dc in range(DC):
                            last = te.matmul(
                                vps[:, j % 2], xn[:, dc, tc * 128:(tc + 1) * 128],
                                wv4[:, o % 2, dc],
                                start=(dc == 0), stop=(dc == DC - 1))
                        sTE.inc(last)
                blk.tensor(f_te)

                def f_sc(se):
                    for j, (o, tc) in enumerate(jobs):
                        se.wait_ge(sTE.h, tm(j))
                        if j >= 2:
                            se.wait_ge(sy[j % 2].h, vdma_m(j - 2))
                        i = se.activation(vst[:, j % 2], vps[:, j % 2],
                                          AF.Copy)
                        sSC.inc(i)
                blk.scalar(f_sc)

                def f_sy(syn):
                    for j, (o, tc) in enumerate(jobs):
                        syn.wait_ge(sSC.h, scm(j))
                        dst = cag_in_t.ap()[voff:voff + TOK * Dv]
                        dst = dst.rearrange("(t d) -> t d", d=Dv)
                        dst = dst[tc * 128:(tc + 1) * 128,
                                  o * NTv:(o + 1) * NTv]
                        i = syn.dma_start(out=dst, in_=vst[:, j % 2])
                        sy[j % 2].inc(i, 16)
                    syn.wait_ge(sy[0].h, sy[0].n)
                    syn.wait_ge(sy[1].h, sy[1].n)
                blk.sync(f_sy)

        def emit_ag(cin, cout, ctr, waits):
            with nc.Block() as blk:
                def f_gp(g):
                    for h, v in waits:
                        g.wait_ge(h, v)
                    i = g.collective_compute(
                        "AllGather", ALU.bypass,
                        replica_groups=[list(range(8))],
                        ins=[cin.ap().opt()], outs=[cout.ap().opt()])
                    ctr.inc(i)
                blk.gpsimd(f_gp)

        def expansion_tiles():
            tiles = []
            for sec, src_t in ((0, gsl), (1, usl)):
                r0 = 0
                while r0 < RG:
                    npr = min(128, RG - r0)
                    src = src_t.ap()[r0:r0 + npr, :]
                    dst = cag2_in.ap()[sec * SEC + r0 * INTER:
                                       sec * SEC + (r0 + npr) * INTER]
                    dst = dst.rearrange("(p i) -> p i", i=INTER)
                    tiles.append((src, npr, INTER, dst))
                    r0 += npr
            amax = max(1, FDMAX // RDC)
            a0 = 0
            while a0 < IC:
                na = min(amax, IC - a0)
                src = dsl.ap().rearrange("(a p) c -> p a c", p=128)
                src = src[:, a0:a0 + na, :]
                dst = cag2_in.ap()[2 * SEC + a0 * 128 * RDC:
                                   2 * SEC + (a0 + na) * 128 * RDC]
                dst = dst.rearrange("(a p c) -> p a c", p=128, c=RDC)
                tiles.append((src, 128, na * RDC, dst))
                a0 += na
            return tiles

        def emit_expansion():
            tiles = expansion_tiles()
            nt = len(tiles)
            order = []
            for t in range(nt):
                order.append(("load", t))
                if t >= 1:
                    order.append(("ship", t - 1))
            order.append(("ship", nt - 1))
            load_pos = {}
            ship_pos = {}
            for pos, (op, t) in enumerate(order):
                (load_pos if op == "load" else ship_pos)[t] = pos + 1
            with (
                nc.sbuf_tensor(uid("ex_i"), [128, 2, FDMAX], BF16) as eibf,
                nc.sbuf_tensor(uid("ex_w"), [128, 2, FDMAX], BF16) as ewb,
                nc.sbuf_tensor(uid("ex_t"), [128, FDMAX], BF16) as etb,
                nc.Block() as blk,
            ):
                ve0 = sVE.n
                l0 = [exL[0].n, exL[1].n]
                s0 = [exS[0].n, exS[1].n]
                lm = lambda t: l0[t % 2] + 16 * (t // 2 + 1)
                sm = lambda t: s0[t % 2] + 16 * (t // 2 + 1)

                def f_gp(g):
                    for op, t in order:
                        src, npr, fd, dst = tiles[t]
                        if op == "load":
                            if t >= 2:
                                g.wait_ge(sVE.h, ve0 + t - 1)
                            dstv = eibf[0:npr, t % 2, 0:fd]
                            if src.ndim == 3:
                                dstv = dstv.rearrange(
                                    "p (a c) -> p a c", c=src.shape[2])
                            i = g.dma_start(out=dstv, in_=src)
                            exL[t % 2].inc(i, 16)
                        else:
                            g.wait_ge(sVE.h, ve0 + t + 1)
                            srcv = ewb[0:npr, t % 2, 0:fd]
                            if dst.ndim == 3:
                                srcv = srcv.rearrange(
                                    "p (a c) -> p a c", c=dst.shape[2])
                            i = g.dma_start(out=dst, in_=srcv)
                            exS[t % 2].inc(i, 16)
                    for c_ in (exL[0], exL[1], exS[0], exS[1]):
                        g.wait_ge(c_.h, c_.n)
                blk.gpsimd(f_gp)

                def f_ve(ve):
                    ve.wait_ge(sW.h, p0_w)
                    for t in range(nt):
                        src, npr, fd, dst = tiles[t]
                        ve.wait_ge(exL[t % 2].h, lm(t))
                        if t >= 2:
                            ve.wait_ge(exS[t % 2].h, sm(t - 2))
                        ib = eibf[0:npr, t % 2, 0:fd]
                        wb = ewb[0:npr, t % 2, 0:fd]
                        tb = etb[0:npr, 0:fd]
                        ve.tensor_scalar(wb, ib, 0.0, luts[0:npr, 0:1],
                                         ALU.is_equal, ALU.mult)
                        i = None
                        for l in range(1, 16):
                            ve.tensor_scalar(tb, ib, float(l),
                                             luts[0:npr, l:l + 1],
                                             ALU.is_equal, ALU.mult)
                            ve.drain()
                            i = ve.tensor_tensor(wb, wb, tb, ALU.add)
                            ve.drain()
                        sVE.inc(i)
                blk.vector(f_ve)
            return [(exS[0].h, exS[0].n), (exS[1].h, exS[1].n)]

        def emit_attention(kind):
            masked = kind == "b"
            Hq = H
            kvloc = kvloc3 if masked else kvloc1
            cag = cag3_out if masked else cag1_out
            Lr = L3 if masked else L1
            Dv = KVD if masked else D
            voff = VOFF3 if masked else VOFF1
            cc = sC3 if masked else sC1
            groups = NKV if masked else H
            hpg = G if masked else 1
            nbuf = 1 if masked else 2
            with (
                nc.sbuf_tensor(uid("at_kh"), [128, nbuf, S], BF16) as kh,
                nc.sbuf_tensor(uid("at_vh"), [128, nbuf, KC, 128], BF16) as vh,
                nc.sbuf_tensor(uid("at_ot"), [128, TOK], BF16) as otmp,
                nc.psum_tensor(uid("at_st"), [128, 2, TOK], F32) as stp,
                nc.psum_tensor(uid("at_pv"), [128, TOK], F32) as pvp,
                nc.psum_tensor(uid("at_dn"), [1, TOK], F32) as dnp,
                nc.sbuf_tensor(uid("at_pt"), [128, KC, TOK], BF16) as ptb,
                ExitStack() as mst,
            ):
                msk = None
                if masked:
                    msk = mst.enter_context(
                        nc.sbuf_tensor(uid("at_mk"), [128, KC, TOK], BF16))
                with nc.Block() as blk:
                    ve0, sc0, te0 = sVE.n, sSC.n, sTE.n
                    s0 = [sy[0].n, sy[1].n]
                    scph = KC + 1
                    veph = (KC if masked else 0) + 2
                    teph = KC + 2
                    exp_m = lambda h, kc: sc0 + h * scph + kc + 1
                    copy_m = lambda h: sc0 + h * scph + scph
                    msk_m = lambda h, kc: ve0 + h * veph + kc + 1
                    recip_m = lambda h: ve0 + h * veph + veph - 1
                    evac_m = lambda h: ve0 + h * veph + veph
                    st_m = lambda h, kc: te0 + h * teph + kc + 1
                    pv_m = lambda h: te0 + h * teph + KC + 1
                    bc_m = lambda h: te0 + h * teph + KC + 2
                    mask_w = sMK.n + (16 if masked else 0)
                    kv_m = lambda gi: (sy[gi % 2].h,
                                       s0[gi % 2] + 64 * (gi // 2 + 1))

                    def f_sy(syn):
                        pid = syn.partition_id()
                        if masked:
                            i = syn.dma_start(out=msk[:], in_=maskb.ap(
                            ).rearrange("(a p) t -> p a t", p=128))
                            sMK.inc(i, 16)
                        syn.wait_ge(cc.h, cc.n)
                        for half in range(2):
                            rb = syn.compute_val(pid // 2 * 2 + half,
                                                 donate=True)
                            i = syn.dma_start(
                                out=kvloc.ap()[half],
                                in_=cag.ap()[bass.ds(rb, 1)].squeeze(0))
                            sW.inc(i, 16)
                        syn.wait_ge(sW.h, sW.n)
                        for gi in range(groups):
                            if gi >= nbuf:
                                syn.wait_ge(sTE.h,
                                            pv_m((gi - nbuf) * hpg + hpg - 1))
                            gb = gi % nbuf
                            for half in range(2):
                                src = kvloc.ap()[half,
                                                 gi * 128 * TOK:
                                                 (gi + 1) * 128 * TOK]
                                src = src.rearrange("(p t) -> p t", p=128)
                                i = syn.dma_start(
                                    out=kh[:, gb, half * TOK:(half + 1) * TOK],
                                    in_=src)
                                sy[gi % 2].inc(i, 16)
                            for half in range(2):
                                src = kvloc.ap()[half, voff:voff + TOK * Dv]
                                src = src.rearrange(
                                    "(a p d) -> p a d", p=128, d=Dv)
                                src = src[:, :, gi * 128:(gi + 1) * 128]
                                i = syn.dma_start(
                                    out=vh[:, gb, half * KCh:(half + 1) * KCh],
                                    in_=src)
                                sy[gi % 2].inc(i, 16)
                        syn.wait_ge(sy[0].h, sy[0].n)
                        syn.wait_ge(sy[1].h, sy[1].n)
                    blk.sync(f_sy)

                    def f_te(te):
                        for h in range(Hq):
                            gi = h // hpg
                            gb = gi % nbuf
                            te.wait_ge(*kv_m(gi))
                            if h >= 1:
                                te.wait_ge(sSC.h, copy_m(h - 1))
                                te.wait_ge(sVE.h, evac_m(h - 1))
                            for kc in range(KC):
                                if kc >= 2:
                                    te.wait_ge(sSC.h, exp_m(h, kc - 2))
                                last = None
                                for n in range(TN):
                                    last = te.matmul(
                                        stp[:, kc % 2, nsl(n)],
                                        kh[:, gb, kc * 128:(kc + 1) * 128],
                                        qbuf[:, h, nsl(n)],
                                        start=True, stop=True)
                                sTE.inc(last)
                            last = None
                            for kc in range(KC):
                                te.wait_ge(sSC.h, exp_m(h, kc))
                                if masked:
                                    te.wait_ge(sVE.h, msk_m(h, kc))
                                for n in range(TN):
                                    te.matmul(pvp[:, nsl(n)],
                                              vh[:, gb, kc],
                                              ptb[:, kc, nsl(n)],
                                              start=(kc == 0),
                                              stop=(kc == KC - 1))
                                for n in range(TN):
                                    last = te.matmul(dnp[:, nsl(n)],
                                                     ones_col_bf[:],
                                                     ptb[:, kc, nsl(n)],
                                                     start=(kc == 0),
                                                     stop=(kc == KC - 1))
                            sTE.inc(last)
                            te.wait_ge(sVE.h, recip_m(h))
                            last = None
                            for n in range(TN):
                                last = te.matmul(stp[:, 0, nsl(n)],
                                                 ones_row_f[:],
                                                 rvec[:, nsl(n)],
                                                 start=True, stop=True)
                            sTE.inc(last)
                    blk.tensor(f_te)

                    def f_sc(se):
                        for h in range(Hq):
                            for kc in range(KC):
                                se.wait_ge(sTE.h, st_m(h, kc))
                                i = se.activation(ptb[:, kc], stp[:, kc % 2],
                                                  AF.Exp, bias=negshift[:],
                                                  scale=SCALE)
                                sSC.inc(i)
                            se.wait_ge(sTE.h, pv_m(h))
                            if h >= 1:
                                se.wait_ge(sVE.h, evac_m(h - 1))
                            i = se.activation(otmp[:], pvp[:], AF.Copy)
                            sSC.inc(i)
                    blk.scalar(f_sc)

                    def f_ve(ve):
                        if masked:
                            ve.wait_ge(sMK.h, mask_w)
                        
                        for h in range(Hq):
                            if masked:
                                for kc in range(KC):
                                    ve.wait_ge(sSC.h, exp_m(h, kc))
                                    i = ve.tensor_mul(ptb[:, kc], ptb[:, kc],
                                                      msk[:, kc])
                                    sVE.inc(i)
                            ve.wait_ge(sTE.h, pv_m(h))
                            i = ve.reciprocal(rvec[:], dnp[:])
                            sVE.inc(i)
                            ve.wait_ge(sTE.h, bc_m(h))
                            ve.wait_ge(sSC.h, copy_m(h))
                            i = ve.tensor_mul(qbuf[:, h], otmp[:], stp[:, 0])
                            sVE.inc(i)
                    blk.vector(f_ve)

        def emit_outproj(wt):
            with (
                nc.sbuf_tensor(uid("op_w"), [128, 2, DC, 128], BF16) as wti,
                nc.psum_tensor(uid("op_ps"), [128, 2, TOK], F32) as ops,
                nc.Block() as blk,
            ):
                ve0, te0 = sVE.n, sTE.n
                g0 = [gp[0].n, gp[1].n]
                tm = lambda j: te0 + j + 1
                em = lambda j: ve0 + j + 1
                gm = lambda j: g0[j % 2] + 16 * (j // 2 + 1)

                def f_gp(g):
                    for j in range(DC):
                        if j >= 2:
                            g.wait_ge(sTE.h, tm(j - 2))
                        i = g.dma_start(out=wti[:, j % 2],
                                        in_=colslice(wt, j * 128, 128))
                        gp[j % 2].inc(i, 16)
                    g.wait_ge(gp[0].h, gp[0].n)
                    g.wait_ge(gp[1].h, gp[1].n)
                blk.gpsimd(f_gp)

                def f_te(te):
                    for j in range(DC):
                        te.wait_ge(gp[j % 2].h, gm(j))
                        if j >= 2:
                            te.wait_ge(sVE.h, em(j - 2))
                        last = None
                        for oc in range(H):
                            for n in range(TN):
                                last = te.matmul(
                                    ops[:, j % 2, nsl(n)], wti[:, j % 2, oc],
                                    qbuf[:, oc, nsl(n)],
                                    start=(oc == 0), stop=(oc == H - 1))
                        sTE.inc(last)
                blk.tensor(f_te)

                def f_ve(ve):
                    for j in range(DC):
                        ve.wait_ge(sTE.h, tm(j))
                        i = ve.tensor_add(xsb[:, j], xsb[:, j],
                                          ops[:, j % 2])
                        sVE.inc(i)
                blk.vector(f_ve)

        def emit_ffn(wait_c2):
            fsl = lambda n: slice(n * FT, (n + 1) * FT)
            with nc.sbuf_tensor(uid("ff_h"), [128, IC, FT], BF16) as hbuf:
                for th in range(FTN):
                    t0 = th * FT
                    with (
                        nc.sbuf_tensor(uid("ff_wg"), [128, 2, DC, 128], BF16) as wg,
                        nc.sbuf_tensor(uid("ff_wu"), [128, 2, DC, 128], BF16) as wu,
                        nc.sbuf_tensor(uid("ff_st"), [128, 2, FT], F32) as stmp,
                        nc.psum_tensor(uid("ff_g"), [128, 2, FT], F32) as gps,
                        nc.psum_tensor(uid("ff_u"), [128, 2, FT], F32) as ups,
                        nc.Block() as blk,
                    ):
                        ve0, sc0, te0 = sVE.n, sSC.n, sTE.n
                        s0 = [sy[0].n, sy[1].n]
                        tm = lambda oc: te0 + oc + 1
                        sim = lambda oc: sc0 + oc + 1
                        hm = lambda oc: ve0 + oc + 1
                        ndg = 8
                        wm = lambda oc: (sy[oc % 2].h,
                                         s0[oc % 2]
                                         + 16 * 2 * ndg * (oc // 2 + 1))

                        def f_sy(syn):
                            if wait_c2:
                                syn.wait_ge(sC2.h, sC2.n)
                            for oc in range(IC):
                                if oc >= 2:
                                    syn.wait_ge(sTE.h, tm(oc - 2))
                                for sec, dstb in ((0, wg), (1, wu)):
                                    if RG % 128 == 0:
                                        rgc = RG // 128
                                        for r in range(8):
                                            src = cag2_out.ap()[
                                                r, sec * SEC:
                                                sec * SEC + RG * INTER]
                                            src = src.rearrange(
                                                "(a p i) -> p a i",
                                                p=128, i=INTER)
                                            src = src[:, :, oc * 128:
                                                      (oc + 1) * 128]
                                            dst = dstb[:, oc % 2,
                                                       r * rgc:(r + 1) * rgc]
                                            i = syn.dma_start(out=dst,
                                                              in_=src)
                                            sy[oc % 2].inc(i, 16)
                                    else:
                                        for r in range(8):
                                            src = cag2_out.ap()[
                                                r, sec * SEC:
                                                sec * SEC + RG * INTER]
                                            src = src.rearrange(
                                                "(a i) -> a i", i=INTER)
                                            src = src[:, oc * 128:
                                                      (oc + 1) * 128]
                                            row0 = r * RG
                                            dst = dstb[
                                                row0 % 128:row0 % 128 + RG,
                                                oc % 2, row0 // 128]
                                            i = syn.dma_start(out=dst, in_=src)
                                            sy[oc % 2].inc(i, 16)
                            syn.wait_ge(sy[0].h, sy[0].n)
                            syn.wait_ge(sy[1].h, sy[1].n)
                        blk.sync(f_sy)

                        def f_te(te):
                            for oc in range(IC):
                                te.wait_ge(*wm(oc))
                                if oc >= 2:
                                    te.wait_ge(sSC.h, sim(oc - 2))
                                    te.wait_ge(sVE.h, hm(oc - 2))
                                for dc in range(DC):
                                    te.matmul(gps[:, oc % 2], wg[:, oc % 2, dc],
                                              xn[:, dc, t0:t0 + FT],
                                              start=(dc == 0),
                                              stop=(dc == DC - 1))
                                last = None
                                for dc in range(DC):
                                    last = te.matmul(
                                        ups[:, oc % 2], wu[:, oc % 2, dc],
                                        xn[:, dc, t0:t0 + FT],
                                        start=(dc == 0), stop=(dc == DC - 1))
                                sTE.inc(last)
                        blk.tensor(f_te)

                        def f_sc(se):
                            for oc in range(IC):
                                se.wait_ge(sTE.h, tm(oc))
                                i = se.activation(stmp[:, oc % 2],
                                                  gps[:, oc % 2], AF.Silu)
                                sSC.inc(i)
                        blk.scalar(f_sc)

                        def f_ve(ve):
                            for oc in range(IC):
                                ve.wait_ge(sSC.h, sim(oc))
                                i = ve.tensor_mul(hbuf[:, oc],
                                                  stmp[:, oc % 2],
                                                  ups[:, oc % 2])
                                sVE.inc(i)
                        blk.vector(f_ve)

                    with (
                        nc.sbuf_tensor(uid("ff_wd"), [128, 2, IC, 128], BF16) as wd,
                        nc.psum_tensor(uid("ff_d"), [128, 2, FT], F32) as dps,
                        nc.Block() as blk,
                    ):
                        ve0, te0 = sVE.n, sTE.n
                        s0 = [sy[0].n, sy[1].n]
                        tm = lambda dj: te0 + dj + 1
                        em = lambda dj: ve0 + dj + 1

                        def down_dmas(dj, dstb):
                            out = []
                            col0 = dj * 128
                            done = 0
                            while done < 128:
                                c = col0 + done
                                r = c // RDC
                                lc = c % RDC
                                w = min(RDC - lc, 128 - done)
                                src = cag2_out.ap()[r, 2 * SEC:3 * SEC]
                                src = src.rearrange("(a p c) -> p a c",
                                                    p=128, c=RDC)
                                src = src[:, :, lc:lc + w]
                                dst = dstb[:, :, done:done + w]
                                out.append((dst, src))
                                done += w
                            return out

                        def f_sy(syn):
                            for dj in range(DC):
                                if dj >= 2:
                                    syn.wait_ge(sTE.h, tm(dj - 2))
                                for dst, src in down_dmas(dj, wd[:, dj % 2]):
                                    i = syn.dma_start(out=dst, in_=src)
                                    sy[dj % 2].inc(i, 16)
                            syn.wait_ge(sy[0].h, sy[0].n)
                            syn.wait_ge(sy[1].h, sy[1].n)
                        blk.sync(f_sy)

                        wm_pred = []
                        acc = [s0[0], s0[1]]
                        for dj in range(DC):
                            acc[dj % 2] += 16 * len(down_dmas(dj, wd[:, 0]))
                            wm_pred.append((sy[dj % 2].h, acc[dj % 2]))

                        def f_te(te):
                            for dj in range(DC):
                                te.wait_ge(*wm_pred[dj])
                                if dj >= 2:
                                    te.wait_ge(sVE.h, em(dj - 2))
                                last = None
                                for oc in range(IC):
                                    last = te.matmul(
                                        dps[:, dj % 2], wd[:, dj % 2, oc],
                                        hbuf[:, oc],
                                        start=(oc == 0), stop=(oc == IC - 1))
                                sTE.inc(last)
                        blk.tensor(f_te)

                        def f_ve(ve):
                            for dj in range(DC):
                                ve.wait_ge(sTE.h, tm(dj))
                                i = ve.tensor_add(xsb[:, dj, t0:t0 + FT],
                                                  xsb[:, dj, t0:t0 + FT],
                                                  dps[:, dj % 2])
                                sVE.inc(i)
                        blk.vector(f_ve)

        first_ln = [True]

        def ln():
            emit_ln(first=first_ln[0])
            first_ln[0] = False

        need_ffn = ("f1" in stages) or ("f2" in stages)
        if "a" in stages:
            ln()
            jobs_qk = [(h, h) for h in range(H)]
            emit_proj(wqkvT, jobs_qk, "ropeq")
            jobs_k = [(H + h, h) for h in range(H)]
            emit_proj(wqkvT, jobs_k, "ropek", cag1_in, 0)
            emit_vproj(lambda c0, w: colslice(wqkvT, 2 * H * 128 + c0, w),
                       D, VOFF1, cag1_in, D)
            emit_ag(cag1_in, cag1_out, sC1,
                    [(sy[0].h, sy[0].n), (sy[1].h, sy[1].n)])
        if need_ffn:
            exp_waits = emit_expansion()
            emit_ag(cag2_in, cag2_out, sC2, exp_waits)
        if "a" in stages:
            emit_attention("a")
            emit_outproj(woT)
        if "f1" in stages:
            ln()
            emit_ffn(wait_c2=True)
        if "b" in stages:
            ln()
            jobs_qb = [(h, h) for h in range(H)]
            emit_proj(wqb, jobs_qb, "copyq")
            jobs_kb = [(g, g) for g in range(NKV)]
            emit_proj(wkb, jobs_kb, "copyk", cag3_in, 0)
            emit_vproj(lambda c0, w: wvb.ap().rearrange(
                "(c p) o -> p c o", p=128)[:, :, c0:c0 + w],
                KVD, VOFF3, cag3_in, KVD)
            emit_ag(cag3_in, cag3_out, sC3,
                    [(sy[0].h, sy[0].n), (sy[1].h, sy[1].n)])
            emit_attention("b")
            emit_outproj(wob)
        if "f2" in stages:
            ln()
            emit_ffn(wait_c2=("f1" not in stages))

        with nc.Block() as blk:
            def f_sy(sy):
                i = sy.dma_start(out=y.ap().rearrange("(c p) t -> p c t",
                                                      p=128), in_=xsb[:])
                sW.inc(i, 16)
                sy.wait_ge(sW.h, sW.n)
            blk.sync(f_sy)

    if verbose:
        print(f"instructions: {nc.n_instructions()}")
    return nc



def _rope_tables(S):
    inv = 1.0 / (10000.0 ** (np.arange(0, 128, 2, dtype=np.float32) / 128.0))
    t = np.arange(S, dtype=np.float32)
    fr = np.einsum("i,j->ij", t, inv)
    emb = np.concatenate([fr, fr], axis=-1)
    return np.cos(emb).astype(np.float32), np.sin(emb).astype(np.float32)


def make_in_maps(cfg, x, w_qkv, w_out, Wq, Wk, Wv, Wo, lut, gate_idx, up_idx,
                 down_idx):
    D, TOK, S = cfg.D, cfg.TOK, cfg.S
    f32 = lambda a: np.ascontiguousarray(np.asarray(a, dtype=np.float32))
    i32 = lambda a: np.ascontiguousarray(np.asarray(a, dtype=np.int32))
    wqkvT = f32(np.asarray(w_qkv).T)
    woT = f32(np.asarray(w_out).T)
    wq, wk, wv, wo = f32(Wq), f32(Wk), f32(Wv), f32(Wo)
    gT = i32(np.asarray(gate_idx).T)
    uT = i32(np.asarray(up_idx).T)
    dT = i32(np.asarray(down_idx).T)
    lutbc = f32(np.tile(np.asarray(lut, dtype=np.float32)[None, :], (128, 1)))
    cos, sin = _rope_tables(S)
    in_maps = []
    for c in range(8):
        b, hh = c // 2, c % 2
        tok = slice(hh * TOK, (hh + 1) * TOK)
        kk = np.arange(S, dtype=np.int64)[:, None]
        qq = hh * TOK + np.arange(TOK, dtype=np.int64)[None, :]
        mask = (kk <= qq).astype(ml_dtypes.bfloat16)
        in_maps.append({
            "xin": f32(np.asarray(x)[b, tok, :].T),
            "wqkvT": wqkvT, "woT": woT,
            "wqb": wq, "wkb": wk, "wvb": wv, "wob": wo,
            "cosq": f32(cos[tok, :].T), "sinq": f32(sin[tok, :].T),
            "maskb": np.ascontiguousarray(mask),
            "lutin": lutbc,
            "gsl": np.ascontiguousarray(gT[c * cfg.RG:(c + 1) * cfg.RG, :]),
            "usl": np.ascontiguousarray(uT[c * cfg.RG:(c + 1) * cfg.RG, :]),
            "dsl": np.ascontiguousarray(dT[:, c * cfg.RDC:(c + 1) * cfg.RDC]),
        })
    return in_maps


def assemble(cfg, results):
    B, S, D, TOK = cfg.B, cfg.S, cfg.D, cfg.TOK
    out = np.empty((B, S, D), dtype=np.float32)
    for c in range(8):
        b, hh = c // 2, c % 2
        out[b, hh * TOK:(hh + 1) * TOK, :] = results[c]["y"].T
    return out


def kernel(L0_attn_a_x, L0_attn_a_w_qkv, L0_attn_a_w_out, L0_attn_b_Wq,
           L0_attn_b_Wk, L0_attn_b_Wv, L0_attn_b_Wo, L0_ffn_lut,
           L0_ffn_gate_idx, L0_ffn_up_idx, L0_ffn_down_idx):
    cfg = FULL
    nc = build(cfg)
    in_maps = make_in_maps(cfg, L0_attn_a_x, L0_attn_a_w_qkv, L0_attn_a_w_out,
                           L0_attn_b_Wq, L0_attn_b_Wk, L0_attn_b_Wv,
                           L0_attn_b_Wo, L0_ffn_lut, L0_ffn_gate_idx,
                           L0_ffn_up_idx, L0_ffn_down_idx)
    res = run_bass_kernel_spmd(nc, in_maps, core_ids=list(range(8)))
    return assemble(cfg, res.results)


# version 03
# speedup vs baseline: 1.2044x; 1.2044x over previous
import math
from contextlib import ExitStack

import numpy as np
import ml_dtypes
import concourse.bass as bass
import concourse.mybir as mybir
from concourse.bass_utils import run_bass_kernel_spmd

F32 = mybir.dt.float32
BF16 = mybir.dt.bfloat16
I32 = mybir.dt.int32
AF = mybir.ActivationFunctionType
ALU = mybir.AluOpType

SCALE = 1.0 / math.sqrt(128.0)
SHIFT = 12.0
EPS = 1e-5
FDMAX = 2304
PAIRS = [[0, 1], [2, 3], [4, 5], [6, 7]]


class Cfg:
    def __init__(self, H, NKV, S, IC, B=4):
        self.H = H
        self.NKV = NKV
        self.G = H // NKV
        self.D = H * 128
        self.DC = self.D // 128
        self.S = S
        self.B = B
        self.TOK = B * S // 8
        assert self.TOK == S // 2
        self.KC = S // 128
        self.KCh = self.TOK // 128
        self.IC = IC
        self.INTER = IC * 128
        self.KVD = NKV * 128
        self.NT = min(512, self.TOK)
        self.TN = self.TOK // self.NT
        self.FT = min(512, self.TOK)
        self.FTN = self.TOK // self.FT
        self.RG = self.D // 8
        self.RDC = self.D // 8
        self.SEC = self.D * self.INTER // 8


FULL = Cfg(H=16, NKV=4, S=2048, IC=43)
SMALL = Cfg(H=2, NKV=1, S=512, IC=2)


def build(cfg, verbose=False, stages=('a', 'f1', 'b', 'f2')):
    H, D, DC, TOK, KC, KCh = cfg.H, cfg.D, cfg.DC, cfg.TOK, cfg.KC, cfg.KCh
    IC, KVD, NT, TN = cfg.IC, cfg.KVD, cfg.NT, cfg.TN
    S, G, NKV, INTER, RG, RDC, SEC = (cfg.S, cfg.G, cfg.NKV, cfg.INTER,
                                      cfg.RG, cfg.RDC, cfg.SEC)
    FT, FTN = cfg.FT, cfg.FTN

    nc = bass.Bass("TRN2", target_bir_lowering=False)

    xin = nc.dram_tensor("xin", [D, TOK], F32, kind="ExternalInput")
    wqkvT = nc.dram_tensor("wqkvT", [D, 3 * D], F32, kind="ExternalInput")
    woT = nc.dram_tensor("woT", [D, D], F32, kind="ExternalInput")
    wqb = nc.dram_tensor("wqb", [D, D], F32, kind="ExternalInput")
    wkb = nc.dram_tensor("wkb", [D, KVD], F32, kind="ExternalInput")
    wvb = nc.dram_tensor("wvb", [D, KVD], F32, kind="ExternalInput")
    wob = nc.dram_tensor("wob", [D, D], F32, kind="ExternalInput")
    cosq = nc.dram_tensor("cosq", [128, TOK], F32, kind="ExternalInput")
    sinq = nc.dram_tensor("sinq", [128, TOK], F32, kind="ExternalInput")
    maskb = nc.dram_tensor("maskb", [S, TOK], BF16, kind="ExternalInput")
    lutin = nc.dram_tensor("lutin", [128, 16], F32, kind="ExternalInput")
    gsl = nc.dram_tensor("gsl", [RG, INTER], I32, kind="ExternalInput")
    usl = nc.dram_tensor("usl", [RG, INTER], I32, kind="ExternalInput")
    dsl = nc.dram_tensor("dsl", [INTER, RDC], I32, kind="ExternalInput")
    y = nc.dram_tensor("y", [D, TOK], F32, kind="ExternalOutput")

    L1 = 2 * D * TOK
    cag1_in = nc.dram_tensor("cag1_in", [L1], BF16)
    cag1_out = nc.dram_tensor("cag1_out", [2, L1], BF16)
    L2 = 3 * SEC
    cag2_in = nc.dram_tensor("cag2_in", [L2], BF16)
    cag2_out = nc.dram_tensor("cag2_out", [8, L2], BF16, addr_space="Shared")
    L3 = 2 * KVD * TOK
    cag3_in = nc.dram_tensor("cag3_in", [L3], BF16)
    cag3_out = nc.dram_tensor("cag3_out", [2, L3], BF16)
    VOFF1 = D * TOK
    VOFF3 = KVD * TOK

    class Ctr:
        def __init__(self, name):
            self.h = nc.alloc_semaphore(name)
            self.n = 0

        def inc(self, instr, k=1):
            instr.then_inc(self.h, k)
            self.n += k
            return self.n

    sW = Ctr("sW")
    gp = [Ctr("gp0"), Ctr("gp1")]
    sy = [Ctr("sy0"), Ctr("sy1")]
    exL = [Ctr("exL0"), Ctr("exL1")]
    exS = [Ctr("exS0"), Ctr("exS1")]
    sEX = Ctr("sEX")
    sMK = Ctr("sMK")
    sC1 = Ctr("sC1")
    sC2 = Ctr("sC2")
    sC3 = Ctr("sC3")
    sTE = Ctr("sTE")
    sSC = Ctr("sSC")
    sVE = Ctr("sVE")

    _uid = [0]

    def uid(s):
        _uid[0] += 1
        return f"{s}{_uid[0]}"

    def nsl(n):
        return slice(n * NT, (n + 1) * NT)

    def colslice(t, j0, w):
        return t.ap().rearrange("(c p) o -> p c o", p=128)[:, :, j0:j0 + w]

    st = ExitStack()
    with st:
        sb = lambda name, shape, dt: st.enter_context(
            nc.sbuf_tensor(name, shape, dt))
        xsb = sb("xsb", [128, DC, TOK], F32)
        xn = sb("xnb", [128, DC, TOK], BF16)
        qbuf = sb("qbuf", [128, H, TOK], BF16)
        luts = sb("luts", [128, 16], F32)
        ones_col_bf = sb("ones_col_bf", [128, 1], BF16)
        ones_row_f = sb("ones_row_f", [1, 128], F32)
        ones_row_bf = sb("ones_row_bf", [1, 128], BF16)
        negshift = sb("negshift", [128, 1], F32)
        rvec = sb("rvec", [1, TOK], F32)

        with nc.Block() as blk:
            def f_sy(sy):
                sy.dma_start(out=xsb[:], in_=xin.ap().rearrange(
                    "(c p) t -> p c t", p=128)).then_inc(sW.h, 16)
                sW.n += 16
                sy.dma_start(out=luts[:], in_=lutin.ap()).then_inc(sW.h, 16)
                sW.n += 16
            blk.sync(f_sy)

            def f_ve(ve):
                ve.memset(ones_col_bf[:], 1.0)
                ve.memset(ones_row_f[:], 1.0)
                ve.memset(ones_row_bf[:], 1.0)
                ve.memset(negshift[:], -SHIFT)
            blk.vector(f_ve)
        p0_w = sW.n

        def emit_ln(first=False):
            with (
                nc.sbuf_tensor(uid("ln_mu"), [1, TOK], F32) as mu,
                nc.sbuf_tensor(uid("ln_es"), [1, TOK], F32) as es,
                nc.sbuf_tensor(uid("ln_rs"), [1, TOK], F32) as rsd,
                nc.sbuf_tensor(uid("ln_mr"), [1, TOK], F32) as mr,
                nc.sbuf_tensor(uid("ln_sq"), [128, 2, TOK], BF16) as sqb,
                nc.sbuf_tensor(uid("ln_nrm"), [128, 2, TOK], F32) as nrm,
                nc.psum_tensor(uid("ln_sum"), [1, TOK], F32) as sumps,
                nc.psum_tensor(uid("ln_sqp"), [1, TOK], F32) as sqps,
                nc.psum_tensor(uid("ln_br"), [128, TOK], F32) as brps,
                nc.psum_tensor(uid("ln_bm"), [128, TOK], F32) as bmps,
                nc.Block() as blk,
            ):
                ve0, sc0, te0 = sVE.n, sSC.n, sTE.n
                copy_m = lambda c: ve0 + c + 1
                sq_m = lambda c: sc0 + c + 1
                sum_m = lambda c: te0 + c + 1
                stats1_m = ve0 + DC + 1
                sqrt_m = sc0 + DC + 1
                stats2_m = ve0 + DC + 2
                bcast_m = te0 + DC + 1

                def f_ve(ve):
                    if first:
                        ve.wait_ge(sW.h, p0_w)
                    for c in range(DC):
                        i = ve.tensor_copy(xn[:, c], xsb[:, c])
                        sVE.inc(i)
                    ve.wait_ge(sTE.h, sum_m(DC - 1))
                    ve.tensor_scalar(mu[:], sumps[:], 1.0 / D, None, ALU.mult)
                    ve.tensor_scalar(es[:], sqps[:], 1.0 / D, None, ALU.mult)
                    ve.drain()
                    ve.tensor_mul(rsd[:], mu[:], mu[:])
                    ve.drain()
                    ve.tensor_sub(es[:], es[:], rsd[:])
                    ve.drain()
                    i = ve.tensor_scalar(es[:], es[:], EPS, None, ALU.add)
                    sVE.inc(i)
                    ve.wait_ge(sSC.h, sqrt_m)
                    ve.reciprocal(rsd[:], es[:])
                    ve.drain()
                    i = ve.tensor_mul(mr[:], mu[:], rsd[:])
                    sVE.inc(i)
                    for c in range(DC):
                        ve.wait_ge(sTE.h, bcast_m)
                        ve.tensor_mul(nrm[:, c % 2], xsb[:, c], brps[:])
                        ve.drain()
                        i = ve.tensor_sub(xn[:, c], nrm[:, c % 2], bmps[:])
                        sVE.inc(i)
                blk.vector(f_ve)

                def f_sc(se):
                    for c in range(DC):
                        se.wait_ge(sVE.h, copy_m(c))
                        if c >= 2:
                            se.wait_ge(sTE.h, sum_m(c - 2))
                        i = se.activation(sqb[:, c % 2], xn[:, c], AF.Square)
                        sSC.inc(i)
                    se.wait_ge(sVE.h, stats1_m)
                    i = se.activation(es[:], es[:], AF.Sqrt)
                    sSC.inc(i)
                blk.scalar(f_sc)

                def f_te(te):
                    for c in range(DC):
                        te.wait_ge(sVE.h, copy_m(c))
                        te.wait_ge(sSC.h, sq_m(c))
                        for n in range(TN):
                            te.matmul(sumps[:, nsl(n)], ones_col_bf[:],
                                      xn[:, c, nsl(n)],
                                      start=(c == 0), stop=(c == DC - 1))
                        last = None
                        for n in range(TN):
                            last = te.matmul(sqps[:, nsl(n)], ones_col_bf[:],
                                             sqb[:, c % 2, nsl(n)],
                                             start=(c == 0), stop=(c == DC - 1))
                        sTE.inc(last)
                    te.wait_ge(sVE.h, stats2_m)
                    for n in range(TN):
                        te.matmul(brps[:, nsl(n)], ones_row_f[:],
                                  rsd[:, nsl(n)], start=True, stop=True)
                    last = None
                    for n in range(TN):
                        last = te.matmul(bmps[:, nsl(n)], ones_row_f[:],
                                         mr[:, nsl(n)], start=True, stop=True)
                    sTE.inc(last)
                blk.tensor(f_te)

        def emit_proj(wt, jobs, kind, cag_in_t=None, koff=0):
            rope = kind.startswith("rope")
            to_cag = kind.endswith("k")
            with (
                nc.sbuf_tensor(uid("pj_w"), [128, 2, DC, 128], BF16) as wqk,
                nc.sbuf_tensor(uid("pj_kst"), [128, 2, TOK], BF16) as kst,
                nc.sbuf_tensor(uid("pj_rt"), [128, 2, TOK], F32) as rtp,
                nc.sbuf_tensor(uid("pj_ut"), [128, 2, TOK], F32) as utp,
                nc.sbuf_tensor(uid("pj_cos"), [128, TOK], F32) as cosb,
                nc.sbuf_tensor(uid("pj_sin"), [128, TOK], F32) as sinb,
                nc.psum_tensor(uid("pj_ps"), [128, 2, TOK], F32) as pps,
                nc.Block() as blk,
            ):
                n_j = len(jobs)
                ve0, sc0, te0, w0 = sVE.n, sSC.n, sTE.n, sW.n
                g0 = [gp[0].n, gp[1].n]
                s0 = [sy[0].n, sy[1].n]
                tm = lambda j: te0 + j + 1
                em0 = ve0 if rope else sc0
                em = lambda j: em0 + j + 1
                gm = lambda j: g0[j % 2] + 16 * (j // 2 + 1)
                kdma_m = lambda j: s0[j % 2] + 16 * (j // 2 + 1)

                def f_sy(syn):
                    if rope:
                        syn.dma_start(out=cosb[:], in_=cosq.ap()).then_inc(
                            sW.h, 16)
                        sW.n += 16
                        syn.dma_start(out=sinb[:], in_=sinq.ap()).then_inc(
                            sW.h, 16)
                        sW.n += 16
                    if to_cag:
                        eh = sVE.h if rope else sSC.h
                        for j, (oc, hidx) in enumerate(jobs):
                            syn.wait_ge(eh, em(j))
                            dst = cag_in_t.ap()[
                                (koff + hidx * 128) * TOK:
                                (koff + (hidx + 1) * 128) * TOK]
                            dst = dst.rearrange("(p t) -> p t", p=128)
                            i = syn.dma_start(out=dst, in_=kst[:, j % 2])
                            sy[j % 2].inc(i, 16)
                        syn.wait_ge(sy[0].h, sy[0].n)
                        syn.wait_ge(sy[1].h, sy[1].n)
                blk.sync(f_sy)

                def f_gp(g):
                    for j, (oc, hidx) in enumerate(jobs):
                        if j >= 2:
                            g.wait_ge(sTE.h, tm(j - 2))
                        i = g.dma_start(out=wqk[:, j % 2],
                                        in_=colslice(wt, oc * 128, 128))
                        gp[j % 2].inc(i, 16)
                    g.wait_ge(gp[0].h, gp[0].n)
                    g.wait_ge(gp[1].h, gp[1].n)
                blk.gpsimd(f_gp)

                def f_te(te):
                    for j, (oc, hidx) in enumerate(jobs):
                        te.wait_ge(gp[j % 2].h, gm(j))
                        if j >= 2:
                            te.wait_ge(sVE.h if rope else sSC.h, em(j - 2))
                        last = None
                        for dc in range(DC):
                            for n in range(TN):
                                last = te.matmul(
                                    pps[:, j % 2, nsl(n)], wqk[:, j % 2, dc],
                                    xn[:, dc, nsl(n)],
                                    start=(dc == 0), stop=(dc == DC - 1))
                        sTE.inc(last)
                blk.tensor(f_te)

                if rope:
                    def f_ve(ve):
                        if rope:
                            ve.wait_ge(sW.h, sW.n)
                        for j, (oc, hidx) in enumerate(jobs):
                            ve.wait_ge(sTE.h, tm(j))
                            ps = pps[:, j % 2]
                            if to_cag:
                                if j >= 2:
                                    ve.wait_ge(sy[j % 2].h, kdma_m(j - 2))
                                dst = kst[:, j % 2]
                            else:
                                dst = qbuf[:, hidx]
                            t = rtp[:, j % 2]
                            u = utp[:, j % 2]
                            ve.tensor_mul(t, ps, cosb[:])
                            ve.tensor_mul(u[0:64], ps[64:128], sinb[0:64])
                            ve.tensor_mul(u[64:128], ps[0:64], sinb[64:128])
                            ve.drain()
                            ve.tensor_sub(dst[0:64], t[0:64], u[0:64])
                            i = ve.tensor_add(dst[64:128], t[64:128],
                                              u[64:128])
                            sVE.inc(i)
                    blk.vector(f_ve)
                else:
                    def f_sc(se):
                        for j, (oc, hidx) in enumerate(jobs):
                            se.wait_ge(sTE.h, tm(j))
                            if to_cag:
                                if j >= 2:
                                    se.wait_ge(sy[j % 2].h, kdma_m(j - 2))
                                dst = kst[:, j % 2]
                            else:
                                dst = qbuf[:, hidx]
                            i = se.activation(dst, pps[:, j % 2], AF.Copy)
                            sSC.inc(i)
                    blk.scalar(f_sc)

        def emit_vproj(wsrc_fn, out_cols, voff, cag_in_t, Dv):
            NTv = min(512, out_cols)
            NOS = out_cols // NTv
            TC = TOK // 128
            with (
                nc.sbuf_tensor(uid("vp_w"), [128, 2, DC, NTv], BF16) as wv4,
                nc.sbuf_tensor(uid("vp_st"), [128, 2, NTv], BF16) as vst,
                nc.psum_tensor(uid("vp_ps"), [128, 2, NTv], F32) as vps,
                nc.Block() as blk,
            ):
                jobs = [(o, tc) for o in range(NOS) for tc in range(TC)]
                sc0, te0 = sSC.n, sTE.n
                g0 = [gp[0].n, gp[1].n]
                s0 = [sy[0].n, sy[1].n]
                tm = lambda j: te0 + j + 1
                scm = lambda j: sc0 + j + 1
                gm = lambda o: g0[o % 2] + 16 * (o // 2 + 1)
                vdma_m = lambda j: s0[j % 2] + 16 * (j // 2 + 1)

                def f_gp(g):
                    for o in range(NOS):
                        if o >= 2:
                            g.wait_ge(sTE.h, te0 + (o - 1) * TC)
                        i = g.dma_start(out=wv4[:, o % 2],
                                        in_=wsrc_fn(o * NTv, NTv))
                        gp[o % 2].inc(i, 16)
                    g.wait_ge(gp[0].h, gp[0].n)
                    g.wait_ge(gp[1].h, gp[1].n)
                blk.gpsimd(f_gp)

                def f_te(te):
                    for j, (o, tc) in enumerate(jobs):
                        if tc == 0:
                            te.wait_ge(gp[o % 2].h, gm(o))
                        if j >= 2:
                            te.wait_ge(sSC.h, scm(j - 2))
                        last = None
                        for dc in range(DC):
                            last = te.matmul(
                                vps[:, j % 2], xn[:, dc, tc * 128:(tc + 1) * 128],
                                wv4[:, o % 2, dc],
                                start=(dc == 0), stop=(dc == DC - 1))
                        sTE.inc(last)
                blk.tensor(f_te)

                def f_sc(se):
                    for j, (o, tc) in enumerate(jobs):
                        se.wait_ge(sTE.h, tm(j))
                        if j >= 2:
                            se.wait_ge(sy[j % 2].h, vdma_m(j - 2))
                        i = se.activation(vst[:, j % 2], vps[:, j % 2],
                                          AF.Copy)
                        sSC.inc(i)
                blk.scalar(f_sc)

                def f_sy(syn):
                    for j, (o, tc) in enumerate(jobs):
                        syn.wait_ge(sSC.h, scm(j))
                        dst = cag_in_t.ap()[voff:voff + TOK * Dv]
                        dst = dst.rearrange("(t d) -> t d", d=Dv)
                        dst = dst[tc * 128:(tc + 1) * 128,
                                  o * NTv:(o + 1) * NTv]
                        i = syn.dma_start(out=dst, in_=vst[:, j % 2])
                        sy[j % 2].inc(i, 16)
                    syn.wait_ge(sy[0].h, sy[0].n)
                    syn.wait_ge(sy[1].h, sy[1].n)
                blk.sync(f_sy)

        def emit_ag(cin, cout, ctr, waits, groups=None):
            with nc.Block() as blk:
                def f_gp(g):
                    for h, v in waits:
                        g.wait_ge(h, v)
                    i = g.collective_compute(
                        "AllGather", ALU.bypass,
                        replica_groups=groups or [list(range(8))],
                        ins=[cin.ap().opt()], outs=[cout.ap().opt()])
                    ctr.inc(i)
                blk.gpsimd(f_gp)

        def expansion_tiles():
            tiles = []
            for sec, src_t in ((0, gsl), (1, usl)):
                dstv = cag2_in.ap()[sec * SEC:sec * SEC + RG * INTER]
                dstv = dstv.rearrange("(p i) -> p i", i=INTER)
                r0 = 0
                while r0 < RG:
                    npr = min(128, RG - r0)
                    c0 = 0
                    while c0 < INTER:
                        w = min(FDMAX, INTER - c0)
                        src = src_t.ap()[r0:r0 + npr, c0:c0 + w]
                        dst = dstv[r0:r0 + npr, c0:c0 + w]
                        tiles.append((src, npr, w, dst))
                        c0 += w
                    r0 += npr
            amax = max(1, FDMAX // RDC)
            a0 = 0
            while a0 < IC:
                na = min(amax, IC - a0)
                src = dsl.ap().rearrange("(a p) c -> p a c", p=128)
                src = src[:, a0:a0 + na, :]
                dst = cag2_in.ap()[2 * SEC + a0 * 128 * RDC:
                                   2 * SEC + (a0 + na) * 128 * RDC]
                dst = dst.rearrange("(a p c) -> p a c", p=128, c=RDC)
                tiles.append((src, 128, na * RDC, dst))
                a0 += na
            return tiles

        def expansion_order(nt):
            order = []
            for t in range(nt):
                order.append(("load", t))
                if t >= 1:
                    order.append(("ship", t - 1))
            order.append(("ship", nt - 1))
            return order

        def expansion_gp_ops(g, tiles, eibf, ewb, ex0):
            for op, t in expansion_order(len(tiles)):
                src, npr, fd, dst = tiles[t]
                if op == "load":
                    if t >= 2:
                        g.wait_ge(sEX.h, ex0 + t - 1)
                    dstv = eibf[0:npr, t % 2, 0:fd]
                    if src.ndim == 3:
                        dstv = dstv.rearrange("p (a c) -> p a c",
                                              c=src.shape[2])
                    i = g.dma_start(out=dstv, in_=src)
                    exL[t % 2].inc(i, 16)
                else:
                    g.wait_ge(sEX.h, ex0 + t + 1)
                    srcv = ewb[0:npr, t % 2, 0:fd]
                    if dst.ndim == 3:
                        srcv = srcv.rearrange("p (a c) -> p a c",
                                              c=dst.shape[2])
                    i = g.dma_start(out=dst, in_=srcv)
                    exS[t % 2].inc(i, 16)

        def expansion_ve_tile(ve, tiles, t, eibf, ewb, etb, l0, s0):
            src, npr, fd, dst = tiles[t]
            ve.wait_ge(exL[t % 2].h, l0[t % 2] + 16 * (t // 2 + 1))
            if t >= 2:
                ve.wait_ge(exS[t % 2].h, s0[t % 2] + 16 * (t // 2 - 1 + 1))
            ib = eibf[0:npr, t % 2, 0:fd]
            wb = ewb[0:npr, t % 2, 0:fd]
            tb = etb[0:npr, 0:fd]
            ve.tensor_scalar(wb, ib, 0.0, luts[0:npr, 0:1],
                             ALU.is_equal, ALU.mult)
            i = None
            for l in range(1, 16):
                ve.tensor_scalar(tb, ib, float(l), luts[0:npr, l:l + 1],
                                 ALU.is_equal, ALU.mult)
                ve.drain()
                i = ve.tensor_tensor(wb, wb, tb, ALU.add)
                ve.drain()
            sEX.inc(i)

        def emit_expansion():
            tiles = expansion_tiles()
            nt = len(tiles)
            with (
                nc.sbuf_tensor(uid("ex_i"), [128, 2, FDMAX], BF16) as eibf,
                nc.sbuf_tensor(uid("ex_w"), [128, 2, FDMAX], BF16) as ewb,
                nc.sbuf_tensor(uid("ex_t"), [128, FDMAX], BF16) as etb,
                nc.Block() as blk,
            ):
                ex0 = sEX.n
                l0 = [exL[0].n, exL[1].n]
                s0 = [exS[0].n, exS[1].n]

                def f_gp(g):
                    expansion_gp_ops(g, tiles, eibf, ewb, ex0)
                    for c_ in (exL[0], exL[1], exS[0], exS[1]):
                        g.wait_ge(c_.h, c_.n)
                blk.gpsimd(f_gp)

                def f_ve(ve):
                    ve.wait_ge(sW.h, p0_w)
                    for t in range(nt):
                        expansion_ve_tile(ve, tiles, t, eibf, ewb, etb, l0, s0)
                blk.vector(f_ve)
            return [(exS[0].h, exS[0].n), (exS[1].h, exS[1].n)]

        def emit_attention(kind, expansion=False):
            masked = kind == "b"
            Hq = H
            cag = cag3_out if masked else cag1_out
            Lr = L3 if masked else L1
            Dv = KVD if masked else D
            voff = VOFF3 if masked else VOFF1
            cc = sC3 if masked else sC1
            groups = NKV if masked else H
            hpg = G if masked else 1
            nbuf = 1 if masked else 2
            with (
                nc.sbuf_tensor(uid("at_kh"), [128, nbuf, S], BF16) as kh,
                nc.sbuf_tensor(uid("at_vh"), [128, nbuf, KC, 128], BF16) as vh,
                nc.sbuf_tensor(uid("at_ot"), [128, TOK], BF16) as otmp,
                nc.psum_tensor(uid("at_st"), [128, 2, TOK], F32) as stp,
                nc.psum_tensor(uid("at_pv"), [128, TOK], F32) as pvp,
                nc.psum_tensor(uid("at_dn"), [1, TOK], F32) as dnp,
                nc.sbuf_tensor(uid("at_pt"), [128, KC, TOK], BF16) as ptb,
                ExitStack() as mst,
            ):
                msk = None
                if masked:
                    msk = mst.enter_context(
                        nc.sbuf_tensor(uid("at_mk"), [128, KC, TOK], BF16))
                tiles = None
                if expansion:
                    tiles = expansion_tiles()
                    eibf = mst.enter_context(
                        nc.sbuf_tensor(uid("ex_i"), [128, 2, FDMAX], BF16))
                    ewb = mst.enter_context(
                        nc.sbuf_tensor(uid("ex_w"), [128, 2, FDMAX], BF16))
                    etb = mst.enter_context(
                        nc.sbuf_tensor(uid("ex_t"), [128, FDMAX], BF16))
                    ex0 = sEX.n
                    exl0 = [exL[0].n, exL[1].n]
                    exs0 = [exS[0].n, exS[1].n]
                with nc.Block() as blk:
                    ve0, sc0, te0 = sVE.n, sSC.n, sTE.n
                    s0 = [sy[0].n, sy[1].n]
                    scph = KC + 1
                    veph = (KC if masked else 0) + 2
                    teph = KC + 2
                    exp_m = lambda h, kc: sc0 + h * scph + kc + 1
                    copy_m = lambda h: sc0 + h * scph + scph
                    msk_m = lambda h, kc: ve0 + h * veph + kc + 1
                    recip_m = lambda h: ve0 + h * veph + veph - 1
                    evac_m = lambda h: ve0 + h * veph + veph
                    st_m = lambda h, kc: te0 + h * teph + kc + 1
                    pv_m = lambda h: te0 + h * teph + KC + 1
                    bc_m = lambda h: te0 + h * teph + KC + 2
                    mask_w = sMK.n + (16 if masked else 0)
                    kv_m = lambda gi: (sy[gi % 2].h,
                                       s0[gi % 2] + 64 * (gi // 2 + 1))

                    def f_sy(syn):
                        if masked:
                            i = syn.dma_start(out=msk[:], in_=maskb.ap(
                            ).rearrange("(a p) t -> p a t", p=128))
                            sMK.inc(i, 16)
                        syn.wait_ge(cc.h, cc.n)
                        for gi in range(groups):
                            if gi >= nbuf:
                                syn.wait_ge(sTE.h,
                                            pv_m((gi - nbuf) * hpg + hpg - 1))
                            gb = gi % nbuf
                            for half in range(2):
                                src = cag.ap()[half,
                                               gi * 128 * TOK:
                                               (gi + 1) * 128 * TOK]
                                src = src.rearrange("(p t) -> p t", p=128)
                                i = syn.dma_start(
                                    out=kh[:, gb, half * TOK:(half + 1) * TOK],
                                    in_=src)
                                sy[gi % 2].inc(i, 16)
                            for half in range(2):
                                src = cag.ap()[half, voff:voff + TOK * Dv]
                                src = src.rearrange(
                                    "(a p d) -> p a d", p=128, d=Dv)
                                src = src[:, :, gi * 128:(gi + 1) * 128]
                                i = syn.dma_start(
                                    out=vh[:, gb, half * KCh:(half + 1) * KCh],
                                    in_=src)
                                sy[gi % 2].inc(i, 16)
                        syn.wait_ge(sy[0].h, sy[0].n)
                        syn.wait_ge(sy[1].h, sy[1].n)
                    blk.sync(f_sy)

                    if expansion:
                        def f_gp(g):
                            expansion_gp_ops(g, tiles, eibf, ewb, ex0)
                            for c_ in (exL[0], exL[1], exS[0], exS[1]):
                                g.wait_ge(c_.h, c_.n)
                            i = g.collective_compute(
                                "AllGather", ALU.bypass,
                                replica_groups=[list(range(8))],
                                ins=[cag2_in.ap().opt()],
                                outs=[cag2_out.ap().opt()])
                            sC2.inc(i)
                        blk.gpsimd(f_gp)

                    def f_te(te):
                        for h in range(Hq):
                            gi = h // hpg
                            gb = gi % nbuf
                            te.wait_ge(*kv_m(gi))
                            if h >= 1:
                                te.wait_ge(sSC.h, copy_m(h - 1))
                                te.wait_ge(sVE.h, evac_m(h - 1))
                            for kc in range(KC):
                                if kc >= 2:
                                    te.wait_ge(sSC.h, exp_m(h, kc - 2))
                                last = None
                                for n in range(TN):
                                    last = te.matmul(
                                        stp[:, kc % 2, nsl(n)],
                                        kh[:, gb, kc * 128:(kc + 1) * 128],
                                        qbuf[:, h, nsl(n)],
                                        start=True, stop=True)
                                sTE.inc(last)
                            last = None
                            for kc in range(KC):
                                te.wait_ge(sSC.h, exp_m(h, kc))
                                if masked:
                                    te.wait_ge(sVE.h, msk_m(h, kc))
                                for n in range(TN):
                                    te.matmul(pvp[:, nsl(n)],
                                              vh[:, gb, kc],
                                              ptb[:, kc, nsl(n)],
                                              start=(kc == 0),
                                              stop=(kc == KC - 1))
                                for n in range(TN):
                                    last = te.matmul(dnp[:, nsl(n)],
                                                     ones_col_bf[:],
                                                     ptb[:, kc, nsl(n)],
                                                     start=(kc == 0),
                                                     stop=(kc == KC - 1))
                            sTE.inc(last)
                            te.wait_ge(sVE.h, recip_m(h))
                            last = None
                            for n in range(TN):
                                last = te.matmul(stp[:, 0, nsl(n)],
                                                 ones_row_f[:],
                                                 rvec[:, nsl(n)],
                                                 start=True, stop=True)
                            sTE.inc(last)
                    blk.tensor(f_te)

                    def f_sc(se):
                        for h in range(Hq):
                            for kc in range(KC):
                                se.wait_ge(sTE.h, st_m(h, kc))
                                i = se.activation(ptb[:, kc], stp[:, kc % 2],
                                                  AF.Exp, bias=negshift[:],
                                                  scale=SCALE)
                                sSC.inc(i)
                            se.wait_ge(sTE.h, pv_m(h))
                            if h >= 1:
                                se.wait_ge(sVE.h, evac_m(h - 1))
                            i = se.activation(otmp[:], pvp[:], AF.Copy)
                            sSC.inc(i)
                    blk.scalar(f_sc)

                    def f_ve(ve):
                        if masked:
                            ve.wait_ge(sMK.h, mask_w)
                        if expansion:
                            ve.wait_ge(sW.h, p0_w)
                        for h in range(Hq):
                            if masked:
                                for kc in range(KC):
                                    ve.wait_ge(sSC.h, exp_m(h, kc))
                                    i = ve.tensor_mul(ptb[:, kc], ptb[:, kc],
                                                      msk[:, kc])
                                    sVE.inc(i)
                            ve.wait_ge(sTE.h, pv_m(h))
                            i = ve.reciprocal(rvec[:], dnp[:])
                            sVE.inc(i)
                            ve.wait_ge(sTE.h, bc_m(h))
                            ve.wait_ge(sSC.h, copy_m(h))
                            i = ve.tensor_mul(qbuf[:, h], otmp[:], stp[:, 0])
                            sVE.inc(i)
                            if expansion and h < len(tiles):
                                expansion_ve_tile(ve, tiles, h, eibf, ewb,
                                                  etb, exl0, exs0)
                        if expansion:
                            for t in range(Hq, len(tiles)):
                                expansion_ve_tile(ve, tiles, t, eibf, ewb,
                                                  etb, exl0, exs0)
                    blk.vector(f_ve)

        def emit_outproj(wt):
            with (
                nc.sbuf_tensor(uid("op_w"), [128, 2, DC, 128], BF16) as wti,
                nc.psum_tensor(uid("op_ps"), [128, 2, TOK], F32) as ops,
                nc.Block() as blk,
            ):
                ve0, te0 = sVE.n, sTE.n
                g0 = [gp[0].n, gp[1].n]
                tm = lambda j: te0 + j + 1
                em = lambda j: ve0 + j + 1
                gm = lambda j: g0[j % 2] + 16 * (j // 2 + 1)

                def f_gp(g):
                    for j in range(DC):
                        if j >= 2:
                            g.wait_ge(sTE.h, tm(j - 2))
                        i = g.dma_start(out=wti[:, j % 2],
                                        in_=colslice(wt, j * 128, 128))
                        gp[j % 2].inc(i, 16)
                    g.wait_ge(gp[0].h, gp[0].n)
                    g.wait_ge(gp[1].h, gp[1].n)
                blk.gpsimd(f_gp)

                def f_te(te):
                    for j in range(DC):
                        te.wait_ge(gp[j % 2].h, gm(j))
                        if j >= 2:
                            te.wait_ge(sVE.h, em(j - 2))
                        last = None
                        for oc in range(H):
                            for n in range(TN):
                                last = te.matmul(
                                    ops[:, j % 2, nsl(n)], wti[:, j % 2, oc],
                                    qbuf[:, oc, nsl(n)],
                                    start=(oc == 0), stop=(oc == H - 1))
                        sTE.inc(last)
                blk.tensor(f_te)

                def f_ve(ve):
                    for j in range(DC):
                        ve.wait_ge(sTE.h, tm(j))
                        i = ve.tensor_add(xsb[:, j], xsb[:, j],
                                          ops[:, j % 2])
                        sVE.inc(i)
                blk.vector(f_ve)

        def emit_ffn(wait_c2):
            fsl = lambda n: slice(n * FT, (n + 1) * FT)
            with nc.sbuf_tensor(uid("ff_h"), [128, IC, FT], BF16) as hbuf:
                for th in range(FTN):
                    t0 = th * FT
                    with (
                        nc.sbuf_tensor(uid("ff_wg"), [128, 2, DC, 128], BF16) as wg,
                        nc.sbuf_tensor(uid("ff_wu"), [128, 2, DC, 128], BF16) as wu,
                        nc.sbuf_tensor(uid("ff_st"), [128, 2, FT], F32) as stmp,
                        nc.psum_tensor(uid("ff_g"), [128, 2, FT], F32) as gps,
                        nc.psum_tensor(uid("ff_u"), [128, 2, FT], F32) as ups,
                        nc.Block() as blk,
                    ):
                        ve0, sc0, te0 = sVE.n, sSC.n, sTE.n
                        s0 = [sy[0].n, sy[1].n]
                        tm = lambda oc: te0 + oc + 1
                        sim = lambda oc: sc0 + oc + 1
                        hm = lambda oc: ve0 + oc + 1
                        ndg = 8
                        wm = lambda oc: (sy[oc % 2].h,
                                         s0[oc % 2]
                                         + 16 * 2 * ndg * (oc // 2 + 1))

                        def f_sy(syn):
                            if wait_c2:
                                syn.wait_ge(sC2.h, sC2.n)
                            for oc in range(IC):
                                if oc >= 2:
                                    syn.wait_ge(sTE.h, tm(oc - 2))
                                for sec, dstb in ((0, wg), (1, wu)):
                                    if RG % 128 == 0:
                                        rgc = RG // 128
                                        for r in range(8):
                                            src = cag2_out.ap()[
                                                r, sec * SEC:
                                                sec * SEC + RG * INTER]
                                            src = src.rearrange(
                                                "(a p i) -> p a i",
                                                p=128, i=INTER)
                                            src = src[:, :, oc * 128:
                                                      (oc + 1) * 128]
                                            dst = dstb[:, oc % 2,
                                                       r * rgc:(r + 1) * rgc]
                                            i = syn.dma_start(out=dst,
                                                              in_=src)
                                            sy[oc % 2].inc(i, 16)
                                    else:
                                        for r in range(8):
                                            src = cag2_out.ap()[
                                                r, sec * SEC:
                                                sec * SEC + RG * INTER]
                                            src = src.rearrange(
                                                "(a i) -> a i", i=INTER)
                                            src = src[:, oc * 128:
                                                      (oc + 1) * 128]
                                            row0 = r * RG
                                            dst = dstb[
                                                row0 % 128:row0 % 128 + RG,
                                                oc % 2, row0 // 128]
                                            i = syn.dma_start(out=dst, in_=src)
                                            sy[oc % 2].inc(i, 16)
                            syn.wait_ge(sy[0].h, sy[0].n)
                            syn.wait_ge(sy[1].h, sy[1].n)
                        blk.sync(f_sy)

                        def f_te(te):
                            for oc in range(IC):
                                te.wait_ge(*wm(oc))
                                if oc >= 2:
                                    te.wait_ge(sSC.h, sim(oc - 2))
                                    te.wait_ge(sVE.h, hm(oc - 2))
                                for dc in range(DC):
                                    te.matmul(gps[:, oc % 2], wg[:, oc % 2, dc],
                                              xn[:, dc, t0:t0 + FT],
                                              start=(dc == 0),
                                              stop=(dc == DC - 1))
                                last = None
                                for dc in range(DC):
                                    last = te.matmul(
                                        ups[:, oc % 2], wu[:, oc % 2, dc],
                                        xn[:, dc, t0:t0 + FT],
                                        start=(dc == 0), stop=(dc == DC - 1))
                                sTE.inc(last)
                        blk.tensor(f_te)

                        def f_sc(se):
                            for oc in range(IC):
                                se.wait_ge(sTE.h, tm(oc))
                                i = se.activation(stmp[:, oc % 2],
                                                  gps[:, oc % 2], AF.Silu)
                                sSC.inc(i)
                        blk.scalar(f_sc)

                        def f_ve(ve):
                            for oc in range(IC):
                                ve.wait_ge(sSC.h, sim(oc))
                                i = ve.tensor_mul(hbuf[:, oc],
                                                  stmp[:, oc % 2],
                                                  ups[:, oc % 2])
                                sVE.inc(i)
                        blk.vector(f_ve)

                    with (
                        nc.sbuf_tensor(uid("ff_wd"), [128, 2, IC, 128], BF16) as wd,
                        nc.psum_tensor(uid("ff_d"), [128, 2, FT], F32) as dps,
                        nc.Block() as blk,
                    ):
                        ve0, te0 = sVE.n, sTE.n
                        s0 = [sy[0].n, sy[1].n]
                        tm = lambda dj: te0 + dj + 1
                        em = lambda dj: ve0 + dj + 1

                        def down_dmas(dj, dstb):
                            out = []
                            col0 = dj * 128
                            done = 0
                            while done < 128:
                                c = col0 + done
                                r = c // RDC
                                lc = c % RDC
                                w = min(RDC - lc, 128 - done)
                                src = cag2_out.ap()[r, 2 * SEC:3 * SEC]
                                src = src.rearrange("(a p c) -> p a c",
                                                    p=128, c=RDC)
                                src = src[:, :, lc:lc + w]
                                dst = dstb[:, :, done:done + w]
                                out.append((dst, src))
                                done += w
                            return out

                        def f_sy(syn):
                            for dj in range(DC):
                                if dj >= 2:
                                    syn.wait_ge(sTE.h, tm(dj - 2))
                                for dst, src in down_dmas(dj, wd[:, dj % 2]):
                                    i = syn.dma_start(out=dst, in_=src)
                                    sy[dj % 2].inc(i, 16)
                            syn.wait_ge(sy[0].h, sy[0].n)
                            syn.wait_ge(sy[1].h, sy[1].n)
                        blk.sync(f_sy)

                        wm_pred = []
                        acc = [s0[0], s0[1]]
                        for dj in range(DC):
                            acc[dj % 2] += 16 * len(down_dmas(dj, wd[:, 0]))
                            wm_pred.append((sy[dj % 2].h, acc[dj % 2]))

                        def f_te(te):
                            for dj in range(DC):
                                te.wait_ge(*wm_pred[dj])
                                if dj >= 2:
                                    te.wait_ge(sVE.h, em(dj - 2))
                                last = None
                                for oc in range(IC):
                                    last = te.matmul(
                                        dps[:, dj % 2], wd[:, dj % 2, oc],
                                        hbuf[:, oc],
                                        start=(oc == 0), stop=(oc == IC - 1))
                                sTE.inc(last)
                        blk.tensor(f_te)

                        def f_ve(ve):
                            for dj in range(DC):
                                ve.wait_ge(sTE.h, tm(dj))
                                i = ve.tensor_add(xsb[:, dj, t0:t0 + FT],
                                                  xsb[:, dj, t0:t0 + FT],
                                                  dps[:, dj % 2])
                                sVE.inc(i)
                        blk.vector(f_ve)

        first_ln = [True]

        def ln():
            emit_ln(first=first_ln[0])
            first_ln[0] = False

        need_ffn = ("f1" in stages) or ("f2" in stages)
        if "a" in stages:
            ln()
            jobs_k = [(H + h, h) for h in range(H)]
            emit_proj(wqkvT, jobs_k, "ropek", cag1_in, 0)
            emit_vproj(lambda c0, w: colslice(wqkvT, 2 * H * 128 + c0, w),
                       D, VOFF1, cag1_in, D)
            emit_ag(cag1_in, cag1_out, sC1,
                    [(sy[0].h, sy[0].n), (sy[1].h, sy[1].n)],
                    groups=PAIRS)
            jobs_qk = [(h, h) for h in range(H)]
            emit_proj(wqkvT, jobs_qk, "ropeq")
            emit_attention("a", expansion=need_ffn)
            emit_outproj(woT)
        elif need_ffn:
            exp_waits = emit_expansion()
            emit_ag(cag2_in, cag2_out, sC2, exp_waits)
        if "f1" in stages:
            ln()
            emit_ffn(wait_c2=True)
        if "b" in stages:
            ln()
            jobs_kb = [(g, g) for g in range(NKV)]
            emit_proj(wkb, jobs_kb, "copyk", cag3_in, 0)
            emit_vproj(lambda c0, w: wvb.ap().rearrange(
                "(c p) o -> p c o", p=128)[:, :, c0:c0 + w],
                KVD, VOFF3, cag3_in, KVD)
            emit_ag(cag3_in, cag3_out, sC3,
                    [(sy[0].h, sy[0].n), (sy[1].h, sy[1].n)],
                    groups=PAIRS)
            jobs_qb = [(h, h) for h in range(H)]
            emit_proj(wqb, jobs_qb, "copyq")
            emit_attention("b")
            emit_outproj(wob)
        if "f2" in stages:
            ln()
            emit_ffn(wait_c2=("f1" not in stages))

        with nc.Block() as blk:
            def f_sy(sy):
                i = sy.dma_start(out=y.ap().rearrange("(c p) t -> p c t",
                                                      p=128), in_=xsb[:])
                sW.inc(i, 16)
                sy.wait_ge(sW.h, sW.n)
            blk.sync(f_sy)

    if verbose:
        print(f"instructions: {nc.n_instructions()}")
    return nc



def _rope_tables(S):
    inv = 1.0 / (10000.0 ** (np.arange(0, 128, 2, dtype=np.float32) / 128.0))
    t = np.arange(S, dtype=np.float32)
    fr = np.einsum("i,j->ij", t, inv)
    emb = np.concatenate([fr, fr], axis=-1)
    return np.cos(emb).astype(np.float32), np.sin(emb).astype(np.float32)


def make_in_maps(cfg, x, w_qkv, w_out, Wq, Wk, Wv, Wo, lut, gate_idx, up_idx,
                 down_idx):
    D, TOK, S = cfg.D, cfg.TOK, cfg.S
    f32 = lambda a: np.ascontiguousarray(np.asarray(a, dtype=np.float32))
    i32 = lambda a: np.ascontiguousarray(np.asarray(a, dtype=np.int32))
    wqkvT = f32(np.asarray(w_qkv).T)
    woT = f32(np.asarray(w_out).T)
    wq, wk, wv, wo = f32(Wq), f32(Wk), f32(Wv), f32(Wo)
    gT = i32(np.asarray(gate_idx).T)
    uT = i32(np.asarray(up_idx).T)
    dT = i32(np.asarray(down_idx).T)
    lutbc = f32(np.tile(np.asarray(lut, dtype=np.float32)[None, :], (128, 1)))
    cos, sin = _rope_tables(S)
    in_maps = []
    for c in range(8):
        b, hh = c // 2, c % 2
        tok = slice(hh * TOK, (hh + 1) * TOK)
        kk = np.arange(S, dtype=np.int64)[:, None]
        qq = hh * TOK + np.arange(TOK, dtype=np.int64)[None, :]
        mask = (kk <= qq).astype(ml_dtypes.bfloat16)
        in_maps.append({
            "xin": f32(np.asarray(x)[b, tok, :].T),
            "wqkvT": wqkvT, "woT": woT,
            "wqb": wq, "wkb": wk, "wvb": wv, "wob": wo,
            "cosq": f32(cos[tok, :].T), "sinq": f32(sin[tok, :].T),
            "maskb": np.ascontiguousarray(mask),
            "lutin": lutbc,
            "gsl": np.ascontiguousarray(gT[c * cfg.RG:(c + 1) * cfg.RG, :]),
            "usl": np.ascontiguousarray(uT[c * cfg.RG:(c + 1) * cfg.RG, :]),
            "dsl": np.ascontiguousarray(dT[:, c * cfg.RDC:(c + 1) * cfg.RDC]),
        })
    return in_maps


def assemble(cfg, results):
    B, S, D, TOK = cfg.B, cfg.S, cfg.D, cfg.TOK
    out = np.empty((B, S, D), dtype=np.float32)
    for c in range(8):
        b, hh = c // 2, c % 2
        out[b, hh * TOK:(hh + 1) * TOK, :] = results[c]["y"].T
    return out


def kernel(L0_attn_a_x, L0_attn_a_w_qkv, L0_attn_a_w_out, L0_attn_b_Wq,
           L0_attn_b_Wk, L0_attn_b_Wv, L0_attn_b_Wo, L0_ffn_lut,
           L0_ffn_gate_idx, L0_ffn_up_idx, L0_ffn_down_idx):
    cfg = FULL
    nc = build(cfg)
    in_maps = make_in_maps(cfg, L0_attn_a_x, L0_attn_a_w_qkv, L0_attn_a_w_out,
                           L0_attn_b_Wq, L0_attn_b_Wk, L0_attn_b_Wv,
                           L0_attn_b_Wo, L0_ffn_lut, L0_ffn_gate_idx,
                           L0_ffn_up_idx, L0_ffn_down_idx)
    res = run_bass_kernel_spmd(nc, in_maps, core_ids=list(range(8)))
    return assemble(cfg, res.results)


# version 05
# speedup vs baseline: 1.3544x; 1.1246x over previous
import math
from contextlib import ExitStack

import numpy as np
import ml_dtypes
import concourse.bass as bass
import concourse.mybir as mybir
from concourse.bass_utils import run_bass_kernel_spmd

F32 = mybir.dt.float32
BF16 = mybir.dt.bfloat16
I32 = mybir.dt.int32
AF = mybir.ActivationFunctionType
ALU = mybir.AluOpType

SCALE = 1.0 / math.sqrt(128.0)
SHIFT = 12.0
EPS = 1e-5
FDMAX = 2304
PAIRS = [[0, 1], [2, 3], [4, 5], [6, 7]]


class Cfg:
    def __init__(self, H, NKV, S, IC, B=4):
        self.H = H
        self.NKV = NKV
        self.G = H // NKV
        self.D = H * 128
        self.DC = self.D // 128
        self.S = S
        self.B = B
        self.TOK = B * S // 8
        assert self.TOK == S // 2
        self.KC = S // 128
        self.KCh = self.TOK // 128
        self.IC = IC
        self.INTER = IC * 128
        self.KVD = NKV * 128
        self.NT = min(512, self.TOK)
        self.TN = self.TOK // self.NT
        self.FT = min(512, self.TOK)
        self.FTN = self.TOK // self.FT
        self.RG = self.D // 8
        self.RDC = self.D // 8
        self.SEC = self.D * self.INTER // 8


FULL = Cfg(H=16, NKV=4, S=2048, IC=43)
SMALL = Cfg(H=2, NKV=1, S=512, IC=2)


def build(cfg, verbose=False, stages=('a', 'f1', 'b', 'f2')):
    H, D, DC, TOK, KC, KCh = cfg.H, cfg.D, cfg.DC, cfg.TOK, cfg.KC, cfg.KCh
    IC, KVD, NT, TN = cfg.IC, cfg.KVD, cfg.NT, cfg.TN
    S, G, NKV, INTER, RG, RDC, SEC = (cfg.S, cfg.G, cfg.NKV, cfg.INTER,
                                      cfg.RG, cfg.RDC, cfg.SEC)
    FT, FTN = cfg.FT, cfg.FTN

    nc = bass.Bass("TRN2", target_bir_lowering=False)

    xin = nc.dram_tensor("xin", [D, TOK], F32, kind="ExternalInput")
    wqkvT = nc.dram_tensor("wqkvT", [D, 3 * D], F32, kind="ExternalInput")
    woT = nc.dram_tensor("woT", [D, D], F32, kind="ExternalInput")
    wqb = nc.dram_tensor("wqb", [D, D], F32, kind="ExternalInput")
    wkb = nc.dram_tensor("wkb", [D, KVD], F32, kind="ExternalInput")
    wvb = nc.dram_tensor("wvb", [D, KVD], F32, kind="ExternalInput")
    wob = nc.dram_tensor("wob", [D, D], F32, kind="ExternalInput")
    cosq = nc.dram_tensor("cosq", [128, TOK], F32, kind="ExternalInput")
    sinq = nc.dram_tensor("sinq", [128, TOK], F32, kind="ExternalInput")
    maskb = nc.dram_tensor("maskb", [S, TOK], BF16, kind="ExternalInput")
    lutin = nc.dram_tensor("lutin", [128, 16], F32, kind="ExternalInput")
    gsl = nc.dram_tensor("gsl", [RG, INTER], I32, kind="ExternalInput")
    usl = nc.dram_tensor("usl", [RG, INTER], I32, kind="ExternalInput")
    dsl = nc.dram_tensor("dsl", [INTER, RDC], I32, kind="ExternalInput")
    y = nc.dram_tensor("y", [D, TOK], F32, kind="ExternalOutput")

    L1 = 2 * D * TOK
    cag1_in = nc.dram_tensor("cag1_in", [L1], BF16)
    cag1_out = nc.dram_tensor("cag1_out", [2, L1], BF16)
    L2 = 3 * SEC
    cag2_in = nc.dram_tensor("cag2_in", [L2], BF16)
    cag2_out = nc.dram_tensor("cag2_out", [8, L2], BF16, addr_space="Shared")
    L3 = 2 * KVD * TOK
    cag3_in = nc.dram_tensor("cag3_in", [L3], BF16)
    cag3_out = nc.dram_tensor("cag3_out", [2, L3], BF16)
    VOFF1 = D * TOK
    VOFF3 = KVD * TOK

    class Ctr:
        def __init__(self, name):
            self.h = nc.alloc_semaphore(name)
            self.n = 0

        def inc(self, instr, k=1):
            instr.then_inc(self.h, k)
            self.n += k
            return self.n

    sW = Ctr("sW")
    gp = [Ctr("gp0"), Ctr("gp1")]
    sy = [Ctr("sy0"), Ctr("sy1")]
    exL = [Ctr("exL0"), Ctr("exL1")]
    exS = [Ctr("exS0"), Ctr("exS1")]
    sEX = Ctr("sEX")
    sMK = Ctr("sMK")
    sC1 = Ctr("sC1")
    sC2 = Ctr("sC2")
    sC3 = Ctr("sC3")
    sTE = Ctr("sTE")
    sSC = Ctr("sSC")
    sVE = Ctr("sVE")

    _uid = [0]

    def uid(s):
        _uid[0] += 1
        return f"{s}{_uid[0]}"

    def nsl(n):
        return slice(n * NT, (n + 1) * NT)

    def colslice(t, j0, w):
        return t.ap().rearrange("(c p) o -> p c o", p=128)[:, :, j0:j0 + w]

    st = ExitStack()
    with st:
        sb = lambda name, shape, dt: st.enter_context(
            nc.sbuf_tensor(name, shape, dt))
        xsb = sb("xsb", [128, DC, TOK], F32)
        xn = sb("xnb", [128, DC, TOK], BF16)
        luts = sb("luts", [128, 16], F32)
        ones_col_bf = sb("ones_col_bf", [128, 1], BF16)
        ones_row_f = sb("ones_row_f", [1, 128], F32)
        ones_row_bf = sb("ones_row_bf", [1, 128], BF16)
        negshift = sb("negshift", [128, 1], F32)

        with nc.Block() as blk:
            def f_sy(sy):
                sy.dma_start(out=xsb[:], in_=xin.ap().rearrange(
                    "(c p) t -> p c t", p=128)).then_inc(sW.h, 16)
                sW.n += 16
                sy.dma_start(out=luts[:], in_=lutin.ap()).then_inc(sW.h, 16)
                sW.n += 16
            blk.sync(f_sy)

            def f_ve(ve):
                ve.memset(ones_col_bf[:], 1.0)
                ve.memset(ones_row_f[:], 1.0)
                ve.memset(ones_row_bf[:], 1.0)
                ve.memset(negshift[:], -SHIFT)
            blk.vector(f_ve)
        p0_w = sW.n

        def emit_ln(first=False):
            with (
                nc.sbuf_tensor(uid("ln_mu"), [1, TOK], F32) as mu,
                nc.sbuf_tensor(uid("ln_es"), [1, TOK], F32) as es,
                nc.sbuf_tensor(uid("ln_rs"), [1, TOK], F32) as rsd,
                nc.sbuf_tensor(uid("ln_mr"), [1, TOK], F32) as mr,
                nc.sbuf_tensor(uid("ln_sq"), [128, 2, TOK], BF16) as sqb,
                nc.sbuf_tensor(uid("ln_nrm"), [128, 2, TOK], F32) as nrm,
                nc.psum_tensor(uid("ln_sum"), [1, TOK], F32) as sumps,
                nc.psum_tensor(uid("ln_sqp"), [1, TOK], F32) as sqps,
                nc.psum_tensor(uid("ln_br"), [128, TOK], F32) as brps,
                nc.psum_tensor(uid("ln_bm"), [128, TOK], F32) as bmps,
                nc.Block() as blk,
            ):
                ve0, sc0, te0 = sVE.n, sSC.n, sTE.n
                copy_m = lambda c: ve0 + c + 1
                sq_m = lambda c: sc0 + c + 1
                sum_m = lambda c: te0 + c + 1
                stats1_m = ve0 + DC + 1
                sqrt_m = sc0 + DC + 1
                stats2_m = ve0 + DC + 2
                bcast_m = te0 + DC + 1

                def f_ve(ve):
                    if first:
                        ve.wait_ge(sW.h, p0_w)
                    for c in range(DC):
                        i = ve.tensor_copy(xn[:, c], xsb[:, c])
                        sVE.inc(i)
                    ve.wait_ge(sTE.h, sum_m(DC - 1))
                    ve.tensor_scalar(mu[:], sumps[:], 1.0 / D, None, ALU.mult)
                    ve.tensor_scalar(es[:], sqps[:], 1.0 / D, None, ALU.mult)
                    ve.drain()
                    ve.tensor_mul(rsd[:], mu[:], mu[:])
                    ve.drain()
                    ve.tensor_sub(es[:], es[:], rsd[:])
                    ve.drain()
                    i = ve.tensor_scalar(es[:], es[:], EPS, None, ALU.add)
                    sVE.inc(i)
                    ve.wait_ge(sSC.h, sqrt_m)
                    ve.reciprocal(rsd[:], es[:])
                    ve.drain()
                    i = ve.tensor_mul(mr[:], mu[:], rsd[:])
                    sVE.inc(i)
                    for c in range(DC):
                        ve.wait_ge(sTE.h, bcast_m)
                        ve.tensor_mul(nrm[:, c % 2], xsb[:, c], brps[:])
                        ve.drain()
                        i = ve.tensor_sub(xn[:, c], nrm[:, c % 2], bmps[:])
                        sVE.inc(i)
                blk.vector(f_ve)

                def f_sc(se):
                    for c in range(DC):
                        se.wait_ge(sVE.h, copy_m(c))
                        if c >= 2:
                            se.wait_ge(sTE.h, sum_m(c - 2))
                        i = se.activation(sqb[:, c % 2], xn[:, c], AF.Square)
                        sSC.inc(i)
                    se.wait_ge(sVE.h, stats1_m)
                    i = se.activation(es[:], es[:], AF.Sqrt)
                    sSC.inc(i)
                blk.scalar(f_sc)

                def f_te(te):
                    for c in range(DC):
                        te.wait_ge(sVE.h, copy_m(c))
                        te.wait_ge(sSC.h, sq_m(c))
                        for n in range(TN):
                            te.matmul(sumps[:, nsl(n)], ones_col_bf[:],
                                      xn[:, c, nsl(n)],
                                      start=(c == 0), stop=(c == DC - 1))
                        last = None
                        for n in range(TN):
                            last = te.matmul(sqps[:, nsl(n)], ones_col_bf[:],
                                             sqb[:, c % 2, nsl(n)],
                                             start=(c == 0), stop=(c == DC - 1))
                        sTE.inc(last)
                    te.wait_ge(sVE.h, stats2_m)
                    for n in range(TN):
                        te.matmul(brps[:, nsl(n)], ones_row_f[:],
                                  rsd[:, nsl(n)], start=True, stop=True)
                    last = None
                    for n in range(TN):
                        last = te.matmul(bmps[:, nsl(n)], ones_row_f[:],
                                         mr[:, nsl(n)], start=True, stop=True)
                    sTE.inc(last)
                blk.tensor(f_te)

        def emit_proj(wt, jobs, kind, cag_in_t=None, koff=0, qb=None):
            rope = kind.startswith("rope")
            to_cag = kind.endswith("k")
            qbuf = qb
            with (
                nc.sbuf_tensor(uid("pj_w"), [128, 2, DC, 128], BF16) as wqk,
                nc.sbuf_tensor(uid("pj_kst"), [128, 2, TOK], BF16) as kst,
                nc.sbuf_tensor(uid("pj_rt"), [128, 2, TOK], F32) as rtp,
                nc.sbuf_tensor(uid("pj_ut"), [128, 2, TOK], F32) as utp,
                nc.sbuf_tensor(uid("pj_cos"), [128, TOK], F32) as cosb,
                nc.sbuf_tensor(uid("pj_sin"), [128, TOK], F32) as sinb,
                nc.psum_tensor(uid("pj_ps"), [128, 2, TOK], F32) as pps,
                nc.Block() as blk,
            ):
                n_j = len(jobs)
                ve0, sc0, te0, w0 = sVE.n, sSC.n, sTE.n, sW.n
                g0 = [gp[0].n, gp[1].n]
                s0 = [sy[0].n, sy[1].n]
                tm = lambda j: te0 + j + 1
                em0 = ve0 if rope else sc0
                em = lambda j: em0 + j + 1
                gm = lambda j: g0[j % 2] + 16 * (j // 2 + 1)
                kdma_m = lambda j: s0[j % 2] + 16 * (j // 2 + 1)

                def f_sy(syn):
                    if rope:
                        syn.dma_start(out=cosb[:], in_=cosq.ap()).then_inc(
                            sW.h, 16)
                        sW.n += 16
                        syn.dma_start(out=sinb[:], in_=sinq.ap()).then_inc(
                            sW.h, 16)
                        sW.n += 16
                    if to_cag:
                        eh = sVE.h if rope else sSC.h
                        for j, (oc, hidx) in enumerate(jobs):
                            syn.wait_ge(eh, em(j))
                            dst = cag_in_t.ap()[
                                (koff + hidx * 128) * TOK:
                                (koff + (hidx + 1) * 128) * TOK]
                            dst = dst.rearrange("(p t) -> p t", p=128)
                            i = syn.dma_start(out=dst, in_=kst[:, j % 2])
                            sy[j % 2].inc(i, 16)
                        syn.wait_ge(sy[0].h, sy[0].n)
                        syn.wait_ge(sy[1].h, sy[1].n)
                blk.sync(f_sy)

                def f_gp(g):
                    for j, (oc, hidx) in enumerate(jobs):
                        if j >= 2:
                            g.wait_ge(sTE.h, tm(j - 2))
                        i = g.dma_start(out=wqk[:, j % 2],
                                        in_=colslice(wt, oc * 128, 128))
                        gp[j % 2].inc(i, 16)
                    g.wait_ge(gp[0].h, gp[0].n)
                    g.wait_ge(gp[1].h, gp[1].n)
                blk.gpsimd(f_gp)

                def f_te(te):
                    for j, (oc, hidx) in enumerate(jobs):
                        te.wait_ge(gp[j % 2].h, gm(j))
                        if j >= 2:
                            te.wait_ge(sVE.h if rope else sSC.h, em(j - 2))
                        last = None
                        for dc in range(DC):
                            for n in range(TN):
                                last = te.matmul(
                                    pps[:, j % 2, nsl(n)], wqk[:, j % 2, dc],
                                    xn[:, dc, nsl(n)],
                                    start=(dc == 0), stop=(dc == DC - 1))
                        sTE.inc(last)
                blk.tensor(f_te)

                if rope:
                    def f_ve(ve):
                        if rope:
                            ve.wait_ge(sW.h, sW.n)
                        for j, (oc, hidx) in enumerate(jobs):
                            ve.wait_ge(sTE.h, tm(j))
                            ps = pps[:, j % 2]
                            if to_cag:
                                if j >= 2:
                                    ve.wait_ge(sy[j % 2].h, kdma_m(j - 2))
                                dst = kst[:, j % 2]
                            else:
                                dst = qbuf[:, hidx]
                            t = rtp[:, j % 2]
                            u = utp[:, j % 2]
                            ve.tensor_mul(t, ps, cosb[:])
                            ve.tensor_mul(u[0:64], ps[64:128], sinb[0:64])
                            ve.tensor_mul(u[64:128], ps[0:64], sinb[64:128])
                            ve.drain()
                            ve.tensor_sub(dst[0:64], t[0:64], u[0:64])
                            i = ve.tensor_add(dst[64:128], t[64:128],
                                              u[64:128])
                            sVE.inc(i)
                    blk.vector(f_ve)
                else:
                    def f_sc(se):
                        for j, (oc, hidx) in enumerate(jobs):
                            se.wait_ge(sTE.h, tm(j))
                            if to_cag:
                                if j >= 2:
                                    se.wait_ge(sy[j % 2].h, kdma_m(j - 2))
                                dst = kst[:, j % 2]
                            else:
                                dst = qbuf[:, hidx]
                            i = se.activation(dst, pps[:, j % 2], AF.Copy)
                            sSC.inc(i)
                    blk.scalar(f_sc)

        def emit_vproj(wsrc_fn, out_cols, voff, cag_in_t, Dv):
            NTv = min(512, out_cols)
            NOS = out_cols // NTv
            TC = TOK // 128
            with (
                nc.sbuf_tensor(uid("vp_w"), [128, 2, DC, NTv], BF16) as wv4,
                nc.sbuf_tensor(uid("vp_st"), [128, 2, NTv], BF16) as vst,
                nc.psum_tensor(uid("vp_ps"), [128, 2, NTv], F32) as vps,
                nc.Block() as blk,
            ):
                jobs = [(o, tc) for o in range(NOS) for tc in range(TC)]
                sc0, te0 = sSC.n, sTE.n
                g0 = [gp[0].n, gp[1].n]
                s0 = [sy[0].n, sy[1].n]
                tm = lambda j: te0 + j + 1
                scm = lambda j: sc0 + j + 1
                gm = lambda o: g0[o % 2] + 16 * (o // 2 + 1)
                vdma_m = lambda j: s0[j % 2] + 16 * (j // 2 + 1)

                def f_gp(g):
                    for o in range(NOS):
                        if o >= 2:
                            g.wait_ge(sTE.h, te0 + (o - 1) * TC)
                        i = g.dma_start(out=wv4[:, o % 2],
                                        in_=wsrc_fn(o * NTv, NTv))
                        gp[o % 2].inc(i, 16)
                    g.wait_ge(gp[0].h, gp[0].n)
                    g.wait_ge(gp[1].h, gp[1].n)
                blk.gpsimd(f_gp)

                def f_te(te):
                    for j, (o, tc) in enumerate(jobs):
                        if tc == 0:
                            te.wait_ge(gp[o % 2].h, gm(o))
                        if j >= 2:
                            te.wait_ge(sSC.h, scm(j - 2))
                        last = None
                        for dc in range(DC):
                            last = te.matmul(
                                vps[:, j % 2], xn[:, dc, tc * 128:(tc + 1) * 128],
                                wv4[:, o % 2, dc],
                                start=(dc == 0), stop=(dc == DC - 1))
                        sTE.inc(last)
                blk.tensor(f_te)

                def f_sc(se):
                    for j, (o, tc) in enumerate(jobs):
                        se.wait_ge(sTE.h, tm(j))
                        if j >= 2:
                            se.wait_ge(sy[j % 2].h, vdma_m(j - 2))
                        i = se.activation(vst[:, j % 2], vps[:, j % 2],
                                          AF.Copy)
                        sSC.inc(i)
                blk.scalar(f_sc)

                def f_sy(syn):
                    for j, (o, tc) in enumerate(jobs):
                        syn.wait_ge(sSC.h, scm(j))
                        dst = cag_in_t.ap()[voff:voff + TOK * Dv]
                        dst = dst.rearrange("(t d) -> t d", d=Dv)
                        dst = dst[tc * 128:(tc + 1) * 128,
                                  o * NTv:(o + 1) * NTv]
                        i = syn.dma_start(out=dst, in_=vst[:, j % 2])
                        sy[j % 2].inc(i, 16)
                    syn.wait_ge(sy[0].h, sy[0].n)
                    syn.wait_ge(sy[1].h, sy[1].n)
                blk.sync(f_sy)

        def emit_ag(cin, cout, ctr, waits, groups=None):
            with nc.Block() as blk:
                def f_gp(g):
                    for h, v in waits:
                        g.wait_ge(h, v)
                    i = g.collective_compute(
                        "AllGather", ALU.bypass,
                        replica_groups=groups or [list(range(8))],
                        ins=[cin.ap().opt()], outs=[cout.ap().opt()])
                    ctr.inc(i)
                blk.gpsimd(f_gp)

        def expansion_tiles():
            tiles = []
            for sec, src_t in ((0, gsl), (1, usl)):
                dstv = cag2_in.ap()[sec * SEC:sec * SEC + RG * INTER]
                dstv = dstv.rearrange("(p i) -> p i", i=INTER)
                r0 = 0
                while r0 < RG:
                    npr = min(128, RG - r0)
                    c0 = 0
                    while c0 < INTER:
                        w = min(FDMAX, INTER - c0)
                        src = src_t.ap()[r0:r0 + npr, c0:c0 + w]
                        dst = dstv[r0:r0 + npr, c0:c0 + w]
                        tiles.append((src, npr, w, dst))
                        c0 += w
                    r0 += npr
            amax = max(1, FDMAX // RDC)
            a0 = 0
            while a0 < IC:
                na = min(amax, IC - a0)
                src = dsl.ap().rearrange("(a p) c -> p a c", p=128)
                src = src[:, a0:a0 + na, :]
                dst = cag2_in.ap()[2 * SEC + a0 * 128 * RDC:
                                   2 * SEC + (a0 + na) * 128 * RDC]
                dst = dst.rearrange("(a p c) -> p a c", p=128, c=RDC)
                tiles.append((src, 128, na * RDC, dst))
                a0 += na
            return tiles

        def expansion_order(nt):
            order = []
            for t in range(nt):
                order.append(("load", t))
                if t >= 1:
                    order.append(("ship", t - 1))
            order.append(("ship", nt - 1))
            return order

        def expansion_gp_ops(g, tiles, eibf, ewb, ex0):
            for op, t in expansion_order(len(tiles)):
                src, npr, fd, dst = tiles[t]
                if op == "load":
                    if t >= 2:
                        g.wait_ge(sEX.h, ex0 + t - 1)
                    dstv = eibf[0:npr, t % 2, 0:fd]
                    if src.ndim == 3:
                        dstv = dstv.rearrange("p (a c) -> p a c",
                                              c=src.shape[2])
                    i = g.dma_start(out=dstv, in_=src)
                    exL[t % 2].inc(i, 16)
                else:
                    g.wait_ge(sEX.h, ex0 + t + 1)
                    srcv = ewb[0:npr, t % 2, 0:fd]
                    if dst.ndim == 3:
                        srcv = srcv.rearrange("p (a c) -> p a c",
                                              c=dst.shape[2])
                    i = g.dma_start(out=dst, in_=srcv)
                    exS[t % 2].inc(i, 16)

        def expansion_ve_tile(ve, tiles, t, eibf, ewb, etb, l0, s0):
            src, npr, fd, dst = tiles[t]
            ve.wait_ge(exL[t % 2].h, l0[t % 2] + 16 * (t // 2 + 1))
            if t >= 2:
                ve.wait_ge(exS[t % 2].h, s0[t % 2] + 16 * (t // 2 - 1 + 1))
            ib = eibf[0:npr, t % 2, 0:fd]
            wb = ewb[0:npr, t % 2, 0:fd]
            tb = etb[0:npr, 0:fd]
            ve.tensor_scalar(wb, ib, 0.0, luts[0:npr, 0:1],
                             ALU.is_equal, ALU.mult)
            i = None
            for l in range(1, 16):
                ve.tensor_scalar(tb, ib, float(l), luts[0:npr, l:l + 1],
                                 ALU.is_equal, ALU.mult)
                ve.drain()
                i = ve.tensor_tensor(wb, wb, tb, ALU.add)
                ve.drain()
            sEX.inc(i)

        def emit_expansion():
            tiles = expansion_tiles()
            nt = len(tiles)
            with (
                nc.sbuf_tensor(uid("ex_i"), [128, 2, FDMAX], BF16) as eibf,
                nc.sbuf_tensor(uid("ex_w"), [128, 2, FDMAX], BF16) as ewb,
                nc.sbuf_tensor(uid("ex_t"), [128, FDMAX], BF16) as etb,
                nc.Block() as blk,
            ):
                ex0 = sEX.n
                l0 = [exL[0].n, exL[1].n]
                s0 = [exS[0].n, exS[1].n]

                def f_gp(g):
                    expansion_gp_ops(g, tiles, eibf, ewb, ex0)
                    for c_ in (exL[0], exL[1], exS[0], exS[1]):
                        g.wait_ge(c_.h, c_.n)
                blk.gpsimd(f_gp)

                def f_ve(ve):
                    ve.wait_ge(sW.h, p0_w)
                    for t in range(nt):
                        expansion_ve_tile(ve, tiles, t, eibf, ewb, etb, l0, s0)
                blk.vector(f_ve)
            return [(exS[0].h, exS[0].n), (exS[1].h, exS[1].n)]

        def emit_attention(kind, expansion=False, qb=None):
            masked = kind == "b"
            qbuf = qb
            Hq = H
            cag = cag3_out if masked else cag1_out
            Lr = L3 if masked else L1
            Dv = KVD if masked else D
            voff = VOFF3 if masked else VOFF1
            cc = sC3 if masked else sC1
            groups = NKV if masked else H
            hpg = G if masked else 1
            nbuf = 1 if masked else 2
            with (
                nc.sbuf_tensor(uid("at_kh"), [128, nbuf, S], BF16) as kh,
                nc.sbuf_tensor(uid("at_vh"), [128, nbuf, KC, 128], BF16) as vh,
                nc.sbuf_tensor(uid("at_ot"), [128, TOK], BF16) as otmp,
                nc.psum_tensor(uid("at_st"), [128, 2, TOK], F32) as stp,
                nc.psum_tensor(uid("at_pv"), [128, TOK], F32) as pvp,
                nc.psum_tensor(uid("at_dn"), [1, TOK], F32) as dnp,
                nc.sbuf_tensor(uid("at_pt"), [128, KC, TOK], BF16) as ptb,
                nc.sbuf_tensor(uid("at_rv"), [1, TOK], F32) as rvec,
                ExitStack() as mst,
            ):
                msk = None
                if masked:
                    msk = mst.enter_context(
                        nc.sbuf_tensor(uid("at_mk"), [128, KC, TOK], BF16))
                tiles = None
                if expansion:
                    tiles = expansion_tiles()
                    eibf = mst.enter_context(
                        nc.sbuf_tensor(uid("ex_i"), [128, 2, FDMAX], BF16))
                    ewb = mst.enter_context(
                        nc.sbuf_tensor(uid("ex_w"), [128, 2, FDMAX], BF16))
                    etb = mst.enter_context(
                        nc.sbuf_tensor(uid("ex_t"), [128, FDMAX], BF16))
                    ex0 = sEX.n
                    exl0 = [exL[0].n, exL[1].n]
                    exs0 = [exS[0].n, exS[1].n]
                with nc.Block() as blk:
                    ve0, sc0, te0 = sVE.n, sSC.n, sTE.n
                    s0 = [sy[0].n, sy[1].n]
                    scph = KC + 1
                    veph = (KC if masked else 0) + 2
                    teph = KC + 2
                    exp_m = lambda h, kc: sc0 + h * scph + kc + 1
                    copy_m = lambda h: sc0 + h * scph + scph
                    msk_m = lambda h, kc: ve0 + h * veph + kc + 1
                    recip_m = lambda h: ve0 + h * veph + veph - 1
                    evac_m = lambda h: ve0 + h * veph + veph
                    st_m = lambda h, kc: te0 + h * teph + kc + 1
                    pv_m = lambda h: te0 + h * teph + KC + 1
                    bc_m = lambda h: te0 + h * teph + KC + 2
                    mask_w = sMK.n + (16 if masked else 0)
                    kv_m = lambda gi: (sy[gi % 2].h,
                                       s0[gi % 2] + 64 * (gi // 2 + 1))

                    def f_sy(syn):
                        if masked:
                            i = syn.dma_start(out=msk[:], in_=maskb.ap(
                            ).rearrange("(a p) t -> p a t", p=128))
                            sMK.inc(i, 16)
                        syn.wait_ge(cc.h, cc.n)
                        for gi in range(groups):
                            if gi >= nbuf:
                                syn.wait_ge(sTE.h,
                                            pv_m((gi - nbuf) * hpg + hpg - 1))
                            gb = gi % nbuf
                            for half in range(2):
                                src = cag.ap()[half,
                                               gi * 128 * TOK:
                                               (gi + 1) * 128 * TOK]
                                src = src.rearrange("(p t) -> p t", p=128)
                                i = syn.dma_start(
                                    out=kh[:, gb, half * TOK:(half + 1) * TOK],
                                    in_=src)
                                sy[gi % 2].inc(i, 16)
                            for half in range(2):
                                src = cag.ap()[half, voff:voff + TOK * Dv]
                                src = src.rearrange(
                                    "(a p d) -> p a d", p=128, d=Dv)
                                src = src[:, :, gi * 128:(gi + 1) * 128]
                                i = syn.dma_start(
                                    out=vh[:, gb, half * KCh:(half + 1) * KCh],
                                    in_=src)
                                sy[gi % 2].inc(i, 16)
                        syn.wait_ge(sy[0].h, sy[0].n)
                        syn.wait_ge(sy[1].h, sy[1].n)
                    blk.sync(f_sy)

                    if expansion:
                        def f_gp(g):
                            expansion_gp_ops(g, tiles, eibf, ewb, ex0)
                            for c_ in (exL[0], exL[1], exS[0], exS[1]):
                                g.wait_ge(c_.h, c_.n)
                            i = g.collective_compute(
                                "AllGather", ALU.bypass,
                                replica_groups=[list(range(8))],
                                ins=[cag2_in.ap().opt()],
                                outs=[cag2_out.ap().opt()])
                            sC2.inc(i)
                        blk.gpsimd(f_gp)

                    def f_te(te):
                        for h in range(Hq):
                            gi = h // hpg
                            gb = gi % nbuf
                            te.wait_ge(*kv_m(gi))
                            if h >= 1:
                                te.wait_ge(sSC.h, copy_m(h - 1))
                                te.wait_ge(sVE.h, evac_m(h - 1))
                            for kc in range(KC):
                                if kc >= 2:
                                    te.wait_ge(sSC.h, exp_m(h, kc - 2))
                                last = None
                                for n in range(TN):
                                    last = te.matmul(
                                        stp[:, kc % 2, nsl(n)],
                                        kh[:, gb, kc * 128:(kc + 1) * 128],
                                        qbuf[:, h, nsl(n)],
                                        start=True, stop=True)
                                sTE.inc(last)
                            last = None
                            for kc in range(KC):
                                te.wait_ge(sSC.h, exp_m(h, kc))
                                if masked:
                                    te.wait_ge(sVE.h, msk_m(h, kc))
                                for n in range(TN):
                                    te.matmul(pvp[:, nsl(n)],
                                              vh[:, gb, kc],
                                              ptb[:, kc, nsl(n)],
                                              start=(kc == 0),
                                              stop=(kc == KC - 1))
                                for n in range(TN):
                                    last = te.matmul(dnp[:, nsl(n)],
                                                     ones_col_bf[:],
                                                     ptb[:, kc, nsl(n)],
                                                     start=(kc == 0),
                                                     stop=(kc == KC - 1))
                            sTE.inc(last)
                            te.wait_ge(sVE.h, recip_m(h))
                            last = None
                            for n in range(TN):
                                last = te.matmul(stp[:, 0, nsl(n)],
                                                 ones_row_f[:],
                                                 rvec[:, nsl(n)],
                                                 start=True, stop=True)
                            sTE.inc(last)
                    blk.tensor(f_te)

                    def f_sc(se):
                        for h in range(Hq):
                            for kc in range(KC):
                                se.wait_ge(sTE.h, st_m(h, kc))
                                i = se.activation(ptb[:, kc], stp[:, kc % 2],
                                                  AF.Exp, bias=negshift[:],
                                                  scale=SCALE)
                                sSC.inc(i)
                            se.wait_ge(sTE.h, pv_m(h))
                            if h >= 1:
                                se.wait_ge(sVE.h, evac_m(h - 1))
                            i = se.activation(otmp[:], pvp[:], AF.Copy)
                            sSC.inc(i)
                    blk.scalar(f_sc)

                    def f_ve(ve):
                        if masked:
                            ve.wait_ge(sMK.h, mask_w)
                        if expansion:
                            ve.wait_ge(sW.h, p0_w)
                        for h in range(Hq):
                            if masked:
                                for kc in range(KC):
                                    ve.wait_ge(sSC.h, exp_m(h, kc))
                                    i = ve.tensor_mul(ptb[:, kc], ptb[:, kc],
                                                      msk[:, kc])
                                    sVE.inc(i)
                            ve.wait_ge(sTE.h, pv_m(h))
                            i = ve.reciprocal(rvec[:], dnp[:])
                            sVE.inc(i)
                            ve.wait_ge(sTE.h, bc_m(h))
                            ve.wait_ge(sSC.h, copy_m(h))
                            i = ve.tensor_mul(qbuf[:, h], otmp[:], stp[:, 0])
                            sVE.inc(i)
                            if expansion and h < len(tiles):
                                expansion_ve_tile(ve, tiles, h, eibf, ewb,
                                                  etb, exl0, exs0)
                        if expansion:
                            for t in range(Hq, len(tiles)):
                                expansion_ve_tile(ve, tiles, t, eibf, ewb,
                                                  etb, exl0, exs0)
                    blk.vector(f_ve)

        def emit_outproj(wt, qb=None):
            qbuf = qb
            with (
                nc.sbuf_tensor(uid("op_w"), [128, 2, DC, 128], BF16) as wti,
                nc.psum_tensor(uid("op_ps"), [128, 2, TOK], F32) as ops,
                nc.Block() as blk,
            ):
                ve0, te0 = sVE.n, sTE.n
                g0 = [gp[0].n, gp[1].n]
                tm = lambda j: te0 + j + 1
                em = lambda j: ve0 + j + 1
                gm = lambda j: g0[j % 2] + 16 * (j // 2 + 1)

                def f_gp(g):
                    for j in range(DC):
                        if j >= 2:
                            g.wait_ge(sTE.h, tm(j - 2))
                        i = g.dma_start(out=wti[:, j % 2],
                                        in_=colslice(wt, j * 128, 128))
                        gp[j % 2].inc(i, 16)
                    g.wait_ge(gp[0].h, gp[0].n)
                    g.wait_ge(gp[1].h, gp[1].n)
                blk.gpsimd(f_gp)

                def f_te(te):
                    for j in range(DC):
                        te.wait_ge(gp[j % 2].h, gm(j))
                        if j >= 2:
                            te.wait_ge(sVE.h, em(j - 2))
                        last = None
                        for oc in range(H):
                            for n in range(TN):
                                last = te.matmul(
                                    ops[:, j % 2, nsl(n)], wti[:, j % 2, oc],
                                    qbuf[:, oc, nsl(n)],
                                    start=(oc == 0), stop=(oc == H - 1))
                        sTE.inc(last)
                blk.tensor(f_te)

                def f_ve(ve):
                    for j in range(DC):
                        ve.wait_ge(sTE.h, tm(j))
                        i = ve.tensor_add(xsb[:, j], xsb[:, j],
                                          ops[:, j % 2])
                        sVE.inc(i)
                blk.vector(f_ve)

        def emit_ffn(wait_c2):
            fsl = lambda n: slice(n * FT, (n + 1) * FT)
            with nc.sbuf_tensor(uid("ff_h"), [128, IC, TOK], BF16) as hbuf:
                jobs = [(oc, th) for oc in range(IC) for th in range(FTN)]
                with (
                    nc.sbuf_tensor(uid("ff_wg"), [128, 2, DC, 128], BF16) as wg,
                    nc.sbuf_tensor(uid("ff_wu"), [128, 2, DC, 128], BF16) as wu,
                    nc.sbuf_tensor(uid("ff_st"), [128, 2, FT], F32) as stmp,
                    nc.psum_tensor(uid("ff_g"), [128, 2, FT], F32) as gps,
                    nc.psum_tensor(uid("ff_u"), [128, 2, FT], F32) as ups,
                    nc.Block() as blk,
                ):
                    ve0, sc0, te0 = sVE.n, sSC.n, sTE.n
                    s0 = [sy[0].n, sy[1].n]
                    g0 = [gp[0].n, gp[1].n]
                    tm = lambda j: te0 + j + 1
                    sim = lambda j: sc0 + j + 1
                    hm = lambda j: ve0 + j + 1
                    ndg = 8
                    wms = lambda oc: (sy[oc % 2].h,
                                      s0[oc % 2] + 16 * ndg * (oc // 2 + 1))
                    wmg = lambda oc: (gp[oc % 2].h,
                                      g0[oc % 2] + 16 * ndg * (oc // 2 + 1))

                    def gu_loads(q, ctrs, sec, dstb):
                        if wait_c2:
                            q.wait_ge(sC2.h, sC2.n)
                        for oc in range(IC):
                            if oc >= 2:
                                q.wait_ge(sTE.h, tm((oc - 2) * FTN + FTN - 1))
                            if RG % 128 == 0:
                                rgc = RG // 128
                                for r in range(8):
                                    src = cag2_out.ap()[
                                        r, sec * SEC:sec * SEC + RG * INTER]
                                    src = src.rearrange("(a p i) -> p a i",
                                                        p=128, i=INTER)
                                    src = src[:, :, oc * 128:(oc + 1) * 128]
                                    dst = dstb[:, oc % 2,
                                               r * rgc:(r + 1) * rgc]
                                    i = q.dma_start(out=dst, in_=src)
                                    ctrs[oc % 2].inc(i, 16)
                            else:
                                for r in range(8):
                                    src = cag2_out.ap()[
                                        r, sec * SEC:sec * SEC + RG * INTER]
                                    src = src.rearrange("(a i) -> a i",
                                                        i=INTER)
                                    src = src[:, oc * 128:(oc + 1) * 128]
                                    row0 = r * RG
                                    dst = dstb[row0 % 128:row0 % 128 + RG,
                                               oc % 2, row0 // 128]
                                    i = q.dma_start(out=dst, in_=src)
                                    ctrs[oc % 2].inc(i, 16)
                        q.wait_ge(ctrs[0].h, ctrs[0].n)
                        q.wait_ge(ctrs[1].h, ctrs[1].n)

                    blk.sync(lambda syn: gu_loads(syn, sy, 0, wg))
                    blk.gpsimd(lambda g: gu_loads(g, gp, 1, wu))

                    def f_te(te):
                        for j, (oc, th) in enumerate(jobs):
                            if th == 0:
                                te.wait_ge(*wms(oc))
                                te.wait_ge(*wmg(oc))
                            if j >= 2:
                                te.wait_ge(sSC.h, sim(j - 2))
                                te.wait_ge(sVE.h, hm(j - 2))
                            for dc in range(DC):
                                te.matmul(gps[:, j % 2], wg[:, oc % 2, dc],
                                          xn[:, dc, fsl(th)],
                                          start=(dc == 0), stop=(dc == DC - 1))
                            last = None
                            for dc in range(DC):
                                last = te.matmul(
                                    ups[:, j % 2], wu[:, oc % 2, dc],
                                    xn[:, dc, fsl(th)],
                                    start=(dc == 0), stop=(dc == DC - 1))
                            sTE.inc(last)
                    blk.tensor(f_te)

                    def f_sc(se):
                        for j, (oc, th) in enumerate(jobs):
                            se.wait_ge(sTE.h, tm(j))
                            i = se.activation(stmp[:, j % 2],
                                              gps[:, j % 2], AF.Silu)
                            sSC.inc(i)
                    blk.scalar(f_sc)

                    def f_ve(ve):
                        for j, (oc, th) in enumerate(jobs):
                            ve.wait_ge(sSC.h, sim(j))
                            i = ve.tensor_mul(hbuf[:, oc, fsl(th)],
                                              stmp[:, j % 2],
                                              ups[:, j % 2])
                            sVE.inc(i)
                    blk.vector(f_ve)

                with (
                    nc.sbuf_tensor(uid("ff_wd"), [128, 2, IC, 128], BF16) as wd,
                    nc.psum_tensor(uid("ff_d"), [128, 2, FT], F32) as dps,
                    nc.Block() as blk,
                ):
                    djobs = [(dj, th) for dj in range(DC) for th in range(FTN)]
                    ve0, te0 = sVE.n, sTE.n
                    s0 = [sy[0].n, sy[1].n]
                    tm = lambda j: te0 + j + 1
                    em = lambda j: ve0 + j + 1

                    def down_dmas(dj, dstb):
                        out = []
                        col0 = dj * 128
                        done = 0
                        while done < 128:
                            c = col0 + done
                            r = c // RDC
                            lc = c % RDC
                            w = min(RDC - lc, 128 - done)
                            src = cag2_out.ap()[r, 2 * SEC:3 * SEC]
                            src = src.rearrange("(a p c) -> p a c",
                                                p=128, c=RDC)
                            src = src[:, :, lc:lc + w]
                            dst = dstb[:, :, done:done + w]
                            out.append((dst, src))
                            done += w
                        return out

                    def f_sy(syn):
                        for dj in range(DC):
                            if dj >= 2:
                                syn.wait_ge(sTE.h,
                                            tm((dj - 2) * FTN + FTN - 1))
                            for dst, src in down_dmas(dj, wd[:, dj % 2]):
                                i = syn.dma_start(out=dst, in_=src)
                                sy[dj % 2].inc(i, 16)
                        syn.wait_ge(sy[0].h, sy[0].n)
                        syn.wait_ge(sy[1].h, sy[1].n)
                    blk.sync(f_sy)

                    wm_pred = []
                    acc = [s0[0], s0[1]]
                    for dj in range(DC):
                        acc[dj % 2] += 16 * len(down_dmas(dj, wd[:, 0]))
                        wm_pred.append((sy[dj % 2].h, acc[dj % 2]))

                    def f_te(te):
                        for j, (dj, th) in enumerate(djobs):
                            if th == 0:
                                te.wait_ge(*wm_pred[dj])
                            if j >= 2:
                                te.wait_ge(sVE.h, em(j - 2))
                            last = None
                            for oc in range(IC):
                                last = te.matmul(
                                    dps[:, j % 2], wd[:, dj % 2, oc],
                                    hbuf[:, oc, fsl(th)],
                                    start=(oc == 0), stop=(oc == IC - 1))
                            sTE.inc(last)
                    blk.tensor(f_te)

                    def f_ve(ve):
                        for j, (dj, th) in enumerate(djobs):
                            ve.wait_ge(sTE.h, tm(j))
                            i = ve.tensor_add(xsb[:, dj, fsl(th)],
                                              xsb[:, dj, fsl(th)],
                                              dps[:, j % 2])
                            sVE.inc(i)
                    blk.vector(f_ve)

        first_ln = [True]

        def ln():
            emit_ln(first=first_ln[0])
            first_ln[0] = False

        need_ffn = ("f1" in stages) or ("f2" in stages)
        if "a" in stages:
            ln()
            with nc.sbuf_tensor("qbuf_a", [128, H, TOK], BF16) as qa:
                jobs_k = [(H + h, h) for h in range(H)]
                emit_proj(wqkvT, jobs_k, "ropek", cag1_in, 0)
                emit_vproj(lambda c0, w: colslice(wqkvT, 2 * H * 128 + c0, w),
                           D, VOFF1, cag1_in, D)
                emit_ag(cag1_in, cag1_out, sC1,
                        [(sy[0].h, sy[0].n), (sy[1].h, sy[1].n)],
                        groups=PAIRS)
                jobs_qk = [(h, h) for h in range(H)]
                emit_proj(wqkvT, jobs_qk, "ropeq", qb=qa)
                emit_attention("a", expansion=need_ffn, qb=qa)
                emit_outproj(woT, qb=qa)
        elif need_ffn:
            exp_waits = emit_expansion()
            emit_ag(cag2_in, cag2_out, sC2, exp_waits)
        if "f1" in stages:
            ln()
            emit_ffn(wait_c2=True)
        if "b" in stages:
            ln()
            with nc.sbuf_tensor("qbuf_b", [128, H, TOK], BF16) as qb_:
                jobs_kb = [(g, g) for g in range(NKV)]
                emit_proj(wkb, jobs_kb, "copyk", cag3_in, 0)
                emit_vproj(lambda c0, w: wvb.ap().rearrange(
                    "(c p) o -> p c o", p=128)[:, :, c0:c0 + w],
                    KVD, VOFF3, cag3_in, KVD)
                emit_ag(cag3_in, cag3_out, sC3,
                        [(sy[0].h, sy[0].n), (sy[1].h, sy[1].n)],
                        groups=PAIRS)
                jobs_qb = [(h, h) for h in range(H)]
                emit_proj(wqb, jobs_qb, "copyq", qb=qb_)
                emit_attention("b", qb=qb_)
                emit_outproj(wob, qb=qb_)
        if "f2" in stages:
            ln()
            emit_ffn(wait_c2=("f1" not in stages))

        with nc.Block() as blk:
            def f_sy(sy):
                i = sy.dma_start(out=y.ap().rearrange("(c p) t -> p c t",
                                                      p=128), in_=xsb[:])
                sW.inc(i, 16)
                sy.wait_ge(sW.h, sW.n)
            blk.sync(f_sy)

    if verbose:
        print(f"instructions: {nc.n_instructions()}")
    return nc



def _rope_tables(S):
    inv = 1.0 / (10000.0 ** (np.arange(0, 128, 2, dtype=np.float32) / 128.0))
    t = np.arange(S, dtype=np.float32)
    fr = np.einsum("i,j->ij", t, inv)
    emb = np.concatenate([fr, fr], axis=-1)
    return np.cos(emb).astype(np.float32), np.sin(emb).astype(np.float32)


def make_in_maps(cfg, x, w_qkv, w_out, Wq, Wk, Wv, Wo, lut, gate_idx, up_idx,
                 down_idx):
    D, TOK, S = cfg.D, cfg.TOK, cfg.S
    f32 = lambda a: np.ascontiguousarray(np.asarray(a, dtype=np.float32))
    i32 = lambda a: np.ascontiguousarray(np.asarray(a, dtype=np.int32))
    wqkvT = f32(np.asarray(w_qkv).T)
    woT = f32(np.asarray(w_out).T)
    wq, wk, wv, wo = f32(Wq), f32(Wk), f32(Wv), f32(Wo)
    gT = i32(np.asarray(gate_idx).T)
    uT = i32(np.asarray(up_idx).T)
    dT = i32(np.asarray(down_idx).T)
    lutbc = f32(np.tile(np.asarray(lut, dtype=np.float32)[None, :], (128, 1)))
    cos, sin = _rope_tables(S)
    in_maps = []
    for c in range(8):
        b, hh = c // 2, c % 2
        tok = slice(hh * TOK, (hh + 1) * TOK)
        kk = np.arange(S, dtype=np.int64)[:, None]
        qq = hh * TOK + np.arange(TOK, dtype=np.int64)[None, :]
        mask = (kk <= qq).astype(ml_dtypes.bfloat16)
        in_maps.append({
            "xin": f32(np.asarray(x)[b, tok, :].T),
            "wqkvT": wqkvT, "woT": woT,
            "wqb": wq, "wkb": wk, "wvb": wv, "wob": wo,
            "cosq": f32(cos[tok, :].T), "sinq": f32(sin[tok, :].T),
            "maskb": np.ascontiguousarray(mask),
            "lutin": lutbc,
            "gsl": np.ascontiguousarray(gT[c * cfg.RG:(c + 1) * cfg.RG, :]),
            "usl": np.ascontiguousarray(uT[c * cfg.RG:(c + 1) * cfg.RG, :]),
            "dsl": np.ascontiguousarray(dT[:, c * cfg.RDC:(c + 1) * cfg.RDC]),
        })
    return in_maps


def assemble(cfg, results):
    B, S, D, TOK = cfg.B, cfg.S, cfg.D, cfg.TOK
    out = np.empty((B, S, D), dtype=np.float32)
    for c in range(8):
        b, hh = c // 2, c % 2
        out[b, hh * TOK:(hh + 1) * TOK, :] = results[c]["y"].T
    return out


def kernel(L0_attn_a_x, L0_attn_a_w_qkv, L0_attn_a_w_out, L0_attn_b_Wq,
           L0_attn_b_Wk, L0_attn_b_Wv, L0_attn_b_Wo, L0_ffn_lut,
           L0_ffn_gate_idx, L0_ffn_up_idx, L0_ffn_down_idx):
    cfg = FULL
    nc = build(cfg)
    in_maps = make_in_maps(cfg, L0_attn_a_x, L0_attn_a_w_qkv, L0_attn_a_w_out,
                           L0_attn_b_Wq, L0_attn_b_Wk, L0_attn_b_Wv,
                           L0_attn_b_Wo, L0_ffn_lut, L0_ffn_gate_idx,
                           L0_ffn_up_idx, L0_ffn_down_idx)
    res = run_bass_kernel_spmd(nc, in_maps, core_ids=list(range(8)))
    return assemble(cfg, res.results)


# version 06
# speedup vs baseline: 1.3569x; 1.0019x over previous
import math
from contextlib import ExitStack

import numpy as np
import ml_dtypes
import concourse.bass as bass
import concourse.mybir as mybir
from concourse.bass_utils import run_bass_kernel_spmd

F32 = mybir.dt.float32
BF16 = mybir.dt.bfloat16
I32 = mybir.dt.int32
AF = mybir.ActivationFunctionType
ALU = mybir.AluOpType

SCALE = 1.0 / math.sqrt(128.0)
SHIFT = 12.0
EPS = 1e-5
FDMAX = 2304
PAIRS = [[0, 1], [2, 3], [4, 5], [6, 7]]


class Cfg:
    def __init__(self, H, NKV, S, IC, B=4):
        self.H = H
        self.NKV = NKV
        self.G = H // NKV
        self.D = H * 128
        self.DC = self.D // 128
        self.S = S
        self.B = B
        self.TOK = B * S // 8
        assert self.TOK == S // 2
        self.KC = S // 128
        self.KCh = self.TOK // 128
        self.IC = IC
        self.INTER = IC * 128
        self.KVD = NKV * 128
        self.NT = min(512, self.TOK)
        self.TN = self.TOK // self.NT
        self.FT = min(512, self.TOK)
        self.FTN = self.TOK // self.FT
        self.RG = self.D // 8
        self.RDC = self.D // 8
        self.SEC = self.D * self.INTER // 8


FULL = Cfg(H=16, NKV=4, S=2048, IC=43)
SMALL = Cfg(H=2, NKV=1, S=512, IC=2)


def build(cfg, verbose=False, stages=('a', 'f1', 'b', 'f2')):
    H, D, DC, TOK, KC, KCh = cfg.H, cfg.D, cfg.DC, cfg.TOK, cfg.KC, cfg.KCh
    IC, KVD, NT, TN = cfg.IC, cfg.KVD, cfg.NT, cfg.TN
    S, G, NKV, INTER, RG, RDC, SEC = (cfg.S, cfg.G, cfg.NKV, cfg.INTER,
                                      cfg.RG, cfg.RDC, cfg.SEC)
    FT, FTN = cfg.FT, cfg.FTN

    nc = bass.Bass("TRN2", target_bir_lowering=False)

    xin = nc.dram_tensor("xin", [D, TOK], F32, kind="ExternalInput")
    wqkvT = nc.dram_tensor("wqkvT", [D, 3 * D], F32, kind="ExternalInput")
    woT = nc.dram_tensor("woT", [D, D], F32, kind="ExternalInput")
    wqb = nc.dram_tensor("wqb", [D, D], F32, kind="ExternalInput")
    wkb = nc.dram_tensor("wkb", [D, KVD], F32, kind="ExternalInput")
    wvb = nc.dram_tensor("wvb", [D, KVD], F32, kind="ExternalInput")
    wob = nc.dram_tensor("wob", [D, D], F32, kind="ExternalInput")
    cosq = nc.dram_tensor("cosq", [128, TOK], F32, kind="ExternalInput")
    sinq = nc.dram_tensor("sinq", [128, TOK], F32, kind="ExternalInput")
    maskb = nc.dram_tensor("maskb", [S, TOK], BF16, kind="ExternalInput")
    lutin = nc.dram_tensor("lutin", [128, 16], F32, kind="ExternalInput")
    gsl = nc.dram_tensor("gsl", [RG, INTER], I32, kind="ExternalInput")
    usl = nc.dram_tensor("usl", [RG, INTER], I32, kind="ExternalInput")
    dsl = nc.dram_tensor("dsl", [INTER, RDC], I32, kind="ExternalInput")
    y = nc.dram_tensor("y", [D, TOK], F32, kind="ExternalOutput")

    L1 = 2 * D * TOK
    cag1_in = nc.dram_tensor("cag1_in", [L1], BF16)
    cag1_out = nc.dram_tensor("cag1_out", [2, L1], BF16)
    L2 = 3 * SEC
    cag2_in = nc.dram_tensor("cag2_in", [L2], BF16)
    cag2_out = nc.dram_tensor("cag2_out", [8, L2], BF16, addr_space="Shared")
    L3 = 2 * KVD * TOK
    cag3_in = nc.dram_tensor("cag3_in", [L3], BF16)
    cag3_out = nc.dram_tensor("cag3_out", [2, L3], BF16)
    VOFF1 = D * TOK
    VOFF3 = KVD * TOK

    class Ctr:
        def __init__(self, name):
            self.h = nc.alloc_semaphore(name)
            self.n = 0

        def inc(self, instr, k=1):
            instr.then_inc(self.h, k)
            self.n += k
            return self.n

    sW = Ctr("sW")
    gp = [Ctr("gp0"), Ctr("gp1")]
    sy = [Ctr("sy0"), Ctr("sy1")]
    exL = [Ctr("exL0"), Ctr("exL1")]
    exS = [Ctr("exS0"), Ctr("exS1")]
    sEX = Ctr("sEX")
    sMK = Ctr("sMK")
    sC1 = Ctr("sC1")
    sC2 = Ctr("sC2")
    sC3 = Ctr("sC3")
    sTE = Ctr("sTE")
    sSC = Ctr("sSC")
    sVE = Ctr("sVE")

    _uid = [0]

    def uid(s):
        _uid[0] += 1
        return f"{s}{_uid[0]}"

    def nsl(n):
        return slice(n * NT, (n + 1) * NT)

    def colslice(t, j0, w):
        return t.ap().rearrange("(c p) o -> p c o", p=128)[:, :, j0:j0 + w]

    st = ExitStack()
    with st:
        sb = lambda name, shape, dt: st.enter_context(
            nc.sbuf_tensor(name, shape, dt))
        xsb = sb("xsb", [128, DC, TOK], F32)
        xn = sb("xnb", [128, DC, TOK], BF16)
        luts = sb("luts", [128, 16], F32)
        ones_col_bf = sb("ones_col_bf", [128, 1], BF16)
        ones_row_f = sb("ones_row_f", [1, 128], F32)
        ones_row_bf = sb("ones_row_bf", [1, 128], BF16)
        negshift = sb("negshift", [128, 1], F32)

        with nc.Block() as blk:
            def f_sy(sy):
                sy.dma_start(out=xsb[:], in_=xin.ap().rearrange(
                    "(c p) t -> p c t", p=128)).then_inc(sW.h, 16)
                sW.n += 16
                sy.dma_start(out=luts[:], in_=lutin.ap()).then_inc(sW.h, 16)
                sW.n += 16
            blk.sync(f_sy)

            def f_ve(ve):
                ve.memset(ones_col_bf[:], 1.0)
                ve.memset(ones_row_f[:], 1.0)
                ve.memset(ones_row_bf[:], 1.0)
                ve.memset(negshift[:], -SHIFT)
            blk.vector(f_ve)
        p0_w = sW.n

        def emit_ln(first=False):
            with (
                nc.sbuf_tensor(uid("ln_mu"), [1, TOK], F32) as mu,
                nc.sbuf_tensor(uid("ln_es"), [1, TOK], F32) as es,
                nc.sbuf_tensor(uid("ln_rs"), [1, TOK], F32) as rsd,
                nc.sbuf_tensor(uid("ln_mr"), [1, TOK], F32) as mr,
                nc.sbuf_tensor(uid("ln_sq"), [128, 2, TOK], BF16) as sqb,
                nc.sbuf_tensor(uid("ln_nrm"), [128, 2, TOK], F32) as nrm,
                nc.psum_tensor(uid("ln_sum"), [1, TOK], F32) as sumps,
                nc.psum_tensor(uid("ln_sqp"), [1, TOK], F32) as sqps,
                nc.psum_tensor(uid("ln_br"), [128, TOK], F32) as brps,
                nc.psum_tensor(uid("ln_bm"), [128, TOK], F32) as bmps,
                nc.Block() as blk,
            ):
                ve0, sc0, te0 = sVE.n, sSC.n, sTE.n
                copy_m = lambda c: ve0 + c + 1
                sq_m = lambda c: sc0 + c + 1
                sum_m = lambda c: te0 + c + 1
                stats1_m = ve0 + DC + 1
                sqrt_m = sc0 + DC + 1
                stats2_m = ve0 + DC + 2
                bcast_m = te0 + DC + 1

                def f_ve(ve):
                    if first:
                        ve.wait_ge(sW.h, p0_w)
                    for c in range(DC):
                        i = ve.tensor_copy(xn[:, c], xsb[:, c])
                        sVE.inc(i)
                    ve.wait_ge(sTE.h, sum_m(DC - 1))
                    ve.tensor_scalar(mu[:], sumps[:], 1.0 / D, None, ALU.mult)
                    ve.tensor_scalar(es[:], sqps[:], 1.0 / D, None, ALU.mult)
                    ve.drain()
                    ve.tensor_mul(rsd[:], mu[:], mu[:])
                    ve.drain()
                    ve.tensor_sub(es[:], es[:], rsd[:])
                    ve.drain()
                    i = ve.tensor_scalar(es[:], es[:], EPS, None, ALU.add)
                    sVE.inc(i)
                    ve.wait_ge(sSC.h, sqrt_m)
                    ve.reciprocal(rsd[:], es[:])
                    ve.drain()
                    i = ve.tensor_mul(mr[:], mu[:], rsd[:])
                    sVE.inc(i)
                    for c in range(DC):
                        ve.wait_ge(sTE.h, bcast_m)
                        ve.tensor_mul(nrm[:, c % 2], xsb[:, c], brps[:])
                        ve.drain()
                        i = ve.tensor_sub(xn[:, c], nrm[:, c % 2], bmps[:])
                        sVE.inc(i)
                blk.vector(f_ve)

                def f_sc(se):
                    for c in range(DC):
                        se.wait_ge(sVE.h, copy_m(c))
                        if c >= 2:
                            se.wait_ge(sTE.h, sum_m(c - 2))
                        i = se.activation(sqb[:, c % 2], xn[:, c], AF.Square)
                        sSC.inc(i)
                    se.wait_ge(sVE.h, stats1_m)
                    i = se.activation(es[:], es[:], AF.Sqrt)
                    sSC.inc(i)
                blk.scalar(f_sc)

                def f_te(te):
                    for c in range(DC):
                        te.wait_ge(sVE.h, copy_m(c))
                        te.wait_ge(sSC.h, sq_m(c))
                        for n in range(TN):
                            te.matmul(sumps[:, nsl(n)], ones_col_bf[:],
                                      xn[:, c, nsl(n)],
                                      start=(c == 0), stop=(c == DC - 1))
                        last = None
                        for n in range(TN):
                            last = te.matmul(sqps[:, nsl(n)], ones_col_bf[:],
                                             sqb[:, c % 2, nsl(n)],
                                             start=(c == 0), stop=(c == DC - 1))
                        sTE.inc(last)
                    te.wait_ge(sVE.h, stats2_m)
                    for n in range(TN):
                        te.matmul(brps[:, nsl(n)], ones_row_f[:],
                                  rsd[:, nsl(n)], start=True, stop=True)
                    last = None
                    for n in range(TN):
                        last = te.matmul(bmps[:, nsl(n)], ones_row_f[:],
                                         mr[:, nsl(n)], start=True, stop=True)
                    sTE.inc(last)
                blk.tensor(f_te)

        def emit_proj(wt, jobs, kind, cag_in_t=None, koff=0, qb=None):
            rope = kind.startswith("rope")
            to_cag = kind.endswith("k")
            qbuf = qb
            with (
                nc.sbuf_tensor(uid("pj_w"), [128, 2, DC, 128], BF16) as wqk,
                nc.sbuf_tensor(uid("pj_kst"), [128, 2, TOK], BF16) as kst,
                nc.sbuf_tensor(uid("pj_rt"), [128, 2, TOK], F32) as rtp,
                nc.sbuf_tensor(uid("pj_ut"), [128, 2, TOK], F32) as utp,
                nc.sbuf_tensor(uid("pj_cos"), [128, TOK], F32) as cosb,
                nc.sbuf_tensor(uid("pj_sin"), [128, TOK], F32) as sinb,
                nc.psum_tensor(uid("pj_ps"), [128, 2, TOK], F32) as pps,
                nc.Block() as blk,
            ):
                n_j = len(jobs)
                ve0, sc0, te0, w0 = sVE.n, sSC.n, sTE.n, sW.n
                g0 = [gp[0].n, gp[1].n]
                s0 = [sy[0].n, sy[1].n]
                tm = lambda j: te0 + j + 1
                em0 = ve0 if rope else sc0
                em = lambda j: em0 + j + 1
                gm = lambda j: g0[j % 2] + 16 * (j // 2 + 1)
                kdma_m = lambda j: s0[j % 2] + 16 * (j // 2 + 1)

                def f_sy(syn):
                    if rope:
                        syn.dma_start(out=cosb[:], in_=cosq.ap()).then_inc(
                            sW.h, 16)
                        sW.n += 16
                        syn.dma_start(out=sinb[:], in_=sinq.ap()).then_inc(
                            sW.h, 16)
                        sW.n += 16
                    if to_cag:
                        eh = sVE.h if rope else sSC.h
                        for j, (oc, hidx) in enumerate(jobs):
                            syn.wait_ge(eh, em(j))
                            dst = cag_in_t.ap()[
                                (koff + hidx * 128) * TOK:
                                (koff + (hidx + 1) * 128) * TOK]
                            dst = dst.rearrange("(p t) -> p t", p=128)
                            i = syn.dma_start(out=dst, in_=kst[:, j % 2])
                            sy[j % 2].inc(i, 16)
                        syn.wait_ge(sy[0].h, sy[0].n)
                        syn.wait_ge(sy[1].h, sy[1].n)
                blk.sync(f_sy)

                def f_gp(g):
                    for j, (oc, hidx) in enumerate(jobs):
                        if j >= 2:
                            g.wait_ge(sTE.h, tm(j - 2))
                        i = g.dma_start(out=wqk[:, j % 2],
                                        in_=colslice(wt, oc * 128, 128))
                        gp[j % 2].inc(i, 16)
                    g.wait_ge(gp[0].h, gp[0].n)
                    g.wait_ge(gp[1].h, gp[1].n)
                blk.gpsimd(f_gp)

                def f_te(te):
                    for j, (oc, hidx) in enumerate(jobs):
                        te.wait_ge(gp[j % 2].h, gm(j))
                        if j >= 2:
                            te.wait_ge(sVE.h if rope else sSC.h, em(j - 2))
                        last = None
                        for dc in range(DC):
                            for n in range(TN):
                                last = te.matmul(
                                    pps[:, j % 2, nsl(n)], wqk[:, j % 2, dc],
                                    xn[:, dc, nsl(n)],
                                    start=(dc == 0), stop=(dc == DC - 1))
                        sTE.inc(last)
                blk.tensor(f_te)

                if rope:
                    def f_ve(ve):
                        if rope:
                            ve.wait_ge(sW.h, sW.n)
                        for j, (oc, hidx) in enumerate(jobs):
                            ve.wait_ge(sTE.h, tm(j))
                            ps = pps[:, j % 2]
                            if to_cag:
                                if j >= 2:
                                    ve.wait_ge(sy[j % 2].h, kdma_m(j - 2))
                                dst = kst[:, j % 2]
                            else:
                                dst = qbuf[:, hidx]
                            t = rtp[:, j % 2]
                            u = utp[:, j % 2]
                            ve.tensor_mul(t, ps, cosb[:])
                            ve.tensor_mul(u[0:64], ps[64:128], sinb[0:64])
                            ve.tensor_mul(u[64:128], ps[0:64], sinb[64:128])
                            ve.drain()
                            ve.tensor_sub(dst[0:64], t[0:64], u[0:64])
                            i = ve.tensor_add(dst[64:128], t[64:128],
                                              u[64:128])
                            sVE.inc(i)
                    blk.vector(f_ve)
                else:
                    def f_sc(se):
                        for j, (oc, hidx) in enumerate(jobs):
                            se.wait_ge(sTE.h, tm(j))
                            if to_cag:
                                if j >= 2:
                                    se.wait_ge(sy[j % 2].h, kdma_m(j - 2))
                                dst = kst[:, j % 2]
                            else:
                                dst = qbuf[:, hidx]
                            i = se.activation(dst, pps[:, j % 2], AF.Copy)
                            sSC.inc(i)
                    blk.scalar(f_sc)

        def emit_vproj(wsrc_fn, out_cols, voff, cag_in_t, Dv):
            NTv = min(512, out_cols)
            NOS = out_cols // NTv
            TC = TOK // 128
            with (
                nc.sbuf_tensor(uid("vp_w"), [128, 2, DC, NTv], BF16) as wv4,
                nc.sbuf_tensor(uid("vp_st"), [128, 2, NTv], BF16) as vst,
                nc.psum_tensor(uid("vp_ps"), [128, 2, NTv], F32) as vps,
                nc.Block() as blk,
            ):
                jobs = [(o, tc) for o in range(NOS) for tc in range(TC)]
                sc0, te0 = sSC.n, sTE.n
                g0 = [gp[0].n, gp[1].n]
                s0 = [sy[0].n, sy[1].n]
                tm = lambda j: te0 + j + 1
                scm = lambda j: sc0 + j + 1
                gm = lambda o: g0[o % 2] + 16 * (o // 2 + 1)
                vdma_m = lambda j: s0[j % 2] + 16 * (j // 2 + 1)

                def f_gp(g):
                    for o in range(NOS):
                        if o >= 2:
                            g.wait_ge(sTE.h, te0 + (o - 1) * TC)
                        i = g.dma_start(out=wv4[:, o % 2],
                                        in_=wsrc_fn(o * NTv, NTv))
                        gp[o % 2].inc(i, 16)
                    g.wait_ge(gp[0].h, gp[0].n)
                    g.wait_ge(gp[1].h, gp[1].n)
                blk.gpsimd(f_gp)

                def f_te(te):
                    for j, (o, tc) in enumerate(jobs):
                        if tc == 0:
                            te.wait_ge(gp[o % 2].h, gm(o))
                        if j >= 2:
                            te.wait_ge(sSC.h, scm(j - 2))
                        last = None
                        for dc in range(DC):
                            last = te.matmul(
                                vps[:, j % 2], xn[:, dc, tc * 128:(tc + 1) * 128],
                                wv4[:, o % 2, dc],
                                start=(dc == 0), stop=(dc == DC - 1))
                        sTE.inc(last)
                blk.tensor(f_te)

                def f_sc(se):
                    for j, (o, tc) in enumerate(jobs):
                        se.wait_ge(sTE.h, tm(j))
                        if j >= 2:
                            se.wait_ge(sy[j % 2].h, vdma_m(j - 2))
                        i = se.activation(vst[:, j % 2], vps[:, j % 2],
                                          AF.Copy)
                        sSC.inc(i)
                blk.scalar(f_sc)

                def f_sy(syn):
                    for j, (o, tc) in enumerate(jobs):
                        syn.wait_ge(sSC.h, scm(j))
                        dst = cag_in_t.ap()[voff:voff + TOK * Dv]
                        dst = dst.rearrange("(t d) -> t d", d=Dv)
                        dst = dst[tc * 128:(tc + 1) * 128,
                                  o * NTv:(o + 1) * NTv]
                        i = syn.dma_start(out=dst, in_=vst[:, j % 2])
                        sy[j % 2].inc(i, 16)
                    syn.wait_ge(sy[0].h, sy[0].n)
                    syn.wait_ge(sy[1].h, sy[1].n)
                blk.sync(f_sy)

        def emit_ag(cin, cout, ctr, waits, groups=None):
            with nc.Block() as blk:
                def f_gp(g):
                    for h, v in waits:
                        g.wait_ge(h, v)
                    i = g.collective_compute(
                        "AllGather", ALU.bypass,
                        replica_groups=groups or [list(range(8))],
                        ins=[cin.ap().opt()], outs=[cout.ap().opt()])
                    ctr.inc(i)
                blk.gpsimd(f_gp)

        def expansion_tiles():
            tiles = []
            for sec, src_t in ((0, gsl), (1, usl)):
                dstv = cag2_in.ap()[sec * SEC:sec * SEC + RG * INTER]
                dstv = dstv.rearrange("(p i) -> p i", i=INTER)
                r0 = 0
                while r0 < RG:
                    npr = min(128, RG - r0)
                    c0 = 0
                    while c0 < INTER:
                        w = min(FDMAX, INTER - c0)
                        src = src_t.ap()[r0:r0 + npr, c0:c0 + w]
                        dst = dstv[r0:r0 + npr, c0:c0 + w]
                        tiles.append((src, npr, w, dst))
                        c0 += w
                    r0 += npr
            amax = max(1, FDMAX // RDC)
            a0 = 0
            while a0 < IC:
                na = min(amax, IC - a0)
                src = dsl.ap().rearrange("(a p) c -> p a c", p=128)
                src = src[:, a0:a0 + na, :]
                dst = cag2_in.ap()[2 * SEC + a0 * 128 * RDC:
                                   2 * SEC + (a0 + na) * 128 * RDC]
                dst = dst.rearrange("(a p c) -> p a c", p=128, c=RDC)
                tiles.append((src, 128, na * RDC, dst))
                a0 += na
            return tiles

        def expansion_order(nt):
            order = []
            for t in range(nt):
                order.append(("load", t))
                if t >= 1:
                    order.append(("ship", t - 1))
            order.append(("ship", nt - 1))
            return order

        def expansion_gp_ops(g, tiles, eibf, ewb, ex0):
            for op, t in expansion_order(len(tiles)):
                src, npr, fd, dst = tiles[t]
                if op == "load":
                    if t >= 2:
                        g.wait_ge(sEX.h, ex0 + t - 1)
                    dstv = eibf[0:npr, t % 2, 0:fd]
                    if src.ndim == 3:
                        dstv = dstv.rearrange("p (a c) -> p a c",
                                              c=src.shape[2])
                    i = g.dma_start(out=dstv, in_=src)
                    exL[t % 2].inc(i, 16)
                else:
                    g.wait_ge(sEX.h, ex0 + t + 1)
                    srcv = ewb[0:npr, t % 2, 0:fd]
                    if dst.ndim == 3:
                        srcv = srcv.rearrange("p (a c) -> p a c",
                                              c=dst.shape[2])
                    i = g.dma_start(out=dst, in_=srcv)
                    exS[t % 2].inc(i, 16)

        def expansion_ve_tile(ve, tiles, t, eibf, ewb, etb, l0, s0):
            src, npr, fd, dst = tiles[t]
            ve.wait_ge(exL[t % 2].h, l0[t % 2] + 16 * (t // 2 + 1))
            if t >= 2:
                ve.wait_ge(exS[t % 2].h, s0[t % 2] + 16 * (t // 2 - 1 + 1))
            ib = eibf[0:npr, t % 2, 0:fd]
            wb = ewb[0:npr, t % 2, 0:fd]
            tb = etb[0:npr, 0:fd]
            ve.tensor_scalar(wb, ib, 0.0, luts[0:npr, 0:1],
                             ALU.is_equal, ALU.mult)
            i = None
            for l in range(1, 16):
                ve.tensor_scalar(tb, ib, float(l), luts[0:npr, l:l + 1],
                                 ALU.is_equal, ALU.mult)
                ve.drain()
                i = ve.tensor_tensor(wb, wb, tb, ALU.add)
                ve.drain()
            sEX.inc(i)

        def emit_expansion():
            tiles = expansion_tiles()
            nt = len(tiles)
            with (
                nc.sbuf_tensor(uid("ex_i"), [128, 2, FDMAX], BF16) as eibf,
                nc.sbuf_tensor(uid("ex_w"), [128, 2, FDMAX], BF16) as ewb,
                nc.sbuf_tensor(uid("ex_t"), [128, FDMAX], BF16) as etb,
                nc.Block() as blk,
            ):
                ex0 = sEX.n
                l0 = [exL[0].n, exL[1].n]
                s0 = [exS[0].n, exS[1].n]

                def f_gp(g):
                    expansion_gp_ops(g, tiles, eibf, ewb, ex0)
                    for c_ in (exL[0], exL[1], exS[0], exS[1]):
                        g.wait_ge(c_.h, c_.n)
                blk.gpsimd(f_gp)

                def f_ve(ve):
                    ve.wait_ge(sW.h, p0_w)
                    for t in range(nt):
                        expansion_ve_tile(ve, tiles, t, eibf, ewb, etb, l0, s0)
                blk.vector(f_ve)
            return [(exS[0].h, exS[0].n), (exS[1].h, exS[1].n)]

        def emit_attention(kind, expansion=False, qb=None):
            masked = kind == "b"
            qbuf = qb
            Hq = H
            cag = cag3_out if masked else cag1_out
            Lr = L3 if masked else L1
            Dv = KVD if masked else D
            voff = VOFF3 if masked else VOFF1
            cc = sC3 if masked else sC1
            groups = NKV if masked else H
            hpg = G if masked else 1
            nbuf = 1 if masked else 2
            with (
                nc.sbuf_tensor(uid("at_kh"), [128, nbuf, S], BF16) as kh,
                nc.sbuf_tensor(uid("at_vh"), [128, nbuf, KC, 128], BF16) as vh,
                nc.sbuf_tensor(uid("at_ot"), [128, TOK], BF16) as otmp,
                nc.psum_tensor(uid("at_st"), [128, 2, TOK], F32) as stp,
                nc.psum_tensor(uid("at_pv"), [128, TOK], F32) as pvp,
                nc.psum_tensor(uid("at_dn"), [1, TOK], F32) as dnp,
                nc.sbuf_tensor(uid("at_pt"), [128, KC, TOK], BF16) as ptb,
                nc.sbuf_tensor(uid("at_rv"), [1, TOK], F32) as rvec,
                ExitStack() as mst,
            ):
                msk = None
                if masked:
                    msk = mst.enter_context(
                        nc.sbuf_tensor(uid("at_mk"), [128, KC, TOK], BF16))
                tiles = None
                if expansion:
                    tiles = expansion_tiles()
                    eibf = mst.enter_context(
                        nc.sbuf_tensor(uid("ex_i"), [128, 2, FDMAX], BF16))
                    ewb = mst.enter_context(
                        nc.sbuf_tensor(uid("ex_w"), [128, 2, FDMAX], BF16))
                    etb = mst.enter_context(
                        nc.sbuf_tensor(uid("ex_t"), [128, FDMAX], BF16))
                    ex0 = sEX.n
                    exl0 = [exL[0].n, exL[1].n]
                    exs0 = [exS[0].n, exS[1].n]
                with nc.Block() as blk:
                    ve0, sc0, te0 = sVE.n, sSC.n, sTE.n
                    s0 = [sy[0].n, sy[1].n]
                    scph = KC + 1
                    veph = (KC if masked else 0) + 2
                    teph = KC + 2
                    exp_m = lambda h, kc: sc0 + h * scph + kc + 1
                    copy_m = lambda h: sc0 + h * scph + scph
                    msk_m = lambda h, kc: ve0 + h * veph + kc + 1
                    recip_m = lambda h: ve0 + h * veph + veph - 1
                    evac_m = lambda h: ve0 + h * veph + veph
                    st_m = lambda h, kc: te0 + h * teph + kc + 1
                    pv_m = lambda h: te0 + h * teph + KC + 1
                    bc_m = lambda h: te0 + h * teph + KC + 2
                    mask_w = sMK.n + (16 if masked else 0)
                    kv_m = lambda gi: (sy[gi % 2].h,
                                       s0[gi % 2] + 64 * (gi // 2 + 1))

                    def f_sy(syn):
                        if masked:
                            i = syn.dma_start(out=msk[:], in_=maskb.ap(
                            ).rearrange("(a p) t -> p a t", p=128))
                            sMK.inc(i, 16)
                        syn.wait_ge(cc.h, cc.n)
                        for gi in range(groups):
                            if gi >= nbuf:
                                syn.wait_ge(sTE.h,
                                            pv_m((gi - nbuf) * hpg + hpg - 1))
                            gb = gi % nbuf
                            for half in range(2):
                                src = cag.ap()[half,
                                               gi * 128 * TOK:
                                               (gi + 1) * 128 * TOK]
                                src = src.rearrange("(p t) -> p t", p=128)
                                i = syn.dma_start(
                                    out=kh[:, gb, half * TOK:(half + 1) * TOK],
                                    in_=src)
                                sy[gi % 2].inc(i, 16)
                            for half in range(2):
                                src = cag.ap()[half, voff:voff + TOK * Dv]
                                src = src.rearrange(
                                    "(a p d) -> p a d", p=128, d=Dv)
                                src = src[:, :, gi * 128:(gi + 1) * 128]
                                i = syn.dma_start(
                                    out=vh[:, gb, half * KCh:(half + 1) * KCh],
                                    in_=src)
                                sy[gi % 2].inc(i, 16)
                        syn.wait_ge(sy[0].h, sy[0].n)
                        syn.wait_ge(sy[1].h, sy[1].n)
                    blk.sync(f_sy)

                    if expansion:
                        def f_gp(g):
                            expansion_gp_ops(g, tiles, eibf, ewb, ex0)
                            for c_ in (exL[0], exL[1], exS[0], exS[1]):
                                g.wait_ge(c_.h, c_.n)
                            i = g.collective_compute(
                                "AllGather", ALU.bypass,
                                replica_groups=[list(range(8))],
                                ins=[cag2_in.ap().opt()],
                                outs=[cag2_out.ap().opt()])
                            sC2.inc(i)
                        blk.gpsimd(f_gp)

                    def f_te(te):
                        for h in range(Hq):
                            gi = h // hpg
                            gb = gi % nbuf
                            te.wait_ge(*kv_m(gi))
                            if h >= 1:
                                te.wait_ge(sSC.h, copy_m(h - 1))
                                te.wait_ge(sVE.h, evac_m(h - 1))
                            for kc in range(KC):
                                if kc >= 2:
                                    te.wait_ge(sSC.h, exp_m(h, kc - 2))
                                last = None
                                for n in range(TN):
                                    last = te.matmul(
                                        stp[:, kc % 2, nsl(n)],
                                        kh[:, gb, kc * 128:(kc + 1) * 128],
                                        qbuf[:, h, nsl(n)],
                                        start=True, stop=True)
                                sTE.inc(last)
                            last = None
                            for kc in range(KC):
                                te.wait_ge(sSC.h, exp_m(h, kc))
                                if masked:
                                    te.wait_ge(sVE.h, msk_m(h, kc))
                                for n in range(TN):
                                    te.matmul(pvp[:, nsl(n)],
                                              vh[:, gb, kc],
                                              ptb[:, kc, nsl(n)],
                                              start=(kc == 0),
                                              stop=(kc == KC - 1))
                                for n in range(TN):
                                    last = te.matmul(dnp[:, nsl(n)],
                                                     ones_col_bf[:],
                                                     ptb[:, kc, nsl(n)],
                                                     start=(kc == 0),
                                                     stop=(kc == KC - 1))
                            sTE.inc(last)
                            te.wait_ge(sVE.h, recip_m(h))
                            last = None
                            for n in range(TN):
                                last = te.matmul(stp[:, 0, nsl(n)],
                                                 ones_row_f[:],
                                                 rvec[:, nsl(n)],
                                                 start=True, stop=True)
                            sTE.inc(last)
                    blk.tensor(f_te)

                    def f_sc(se):
                        for h in range(Hq):
                            for kc in range(KC):
                                se.wait_ge(sTE.h, st_m(h, kc))
                                i = se.activation(ptb[:, kc], stp[:, kc % 2],
                                                  AF.Exp, bias=negshift[:],
                                                  scale=SCALE)
                                sSC.inc(i)
                            se.wait_ge(sTE.h, pv_m(h))
                            if h >= 1:
                                se.wait_ge(sVE.h, evac_m(h - 1))
                            i = se.activation(otmp[:], pvp[:], AF.Copy)
                            sSC.inc(i)
                    blk.scalar(f_sc)

                    tptr = [0]

                    def f_ve(ve):
                        if masked:
                            ve.wait_ge(sMK.h, mask_w)
                        if expansion:
                            ve.wait_ge(sW.h, p0_w)
                        for h in range(Hq):
                            if masked:
                                for kc in range(KC):
                                    ve.wait_ge(sSC.h, exp_m(h, kc))
                                    i = ve.tensor_mul(ptb[:, kc], ptb[:, kc],
                                                      msk[:, kc])
                                    sVE.inc(i)
                            ve.wait_ge(sTE.h, pv_m(h))
                            i = ve.reciprocal(rvec[:], dnp[:])
                            sVE.inc(i)
                            ve.wait_ge(sTE.h, bc_m(h))
                            ve.wait_ge(sSC.h, copy_m(h))
                            i = ve.tensor_mul(qbuf[:, h], otmp[:], stp[:, 0])
                            sVE.inc(i)
                            if expansion:
                                while tptr[0] < min(len(tiles), 2 * (h + 1)):
                                    expansion_ve_tile(ve, tiles, tptr[0],
                                                      eibf, ewb, etb,
                                                      exl0, exs0)
                                    tptr[0] += 1
                        if expansion:
                            for t in range(tptr[0], len(tiles)):
                                expansion_ve_tile(ve, tiles, t, eibf, ewb,
                                                  etb, exl0, exs0)
                    blk.vector(f_ve)

        def emit_outproj(wt, qb=None):
            qbuf = qb
            with (
                nc.sbuf_tensor(uid("op_w"), [128, 2, DC, 128], BF16) as wti,
                nc.psum_tensor(uid("op_ps"), [128, 2, TOK], F32) as ops,
                nc.Block() as blk,
            ):
                ve0, te0 = sVE.n, sTE.n
                g0 = [gp[0].n, gp[1].n]
                tm = lambda j: te0 + j + 1
                em = lambda j: ve0 + j + 1
                gm = lambda j: g0[j % 2] + 16 * (j // 2 + 1)

                def f_gp(g):
                    for j in range(DC):
                        if j >= 2:
                            g.wait_ge(sTE.h, tm(j - 2))
                        i = g.dma_start(out=wti[:, j % 2],
                                        in_=colslice(wt, j * 128, 128))
                        gp[j % 2].inc(i, 16)
                    g.wait_ge(gp[0].h, gp[0].n)
                    g.wait_ge(gp[1].h, gp[1].n)
                blk.gpsimd(f_gp)

                def f_te(te):
                    for j in range(DC):
                        te.wait_ge(gp[j % 2].h, gm(j))
                        if j >= 2:
                            te.wait_ge(sVE.h, em(j - 2))
                        last = None
                        for oc in range(H):
                            for n in range(TN):
                                last = te.matmul(
                                    ops[:, j % 2, nsl(n)], wti[:, j % 2, oc],
                                    qbuf[:, oc, nsl(n)],
                                    start=(oc == 0), stop=(oc == H - 1))
                        sTE.inc(last)
                blk.tensor(f_te)

                def f_ve(ve):
                    for j in range(DC):
                        ve.wait_ge(sTE.h, tm(j))
                        i = ve.tensor_add(xsb[:, j], xsb[:, j],
                                          ops[:, j % 2])
                        sVE.inc(i)
                blk.vector(f_ve)

        def emit_ffn(wait_c2, store_y=False):
            fsl = lambda n: slice(n * FT, (n + 1) * FT)
            with nc.sbuf_tensor(uid("ff_h"), [128, IC, TOK], BF16) as hbuf:
                jobs = [(oc, th) for oc in range(IC) for th in range(FTN)]
                with (
                    nc.sbuf_tensor(uid("ff_wg"), [128, 2, DC, 128], BF16) as wg,
                    nc.sbuf_tensor(uid("ff_wu"), [128, 2, DC, 128], BF16) as wu,
                    nc.sbuf_tensor(uid("ff_st"), [128, 4, FT], BF16) as stmp,
                    nc.psum_tensor(uid("ff_g"), [128, 4, FT], F32) as gps,
                    nc.psum_tensor(uid("ff_u"), [128, 4, FT], F32) as ups,
                    nc.Block() as blk,
                ):
                    ve0, sc0, te0 = sVE.n, sSC.n, sTE.n
                    s0 = [sy[0].n, sy[1].n]
                    g0 = [gp[0].n, gp[1].n]
                    tm = lambda j: te0 + j + 1
                    sim = lambda j: sc0 + j + 1
                    hm = lambda j: ve0 + j + 1
                    ndg = 8
                    wms = lambda oc: (sy[oc % 2].h,
                                      s0[oc % 2] + 16 * ndg * (oc // 2 + 1))
                    wmg = lambda oc: (gp[oc % 2].h,
                                      g0[oc % 2] + 16 * ndg * (oc // 2 + 1))

                    def gu_loads(q, ctrs, sec, dstb):
                        if wait_c2:
                            q.wait_ge(sC2.h, sC2.n)
                        for oc in range(IC):
                            if oc >= 2:
                                q.wait_ge(sTE.h, tm((oc - 2) * FTN + FTN - 1))
                            if RG % 128 == 0:
                                rgc = RG // 128
                                for r in range(8):
                                    src = cag2_out.ap()[
                                        r, sec * SEC:sec * SEC + RG * INTER]
                                    src = src.rearrange("(a p i) -> p a i",
                                                        p=128, i=INTER)
                                    src = src[:, :, oc * 128:(oc + 1) * 128]
                                    dst = dstb[:, oc % 2,
                                               r * rgc:(r + 1) * rgc]
                                    i = q.dma_start(out=dst, in_=src)
                                    ctrs[oc % 2].inc(i, 16)
                            else:
                                for r in range(8):
                                    src = cag2_out.ap()[
                                        r, sec * SEC:sec * SEC + RG * INTER]
                                    src = src.rearrange("(a i) -> a i",
                                                        i=INTER)
                                    src = src[:, oc * 128:(oc + 1) * 128]
                                    row0 = r * RG
                                    dst = dstb[row0 % 128:row0 % 128 + RG,
                                               oc % 2, row0 // 128]
                                    i = q.dma_start(out=dst, in_=src)
                                    ctrs[oc % 2].inc(i, 16)
                        q.wait_ge(ctrs[0].h, ctrs[0].n)
                        q.wait_ge(ctrs[1].h, ctrs[1].n)

                    blk.sync(lambda syn: gu_loads(syn, sy, 0, wg))
                    blk.gpsimd(lambda g: gu_loads(g, gp, 1, wu))

                    def f_te(te):
                        for j, (oc, th) in enumerate(jobs):
                            if th == 0:
                                te.wait_ge(*wms(oc))
                                te.wait_ge(*wmg(oc))
                            if j >= 4:
                                te.wait_ge(sSC.h, sim(j - 4))
                                te.wait_ge(sVE.h, hm(j - 4))
                            for dc in range(DC):
                                te.matmul(gps[:, j % 4], wg[:, oc % 2, dc],
                                          xn[:, dc, fsl(th)],
                                          start=(dc == 0), stop=(dc == DC - 1))
                            last = None
                            for dc in range(DC):
                                last = te.matmul(
                                    ups[:, j % 4], wu[:, oc % 2, dc],
                                    xn[:, dc, fsl(th)],
                                    start=(dc == 0), stop=(dc == DC - 1))
                            sTE.inc(last)
                    blk.tensor(f_te)

                    def f_sc(se):
                        for j, (oc, th) in enumerate(jobs):
                            se.wait_ge(sTE.h, tm(j))
                            i = se.activation(stmp[:, j % 4],
                                              gps[:, j % 4], AF.Silu)
                            sSC.inc(i)
                    blk.scalar(f_sc)

                    def f_ve(ve):
                        for j, (oc, th) in enumerate(jobs):
                            ve.wait_ge(sSC.h, sim(j))
                            i = ve.tensor_mul(hbuf[:, oc, fsl(th)],
                                              stmp[:, j % 4],
                                              ups[:, j % 4])
                            sVE.inc(i)
                    blk.vector(f_ve)

                with (
                    nc.sbuf_tensor(uid("ff_wd"), [128, 2, IC, 128], BF16) as wd,
                    nc.psum_tensor(uid("ff_d"), [128, 4, FT], F32) as dps,
                    nc.Block() as blk,
                ):
                    djobs = [(dj, th) for dj in range(DC) for th in range(FTN)]
                    ve0, te0 = sVE.n, sTE.n
                    s0 = [sy[0].n, sy[1].n]
                    tm = lambda j: te0 + j + 1
                    em = lambda j: ve0 + j + 1

                    def down_dmas(dj, dstb):
                        out = []
                        col0 = dj * 128
                        done = 0
                        while done < 128:
                            c = col0 + done
                            r = c // RDC
                            lc = c % RDC
                            w = min(RDC - lc, 128 - done)
                            src = cag2_out.ap()[r, 2 * SEC:3 * SEC]
                            src = src.rearrange("(a p c) -> p a c",
                                                p=128, c=RDC)
                            src = src[:, :, lc:lc + w]
                            dst = dstb[:, :, done:done + w]
                            out.append((dst, src))
                            done += w
                        return out

                    def f_sy(syn):
                        for dj in range(DC):
                            if dj >= 2:
                                syn.wait_ge(sTE.h,
                                            tm((dj - 2) * FTN + FTN - 1))
                            for dst, src in down_dmas(dj, wd[:, dj % 2]):
                                i = syn.dma_start(out=dst, in_=src)
                                sy[dj % 2].inc(i, 16)
                        syn.wait_ge(sy[0].h, sy[0].n)
                        syn.wait_ge(sy[1].h, sy[1].n)
                    blk.sync(f_sy)

                    wm_pred = []
                    acc = [s0[0], s0[1]]
                    for dj in range(DC):
                        acc[dj % 2] += 16 * len(down_dmas(dj, wd[:, 0]))
                        wm_pred.append((sy[dj % 2].h, acc[dj % 2]))

                    def f_te(te):
                        for j, (dj, th) in enumerate(djobs):
                            if th == 0:
                                te.wait_ge(*wm_pred[dj])
                            if j >= 4:
                                te.wait_ge(sVE.h, em(j - 4))
                            last = None
                            for oc in range(IC):
                                last = te.matmul(
                                    dps[:, j % 4], wd[:, dj % 2, oc],
                                    hbuf[:, oc, fsl(th)],
                                    start=(oc == 0), stop=(oc == IC - 1))
                            sTE.inc(last)
                    blk.tensor(f_te)

                    def f_ve(ve):
                        for j, (dj, th) in enumerate(djobs):
                            ve.wait_ge(sTE.h, tm(j))
                            i = ve.tensor_add(xsb[:, dj, fsl(th)],
                                              xsb[:, dj, fsl(th)],
                                              dps[:, j % 4])
                            sVE.inc(i)
                    blk.vector(f_ve)

                    if store_y:
                        def f_sc(se):
                            for dj in range(DC):
                                se.wait_ge(sVE.h,
                                           em(dj * FTN + FTN - 1))
                                dst = y.ap().rearrange(
                                    "(c p) t -> p c t", p=128)[:, dj]
                                i = se.dma_start(out=dst, in_=xsb[:, dj])
                                sW.inc(i, 16)
                        blk.scalar(f_sc)

        first_ln = [True]

        def ln():
            emit_ln(first=first_ln[0])
            first_ln[0] = False

        need_ffn = ("f1" in stages) or ("f2" in stages)
        if "a" in stages:
            ln()
            with nc.sbuf_tensor("qbuf_a", [128, H, TOK], BF16) as qa:
                jobs_k = [(H + h, h) for h in range(H)]
                emit_proj(wqkvT, jobs_k, "ropek", cag1_in, 0)
                emit_vproj(lambda c0, w: colslice(wqkvT, 2 * H * 128 + c0, w),
                           D, VOFF1, cag1_in, D)
                emit_ag(cag1_in, cag1_out, sC1,
                        [(sy[0].h, sy[0].n), (sy[1].h, sy[1].n)],
                        groups=PAIRS)
                jobs_qk = [(h, h) for h in range(H)]
                emit_proj(wqkvT, jobs_qk, "ropeq", qb=qa)
                emit_attention("a", expansion=need_ffn, qb=qa)
                emit_outproj(woT, qb=qa)
        elif need_ffn:
            exp_waits = emit_expansion()
            emit_ag(cag2_in, cag2_out, sC2, exp_waits)
        if "f1" in stages:
            ln()
            emit_ffn(wait_c2=True)
        if "b" in stages:
            ln()
            with nc.sbuf_tensor("qbuf_b", [128, H, TOK], BF16) as qb_:
                jobs_kb = [(g, g) for g in range(NKV)]
                emit_proj(wkb, jobs_kb, "copyk", cag3_in, 0)
                emit_vproj(lambda c0, w: wvb.ap().rearrange(
                    "(c p) o -> p c o", p=128)[:, :, c0:c0 + w],
                    KVD, VOFF3, cag3_in, KVD)
                emit_ag(cag3_in, cag3_out, sC3,
                        [(sy[0].h, sy[0].n), (sy[1].h, sy[1].n)],
                        groups=PAIRS)
                jobs_qb = [(h, h) for h in range(H)]
                emit_proj(wqb, jobs_qb, "copyq", qb=qb_)
                emit_attention("b", qb=qb_)
                emit_outproj(wob, qb=qb_)
        if "f2" in stages:
            ln()
            emit_ffn(wait_c2=("f1" not in stages),
                     store_y=True)

        with nc.Block() as blk:
            def f_sy(sy):
                if "f2" not in stages:
                    i = sy.dma_start(out=y.ap().rearrange(
                        "(c p) t -> p c t", p=128), in_=xsb[:])
                    sW.inc(i, 16)
                sy.wait_ge(sW.h, sW.n)
            blk.sync(f_sy)

    if verbose:
        print(f"instructions: {nc.n_instructions()}")
    return nc



def _rope_tables(S):
    inv = 1.0 / (10000.0 ** (np.arange(0, 128, 2, dtype=np.float32) / 128.0))
    t = np.arange(S, dtype=np.float32)
    fr = np.einsum("i,j->ij", t, inv)
    emb = np.concatenate([fr, fr], axis=-1)
    return np.cos(emb).astype(np.float32), np.sin(emb).astype(np.float32)


def make_in_maps(cfg, x, w_qkv, w_out, Wq, Wk, Wv, Wo, lut, gate_idx, up_idx,
                 down_idx):
    D, TOK, S = cfg.D, cfg.TOK, cfg.S
    f32 = lambda a: np.ascontiguousarray(np.asarray(a, dtype=np.float32))
    i32 = lambda a: np.ascontiguousarray(np.asarray(a, dtype=np.int32))
    wqkvT = f32(np.asarray(w_qkv).T)
    woT = f32(np.asarray(w_out).T)
    wq, wk, wv, wo = f32(Wq), f32(Wk), f32(Wv), f32(Wo)
    gT = i32(np.asarray(gate_idx).T)
    uT = i32(np.asarray(up_idx).T)
    dT = i32(np.asarray(down_idx).T)
    lutbc = f32(np.tile(np.asarray(lut, dtype=np.float32)[None, :], (128, 1)))
    cos, sin = _rope_tables(S)
    in_maps = []
    for c in range(8):
        b, hh = c // 2, c % 2
        tok = slice(hh * TOK, (hh + 1) * TOK)
        kk = np.arange(S, dtype=np.int64)[:, None]
        qq = hh * TOK + np.arange(TOK, dtype=np.int64)[None, :]
        mask = (kk <= qq).astype(ml_dtypes.bfloat16)
        in_maps.append({
            "xin": f32(np.asarray(x)[b, tok, :].T),
            "wqkvT": wqkvT, "woT": woT,
            "wqb": wq, "wkb": wk, "wvb": wv, "wob": wo,
            "cosq": f32(cos[tok, :].T), "sinq": f32(sin[tok, :].T),
            "maskb": np.ascontiguousarray(mask),
            "lutin": lutbc,
            "gsl": np.ascontiguousarray(gT[c * cfg.RG:(c + 1) * cfg.RG, :]),
            "usl": np.ascontiguousarray(uT[c * cfg.RG:(c + 1) * cfg.RG, :]),
            "dsl": np.ascontiguousarray(dT[:, c * cfg.RDC:(c + 1) * cfg.RDC]),
        })
    return in_maps


def assemble(cfg, results):
    B, S, D, TOK = cfg.B, cfg.S, cfg.D, cfg.TOK
    out = np.empty((B, S, D), dtype=np.float32)
    for c in range(8):
        b, hh = c // 2, c % 2
        out[b, hh * TOK:(hh + 1) * TOK, :] = results[c]["y"].T
    return out


def kernel(L0_attn_a_x, L0_attn_a_w_qkv, L0_attn_a_w_out, L0_attn_b_Wq,
           L0_attn_b_Wk, L0_attn_b_Wv, L0_attn_b_Wo, L0_ffn_lut,
           L0_ffn_gate_idx, L0_ffn_up_idx, L0_ffn_down_idx):
    cfg = FULL
    nc = build(cfg)
    in_maps = make_in_maps(cfg, L0_attn_a_x, L0_attn_a_w_qkv, L0_attn_a_w_out,
                           L0_attn_b_Wq, L0_attn_b_Wk, L0_attn_b_Wv,
                           L0_attn_b_Wo, L0_ffn_lut, L0_ffn_gate_idx,
                           L0_ffn_up_idx, L0_ffn_down_idx)
    res = run_bass_kernel_spmd(nc, in_maps, core_ids=list(range(8)))
    return assemble(cfg, res.results)


# version 07 (final)
# speedup vs baseline: 1.3868x; 1.0220x over previous
import math
from contextlib import ExitStack

import numpy as np
import ml_dtypes
import concourse.bass as bass
import concourse.mybir as mybir
from concourse.bass_utils import run_bass_kernel_spmd

F32 = mybir.dt.float32
BF16 = mybir.dt.bfloat16
I32 = mybir.dt.int32
AF = mybir.ActivationFunctionType
ALU = mybir.AluOpType

SCALE = 1.0 / math.sqrt(128.0)
SHIFT = 12.0
EPS = 1e-5
FDMAX = 2304
PAIRS = [[0, 1], [2, 3], [4, 5], [6, 7]]


class Cfg:
    def __init__(self, H, NKV, S, IC, B=4):
        self.H = H
        self.NKV = NKV
        self.G = H // NKV
        self.D = H * 128
        self.DC = self.D // 128
        self.S = S
        self.B = B
        self.TOK = B * S // 8
        assert self.TOK == S // 2
        self.KC = S // 128
        self.KCh = self.TOK // 128
        self.IC = IC
        self.INTER = IC * 128
        self.KVD = NKV * 128
        self.NT = min(512, self.TOK)
        self.TN = self.TOK // self.NT
        self.FT = min(512, self.TOK)
        self.FTN = self.TOK // self.FT
        self.RG = self.D // 8
        self.RDC = self.D // 8
        self.SEC = self.D * self.INTER // 8


FULL = Cfg(H=16, NKV=4, S=2048, IC=43)
SMALL = Cfg(H=2, NKV=1, S=512, IC=2)


def build(cfg, verbose=False, stages=('a', 'f1', 'b', 'f2')):
    H, D, DC, TOK, KC, KCh = cfg.H, cfg.D, cfg.DC, cfg.TOK, cfg.KC, cfg.KCh
    IC, KVD, NT, TN = cfg.IC, cfg.KVD, cfg.NT, cfg.TN
    S, G, NKV, INTER, RG, RDC, SEC = (cfg.S, cfg.G, cfg.NKV, cfg.INTER,
                                      cfg.RG, cfg.RDC, cfg.SEC)
    FT, FTN = cfg.FT, cfg.FTN

    nc = bass.Bass("TRN2", target_bir_lowering=False)

    xin = nc.dram_tensor("xin", [D, TOK], F32, kind="ExternalInput")
    wqkvT = nc.dram_tensor("wqkvT", [D, 3 * D], F32, kind="ExternalInput")
    woT = nc.dram_tensor("woT", [D, D], F32, kind="ExternalInput")
    wqb = nc.dram_tensor("wqb", [D, D], F32, kind="ExternalInput")
    wkb = nc.dram_tensor("wkb", [D, KVD], F32, kind="ExternalInput")
    wvb = nc.dram_tensor("wvb", [D, KVD], F32, kind="ExternalInput")
    wob = nc.dram_tensor("wob", [D, D], F32, kind="ExternalInput")
    cosq = nc.dram_tensor("cosq", [128, TOK], F32, kind="ExternalInput")
    sinq = nc.dram_tensor("sinq", [128, TOK], F32, kind="ExternalInput")
    maskb = nc.dram_tensor("maskb", [S, TOK], BF16, kind="ExternalInput")
    lutin = nc.dram_tensor("lutin", [128, 16], F32, kind="ExternalInput")
    gsl = nc.dram_tensor("gsl", [RG, INTER], I32, kind="ExternalInput")
    usl = nc.dram_tensor("usl", [RG, INTER], I32, kind="ExternalInput")
    dsl = nc.dram_tensor("dsl", [INTER, RDC], I32, kind="ExternalInput")
    y = nc.dram_tensor("y", [D, TOK], F32, kind="ExternalOutput")

    L1 = 2 * D * TOK
    cag1_in = nc.dram_tensor("cag1_in", [L1], BF16)
    cag1_out = nc.dram_tensor("cag1_out", [2, L1], BF16)
    L2 = 3 * SEC
    cag2_in = nc.dram_tensor("cag2_in", [L2], BF16)
    cag2_out = nc.dram_tensor("cag2_out", [8, L2], BF16, addr_space="Shared")
    L3 = 2 * KVD * TOK
    cag3_in = nc.dram_tensor("cag3_in", [L3], BF16)
    cag3_out = nc.dram_tensor("cag3_out", [2, L3], BF16)
    VOFF1 = D * TOK
    VOFF3 = KVD * TOK

    class Ctr:
        def __init__(self, name):
            self.h = nc.alloc_semaphore(name)
            self.n = 0

        def inc(self, instr, k=1):
            instr.then_inc(self.h, k)
            self.n += k
            return self.n

    sW = Ctr("sW")
    gp = [Ctr("gp0"), Ctr("gp1")]
    sy = [Ctr("sy0"), Ctr("sy1")]
    exL = [Ctr("exL0"), Ctr("exL1")]
    exS = [Ctr("exS0"), Ctr("exS1")]
    sEX = Ctr("sEX")
    sMK = Ctr("sMK")
    sC1 = Ctr("sC1")
    sC2 = Ctr("sC2")
    sC3 = Ctr("sC3")
    sTE = Ctr("sTE")
    sSC = Ctr("sSC")
    sVE = Ctr("sVE")

    _uid = [0]

    def uid(s):
        _uid[0] += 1
        return f"{s}{_uid[0]}"

    def nsl(n):
        return slice(n * NT, (n + 1) * NT)

    def colslice(t, j0, w):
        return t.ap().rearrange("(c p) o -> p c o", p=128)[:, :, j0:j0 + w]

    st = ExitStack()
    with st:
        sb = lambda name, shape, dt: st.enter_context(
            nc.sbuf_tensor(name, shape, dt))
        xsb = sb("xsb", [128, DC, TOK], F32)
        xn = sb("xnb", [128, DC, TOK], BF16)
        luts = sb("luts", [128, 16], F32)
        ones_col_bf = sb("ones_col_bf", [128, 1], BF16)
        ones_row_f = sb("ones_row_f", [1, 128], F32)
        ones_row_bf = sb("ones_row_bf", [1, 128], BF16)
        negshift = sb("negshift", [128, 1], F32)

        with nc.Block() as blk:
            def f_sy(sy):
                sy.dma_start(out=xsb[:], in_=xin.ap().rearrange(
                    "(c p) t -> p c t", p=128)).then_inc(sW.h, 16)
                sW.n += 16
                sy.dma_start(out=luts[:], in_=lutin.ap()).then_inc(sW.h, 16)
                sW.n += 16
            blk.sync(f_sy)

            def f_ve(ve):
                ve.memset(ones_col_bf[:], 1.0)
                ve.memset(ones_row_f[:], 1.0)
                ve.memset(ones_row_bf[:], 1.0)
                ve.memset(negshift[:], -SHIFT)
            blk.vector(f_ve)
        p0_w = sW.n

        def emit_ln(first=False):
            with (
                nc.sbuf_tensor(uid("ln_mu"), [1, TOK], F32) as mu,
                nc.sbuf_tensor(uid("ln_es"), [1, TOK], F32) as es,
                nc.sbuf_tensor(uid("ln_rs"), [1, TOK], F32) as rsd,
                nc.sbuf_tensor(uid("ln_mr"), [1, TOK], F32) as mr,
                nc.sbuf_tensor(uid("ln_sq"), [128, 2, TOK], BF16) as sqb,
                nc.sbuf_tensor(uid("ln_nrm"), [128, 2, TOK], F32) as nrm,
                nc.psum_tensor(uid("ln_sum"), [1, TOK], F32) as sumps,
                nc.psum_tensor(uid("ln_sqp"), [1, TOK], F32) as sqps,
                nc.psum_tensor(uid("ln_br"), [128, TOK], F32) as brps,
                nc.psum_tensor(uid("ln_bm"), [128, TOK], F32) as bmps,
                nc.Block() as blk,
            ):
                ve0, sc0, te0 = sVE.n, sSC.n, sTE.n
                copy_m = lambda c: ve0 + c + 1
                sq_m = lambda c: sc0 + c + 1
                sum_m = lambda c: te0 + c + 1
                stats1_m = ve0 + DC + 1
                sqrt_m = sc0 + DC + 1
                stats2_m = ve0 + DC + 2
                bcast_m = te0 + DC + 1

                def f_ve(ve):
                    if first:
                        ve.wait_ge(sW.h, p0_w)
                    for c in range(DC):
                        i = ve.tensor_copy(xn[:, c], xsb[:, c])
                        sVE.inc(i)
                    ve.wait_ge(sTE.h, sum_m(DC - 1))
                    ve.tensor_scalar(mu[:], sumps[:], 1.0 / D, None, ALU.mult)
                    ve.tensor_scalar(es[:], sqps[:], 1.0 / D, None, ALU.mult)
                    ve.drain()
                    ve.tensor_mul(rsd[:], mu[:], mu[:])
                    ve.drain()
                    ve.tensor_sub(es[:], es[:], rsd[:])
                    ve.drain()
                    i = ve.tensor_scalar(es[:], es[:], EPS, None, ALU.add)
                    sVE.inc(i)
                    ve.wait_ge(sSC.h, sqrt_m)
                    ve.reciprocal(rsd[:], es[:])
                    ve.drain()
                    i = ve.tensor_mul(mr[:], mu[:], rsd[:])
                    sVE.inc(i)
                    for c in range(DC):
                        ve.wait_ge(sTE.h, bcast_m)
                        ve.tensor_mul(nrm[:, c % 2], xsb[:, c], brps[:])
                        ve.drain()
                        i = ve.tensor_sub(xn[:, c], nrm[:, c % 2], bmps[:])
                        sVE.inc(i)
                blk.vector(f_ve)

                def f_sc(se):
                    for c in range(DC):
                        se.wait_ge(sVE.h, copy_m(c))
                        if c >= 2:
                            se.wait_ge(sTE.h, sum_m(c - 2))
                        i = se.activation(sqb[:, c % 2], xn[:, c], AF.Square)
                        sSC.inc(i)
                    se.wait_ge(sVE.h, stats1_m)
                    i = se.activation(es[:], es[:], AF.Sqrt)
                    sSC.inc(i)
                blk.scalar(f_sc)

                def f_te(te):
                    for c in range(DC):
                        te.wait_ge(sVE.h, copy_m(c))
                        te.wait_ge(sSC.h, sq_m(c))
                        for n in range(TN):
                            te.matmul(sumps[:, nsl(n)], ones_col_bf[:],
                                      xn[:, c, nsl(n)],
                                      start=(c == 0), stop=(c == DC - 1))
                        last = None
                        for n in range(TN):
                            last = te.matmul(sqps[:, nsl(n)], ones_col_bf[:],
                                             sqb[:, c % 2, nsl(n)],
                                             start=(c == 0), stop=(c == DC - 1))
                        sTE.inc(last)
                    te.wait_ge(sVE.h, stats2_m)
                    for n in range(TN):
                        te.matmul(brps[:, nsl(n)], ones_row_f[:],
                                  rsd[:, nsl(n)], start=True, stop=True)
                    last = None
                    for n in range(TN):
                        last = te.matmul(bmps[:, nsl(n)], ones_row_f[:],
                                         mr[:, nsl(n)], start=True, stop=True)
                    sTE.inc(last)
                blk.tensor(f_te)

        def emit_proj(wt, jobs, kind, cag_in_t=None, koff=0, qb=None):
            rope = kind.startswith("rope")
            to_cag = kind.endswith("k")
            qbuf = qb
            with (
                nc.sbuf_tensor(uid("pj_w"), [128, 2, DC, 128], BF16) as wqk,
                nc.sbuf_tensor(uid("pj_kst"), [128, 2, TOK], BF16) as kst,
                nc.sbuf_tensor(uid("pj_rt"), [128, 2, TOK], F32) as rtp,
                nc.sbuf_tensor(uid("pj_ut"), [128, 2, TOK], F32) as utp,
                nc.sbuf_tensor(uid("pj_cos"), [128, TOK], F32) as cosb,
                nc.sbuf_tensor(uid("pj_sin"), [128, TOK], F32) as sinb,
                nc.psum_tensor(uid("pj_ps"), [128, 2, TOK], F32) as pps,
                nc.Block() as blk,
            ):
                n_j = len(jobs)
                ve0, sc0, te0, w0 = sVE.n, sSC.n, sTE.n, sW.n
                g0 = [gp[0].n, gp[1].n]
                s0 = [sy[0].n, sy[1].n]
                tm = lambda j: te0 + j + 1
                em0 = ve0 if rope else sc0
                em = lambda j: em0 + j + 1
                gm = lambda j: g0[j % 2] + 16 * (j // 2 + 1)
                kdma_m = lambda j: s0[j % 2] + 16 * (j // 2 + 1)

                def f_sy(syn):
                    if rope:
                        syn.dma_start(out=cosb[:], in_=cosq.ap()).then_inc(
                            sW.h, 16)
                        sW.n += 16
                        syn.dma_start(out=sinb[:], in_=sinq.ap()).then_inc(
                            sW.h, 16)
                        sW.n += 16
                    if to_cag:
                        eh = sVE.h if rope else sSC.h
                        for j, (oc, hidx) in enumerate(jobs):
                            syn.wait_ge(eh, em(j))
                            dst = cag_in_t.ap()[
                                (koff + hidx * 128) * TOK:
                                (koff + (hidx + 1) * 128) * TOK]
                            dst = dst.rearrange("(p t) -> p t", p=128)
                            i = syn.dma_start(out=dst, in_=kst[:, j % 2])
                            sy[j % 2].inc(i, 16)
                        syn.wait_ge(sy[0].h, sy[0].n)
                        syn.wait_ge(sy[1].h, sy[1].n)
                blk.sync(f_sy)

                def f_gp(g):
                    for j, (oc, hidx) in enumerate(jobs):
                        if j >= 2:
                            g.wait_ge(sTE.h, tm(j - 2))
                        i = g.dma_start(out=wqk[:, j % 2],
                                        in_=colslice(wt, oc * 128, 128))
                        gp[j % 2].inc(i, 16)
                    g.wait_ge(gp[0].h, gp[0].n)
                    g.wait_ge(gp[1].h, gp[1].n)
                blk.gpsimd(f_gp)

                def f_te(te):
                    for j, (oc, hidx) in enumerate(jobs):
                        te.wait_ge(gp[j % 2].h, gm(j))
                        if j >= 2:
                            te.wait_ge(sVE.h if rope else sSC.h, em(j - 2))
                        last = None
                        for dc in range(DC):
                            for n in range(TN):
                                last = te.matmul(
                                    pps[:, j % 2, nsl(n)], wqk[:, j % 2, dc],
                                    xn[:, dc, nsl(n)],
                                    start=(dc == 0), stop=(dc == DC - 1))
                        sTE.inc(last)
                blk.tensor(f_te)

                if rope:
                    def f_ve(ve):
                        if rope:
                            ve.wait_ge(sW.h, sW.n)
                        for j, (oc, hidx) in enumerate(jobs):
                            ve.wait_ge(sTE.h, tm(j))
                            ps = pps[:, j % 2]
                            if to_cag:
                                if j >= 2:
                                    ve.wait_ge(sy[j % 2].h, kdma_m(j - 2))
                                dst = kst[:, j % 2]
                            else:
                                dst = qbuf[:, hidx]
                            t = rtp[:, j % 2]
                            u = utp[:, j % 2]
                            ve.tensor_mul(t, ps, cosb[:])
                            ve.tensor_mul(u[0:64], ps[64:128], sinb[0:64])
                            ve.tensor_mul(u[64:128], ps[0:64], sinb[64:128])
                            ve.drain()
                            ve.tensor_sub(dst[0:64], t[0:64], u[0:64])
                            i = ve.tensor_add(dst[64:128], t[64:128],
                                              u[64:128])
                            sVE.inc(i)
                    blk.vector(f_ve)
                else:
                    def f_sc(se):
                        for j, (oc, hidx) in enumerate(jobs):
                            se.wait_ge(sTE.h, tm(j))
                            if to_cag:
                                if j >= 2:
                                    se.wait_ge(sy[j % 2].h, kdma_m(j - 2))
                                dst = kst[:, j % 2]
                            else:
                                dst = qbuf[:, hidx]
                            i = se.activation(dst, pps[:, j % 2], AF.Copy)
                            sSC.inc(i)
                    blk.scalar(f_sc)

        def emit_vproj(wsrc_fn, out_cols, voff, cag_in_t, Dv):
            NTv = min(512, out_cols)
            NOS = out_cols // NTv
            TC = TOK // 128
            with (
                nc.sbuf_tensor(uid("vp_w"), [128, 2, DC, NTv], BF16) as wv4,
                nc.sbuf_tensor(uid("vp_st"), [128, 2, NTv], BF16) as vst,
                nc.psum_tensor(uid("vp_ps"), [128, 2, NTv], F32) as vps,
                nc.Block() as blk,
            ):
                jobs = [(o, tc) for o in range(NOS) for tc in range(TC)]
                sc0, te0 = sSC.n, sTE.n
                g0 = [gp[0].n, gp[1].n]
                s0 = [sy[0].n, sy[1].n]
                tm = lambda j: te0 + j + 1
                scm = lambda j: sc0 + j + 1
                gm = lambda o: g0[o % 2] + 16 * (o // 2 + 1)
                vdma_m = lambda j: s0[j % 2] + 16 * (j // 2 + 1)

                def f_gp(g):
                    for o in range(NOS):
                        if o >= 2:
                            g.wait_ge(sTE.h, te0 + (o - 1) * TC)
                        i = g.dma_start(out=wv4[:, o % 2],
                                        in_=wsrc_fn(o * NTv, NTv))
                        gp[o % 2].inc(i, 16)
                    g.wait_ge(gp[0].h, gp[0].n)
                    g.wait_ge(gp[1].h, gp[1].n)
                blk.gpsimd(f_gp)

                def f_te(te):
                    for j, (o, tc) in enumerate(jobs):
                        if tc == 0:
                            te.wait_ge(gp[o % 2].h, gm(o))
                        if j >= 2:
                            te.wait_ge(sSC.h, scm(j - 2))
                        last = None
                        for dc in range(DC):
                            last = te.matmul(
                                vps[:, j % 2], xn[:, dc, tc * 128:(tc + 1) * 128],
                                wv4[:, o % 2, dc],
                                start=(dc == 0), stop=(dc == DC - 1))
                        sTE.inc(last)
                blk.tensor(f_te)

                def f_sc(se):
                    for j, (o, tc) in enumerate(jobs):
                        se.wait_ge(sTE.h, tm(j))
                        if j >= 2:
                            se.wait_ge(sy[j % 2].h, vdma_m(j - 2))
                        i = se.activation(vst[:, j % 2], vps[:, j % 2],
                                          AF.Copy)
                        sSC.inc(i)
                blk.scalar(f_sc)

                def f_sy(syn):
                    for j, (o, tc) in enumerate(jobs):
                        syn.wait_ge(sSC.h, scm(j))
                        dst = cag_in_t.ap()[voff:voff + TOK * Dv]
                        dst = dst.rearrange("(t d) -> t d", d=Dv)
                        dst = dst[tc * 128:(tc + 1) * 128,
                                  o * NTv:(o + 1) * NTv]
                        i = syn.dma_start(out=dst, in_=vst[:, j % 2])
                        sy[j % 2].inc(i, 16)
                    syn.wait_ge(sy[0].h, sy[0].n)
                    syn.wait_ge(sy[1].h, sy[1].n)
                blk.sync(f_sy)

        def emit_ag(cin, cout, ctr, waits, groups=None):
            with nc.Block() as blk:
                def f_gp(g):
                    for h, v in waits:
                        g.wait_ge(h, v)
                    i = g.collective_compute(
                        "AllGather", ALU.bypass,
                        replica_groups=groups or [list(range(8))],
                        ins=[cin.ap().opt()], outs=[cout.ap().opt()])
                    ctr.inc(i)
                blk.gpsimd(f_gp)

        def expansion_tiles():
            tiles = []
            for sec, src_t in ((0, gsl), (1, usl)):
                dstv = cag2_in.ap()[sec * SEC:sec * SEC + RG * INTER]
                dstv = dstv.rearrange("(p i) -> p i", i=INTER)
                r0 = 0
                while r0 < RG:
                    npr = min(128, RG - r0)
                    c0 = 0
                    while c0 < INTER:
                        w = min(FDMAX, INTER - c0)
                        src = src_t.ap()[r0:r0 + npr, c0:c0 + w]
                        dst = dstv[r0:r0 + npr, c0:c0 + w]
                        tiles.append((src, npr, w, dst))
                        c0 += w
                    r0 += npr
            amax = max(1, FDMAX // RDC)
            a0 = 0
            while a0 < IC:
                na = min(amax, IC - a0)
                src = dsl.ap().rearrange("(a p) c -> p a c", p=128)
                src = src[:, a0:a0 + na, :]
                dst = cag2_in.ap()[2 * SEC + a0 * 128 * RDC:
                                   2 * SEC + (a0 + na) * 128 * RDC]
                dst = dst.rearrange("(a p c) -> p a c", p=128, c=RDC)
                tiles.append((src, 128, na * RDC, dst))
                a0 += na
            return tiles

        def expansion_order(nt):
            order = []
            for t in range(nt):
                order.append(("load", t))
                if t >= 1:
                    order.append(("ship", t - 1))
            order.append(("ship", nt - 1))
            return order

        def expansion_gp_ops(g, tiles, eibf, ewb, ex0):
            for op, t in expansion_order(len(tiles)):
                src, npr, fd, dst = tiles[t]
                if op == "load":
                    if t >= 2:
                        g.wait_ge(sEX.h, ex0 + t - 1)
                    dstv = eibf[0:npr, t % 2, 0:fd]
                    if src.ndim == 3:
                        dstv = dstv.rearrange("p (a c) -> p a c",
                                              c=src.shape[2])
                    i = g.dma_start(out=dstv, in_=src)
                    exL[t % 2].inc(i, 16)
                else:
                    g.wait_ge(sEX.h, ex0 + t + 1)
                    srcv = ewb[0:npr, t % 2, 0:fd]
                    if dst.ndim == 3:
                        srcv = srcv.rearrange("p (a c) -> p a c",
                                              c=dst.shape[2])
                    i = g.dma_start(out=dst, in_=srcv)
                    exS[t % 2].inc(i, 16)

        def expansion_ve_tile(ve, tiles, t, eibf, ewb, etb, l0, s0):
            src, npr, fd, dst = tiles[t]
            ve.wait_ge(exL[t % 2].h, l0[t % 2] + 16 * (t // 2 + 1))
            if t >= 2:
                ve.wait_ge(exS[t % 2].h, s0[t % 2] + 16 * (t // 2 - 1 + 1))
            ib = eibf[0:npr, t % 2, 0:fd]
            wb = ewb[0:npr, t % 2, 0:fd]
            tb = etb[0:npr, 0:fd]
            dr = ((lambda: ve.drain()) if (npr < 128 or fd < 512)
                  else (lambda: None))
            ve.tensor_scalar(wb, ib, 0.0, luts[0:npr, 0:1],
                             ALU.is_equal, ALU.mult)
            i = None
            for l in range(1, 16):
                ve.tensor_scalar(tb, ib, float(l), luts[0:npr, l:l + 1],
                                 ALU.is_equal, ALU.mult)
                dr()
                i = ve.tensor_tensor(wb, wb, tb, ALU.add)
                dr()
            sEX.inc(i)

        def emit_expansion():
            tiles = expansion_tiles()
            nt = len(tiles)
            with (
                nc.sbuf_tensor(uid("ex_i"), [128, 2, FDMAX], BF16) as eibf,
                nc.sbuf_tensor(uid("ex_w"), [128, 2, FDMAX], BF16) as ewb,
                nc.sbuf_tensor(uid("ex_t"), [128, FDMAX], BF16) as etb,
                nc.Block() as blk,
            ):
                ex0 = sEX.n
                l0 = [exL[0].n, exL[1].n]
                s0 = [exS[0].n, exS[1].n]

                def f_gp(g):
                    expansion_gp_ops(g, tiles, eibf, ewb, ex0)
                    for c_ in (exL[0], exL[1], exS[0], exS[1]):
                        g.wait_ge(c_.h, c_.n)
                blk.gpsimd(f_gp)

                def f_ve(ve):
                    ve.wait_ge(sW.h, p0_w)
                    for t in range(nt):
                        expansion_ve_tile(ve, tiles, t, eibf, ewb, etb, l0, s0)
                blk.vector(f_ve)
            return [(exS[0].h, exS[0].n), (exS[1].h, exS[1].n)]

        def emit_attention(kind, expansion=False, qb=None):
            masked = kind == "b"
            qbuf = qb
            Hq = H
            cag = cag3_out if masked else cag1_out
            Lr = L3 if masked else L1
            Dv = KVD if masked else D
            voff = VOFF3 if masked else VOFF1
            cc = sC3 if masked else sC1
            groups = NKV if masked else H
            hpg = G if masked else 1
            nbuf = 1 if masked else 2
            with (
                nc.sbuf_tensor(uid("at_kh"), [128, nbuf, S], BF16) as kh,
                nc.sbuf_tensor(uid("at_vh"), [128, nbuf, KC, 128], BF16) as vh,
                nc.sbuf_tensor(uid("at_ot"), [128, TOK], BF16) as otmp,
                nc.psum_tensor(uid("at_st"), [128, 2, TOK], F32) as stp,
                nc.psum_tensor(uid("at_pv"), [128, TOK], F32) as pvp,
                nc.psum_tensor(uid("at_dn"), [1, TOK], F32) as dnp,
                nc.sbuf_tensor(uid("at_pt"), [128, KC, TOK], BF16) as ptb,
                nc.sbuf_tensor(uid("at_rv"), [1, TOK], F32) as rvec,
                ExitStack() as mst,
            ):
                msk = None
                if masked:
                    msk = mst.enter_context(
                        nc.sbuf_tensor(uid("at_mk"), [128, KC, TOK], BF16))
                tiles = None
                if expansion:
                    tiles = expansion_tiles()
                    eibf = mst.enter_context(
                        nc.sbuf_tensor(uid("ex_i"), [128, 2, FDMAX], BF16))
                    ewb = mst.enter_context(
                        nc.sbuf_tensor(uid("ex_w"), [128, 2, FDMAX], BF16))
                    etb = mst.enter_context(
                        nc.sbuf_tensor(uid("ex_t"), [128, FDMAX], BF16))
                    ex0 = sEX.n
                    exl0 = [exL[0].n, exL[1].n]
                    exs0 = [exS[0].n, exS[1].n]
                with nc.Block() as blk:
                    ve0, sc0, te0 = sVE.n, sSC.n, sTE.n
                    s0 = [sy[0].n, sy[1].n]
                    scph = KC + 1
                    veph = (KC if masked else 0) + 2
                    teph = KC + 2
                    exp_m = lambda h, kc: sc0 + h * scph + kc + 1
                    copy_m = lambda h: sc0 + h * scph + scph
                    msk_m = lambda h, kc: ve0 + h * veph + kc + 1
                    recip_m = lambda h: ve0 + h * veph + veph - 1
                    evac_m = lambda h: ve0 + h * veph + veph
                    st_m = lambda h, kc: te0 + h * teph + kc + 1
                    pv_m = lambda h: te0 + h * teph + KC + 1
                    bc_m = lambda h: te0 + h * teph + KC + 2
                    mask_w = sMK.n + (16 if masked else 0)
                    kv_m = lambda gi: (sy[gi % 2].h,
                                       s0[gi % 2] + 64 * (gi // 2 + 1))

                    def f_sy(syn):
                        if masked:
                            i = syn.dma_start(out=msk[:], in_=maskb.ap(
                            ).rearrange("(a p) t -> p a t", p=128))
                            sMK.inc(i, 16)
                        syn.wait_ge(cc.h, cc.n)
                        for gi in range(groups):
                            if gi >= nbuf:
                                syn.wait_ge(sTE.h,
                                            pv_m((gi - nbuf) * hpg + hpg - 1))
                            gb = gi % nbuf
                            for half in range(2):
                                src = cag.ap()[half,
                                               gi * 128 * TOK:
                                               (gi + 1) * 128 * TOK]
                                src = src.rearrange("(p t) -> p t", p=128)
                                i = syn.dma_start(
                                    out=kh[:, gb, half * TOK:(half + 1) * TOK],
                                    in_=src)
                                sy[gi % 2].inc(i, 16)
                            for half in range(2):
                                src = cag.ap()[half, voff:voff + TOK * Dv]
                                src = src.rearrange(
                                    "(a p d) -> p a d", p=128, d=Dv)
                                src = src[:, :, gi * 128:(gi + 1) * 128]
                                i = syn.dma_start(
                                    out=vh[:, gb, half * KCh:(half + 1) * KCh],
                                    in_=src)
                                sy[gi % 2].inc(i, 16)
                        syn.wait_ge(sy[0].h, sy[0].n)
                        syn.wait_ge(sy[1].h, sy[1].n)
                    blk.sync(f_sy)

                    if expansion:
                        def f_gp(g):
                            expansion_gp_ops(g, tiles, eibf, ewb, ex0)
                            for c_ in (exL[0], exL[1], exS[0], exS[1]):
                                g.wait_ge(c_.h, c_.n)
                            i = g.collective_compute(
                                "AllGather", ALU.bypass,
                                replica_groups=[list(range(8))],
                                ins=[cag2_in.ap().opt()],
                                outs=[cag2_out.ap().opt()])
                            sC2.inc(i)
                        blk.gpsimd(f_gp)

                    def f_te(te):
                        for h in range(Hq):
                            gi = h // hpg
                            gb = gi % nbuf
                            te.wait_ge(*kv_m(gi))
                            if h >= 1:
                                te.wait_ge(sSC.h, copy_m(h - 1))
                                te.wait_ge(sVE.h, evac_m(h - 1))
                            for kc in range(KC):
                                if kc >= 2:
                                    te.wait_ge(sSC.h, exp_m(h, kc - 2))
                                last = None
                                for n in range(TN):
                                    last = te.matmul(
                                        stp[:, kc % 2, nsl(n)],
                                        kh[:, gb, kc * 128:(kc + 1) * 128],
                                        qbuf[:, h, nsl(n)],
                                        start=True, stop=True)
                                sTE.inc(last)
                            last = None
                            for kc in range(KC):
                                te.wait_ge(sSC.h, exp_m(h, kc))
                                if masked:
                                    te.wait_ge(sVE.h, msk_m(h, kc))
                                for n in range(TN):
                                    te.matmul(pvp[:, nsl(n)],
                                              vh[:, gb, kc],
                                              ptb[:, kc, nsl(n)],
                                              start=(kc == 0),
                                              stop=(kc == KC - 1))
                                for n in range(TN):
                                    last = te.matmul(dnp[:, nsl(n)],
                                                     ones_col_bf[:],
                                                     ptb[:, kc, nsl(n)],
                                                     start=(kc == 0),
                                                     stop=(kc == KC - 1))
                            sTE.inc(last)
                            te.wait_ge(sVE.h, recip_m(h))
                            last = None
                            for n in range(TN):
                                last = te.matmul(stp[:, 0, nsl(n)],
                                                 ones_row_f[:],
                                                 rvec[:, nsl(n)],
                                                 start=True, stop=True)
                            sTE.inc(last)
                    blk.tensor(f_te)

                    def f_sc(se):
                        for h in range(Hq):
                            for kc in range(KC):
                                se.wait_ge(sTE.h, st_m(h, kc))
                                i = se.activation(ptb[:, kc], stp[:, kc % 2],
                                                  AF.Exp, bias=negshift[:],
                                                  scale=SCALE)
                                sSC.inc(i)
                            se.wait_ge(sTE.h, pv_m(h))
                            if h >= 1:
                                se.wait_ge(sVE.h, evac_m(h - 1))
                            i = se.activation(otmp[:], pvp[:], AF.Copy)
                            sSC.inc(i)
                    blk.scalar(f_sc)

                    tptr = [0]

                    def f_ve(ve):
                        if masked:
                            ve.wait_ge(sMK.h, mask_w)
                        if expansion:
                            ve.wait_ge(sW.h, p0_w)
                        for h in range(Hq):
                            if masked:
                                for kc in range(KC):
                                    ve.wait_ge(sSC.h, exp_m(h, kc))
                                    i = ve.tensor_mul(ptb[:, kc], ptb[:, kc],
                                                      msk[:, kc])
                                    sVE.inc(i)
                            ve.wait_ge(sTE.h, pv_m(h))
                            i = ve.reciprocal(rvec[:], dnp[:])
                            sVE.inc(i)
                            ve.wait_ge(sTE.h, bc_m(h))
                            ve.wait_ge(sSC.h, copy_m(h))
                            i = ve.tensor_mul(qbuf[:, h], otmp[:], stp[:, 0])
                            sVE.inc(i)
                            if expansion:
                                while tptr[0] < min(len(tiles), 2 * (h + 1)):
                                    expansion_ve_tile(ve, tiles, tptr[0],
                                                      eibf, ewb, etb,
                                                      exl0, exs0)
                                    tptr[0] += 1
                        if expansion:
                            for t in range(tptr[0], len(tiles)):
                                expansion_ve_tile(ve, tiles, t, eibf, ewb,
                                                  etb, exl0, exs0)
                    blk.vector(f_ve)

        def emit_outproj(wt, qb=None):
            qbuf = qb
            with (
                nc.sbuf_tensor(uid("op_w"), [128, 2, DC, 128], BF16) as wti,
                nc.psum_tensor(uid("op_ps"), [128, 2, TOK], F32) as ops,
                nc.Block() as blk,
            ):
                ve0, te0 = sVE.n, sTE.n
                g0 = [gp[0].n, gp[1].n]
                tm = lambda j: te0 + j + 1
                em = lambda j: ve0 + j + 1
                gm = lambda j: g0[j % 2] + 16 * (j // 2 + 1)

                def f_gp(g):
                    for j in range(DC):
                        if j >= 2:
                            g.wait_ge(sTE.h, tm(j - 2))
                        i = g.dma_start(out=wti[:, j % 2],
                                        in_=colslice(wt, j * 128, 128))
                        gp[j % 2].inc(i, 16)
                    g.wait_ge(gp[0].h, gp[0].n)
                    g.wait_ge(gp[1].h, gp[1].n)
                blk.gpsimd(f_gp)

                def f_te(te):
                    for j in range(DC):
                        te.wait_ge(gp[j % 2].h, gm(j))
                        if j >= 2:
                            te.wait_ge(sVE.h, em(j - 2))
                        last = None
                        for oc in range(H):
                            for n in range(TN):
                                last = te.matmul(
                                    ops[:, j % 2, nsl(n)], wti[:, j % 2, oc],
                                    qbuf[:, oc, nsl(n)],
                                    start=(oc == 0), stop=(oc == H - 1))
                        sTE.inc(last)
                blk.tensor(f_te)

                def f_ve(ve):
                    for j in range(DC):
                        ve.wait_ge(sTE.h, tm(j))
                        i = ve.tensor_add(xsb[:, j], xsb[:, j],
                                          ops[:, j % 2])
                        sVE.inc(i)
                blk.vector(f_ve)

        def emit_ffn(wait_c2, store_y=False):
            fsl = lambda n: slice(n * FT, (n + 1) * FT)
            with nc.sbuf_tensor(uid("ff_h"), [128, IC, TOK], BF16) as hbuf:
                jobs = [(oc, th) for oc in range(IC) for th in range(FTN)]
                with (
                    nc.sbuf_tensor(uid("ff_wg"), [128, 2, DC, 128], BF16) as wg,
                    nc.sbuf_tensor(uid("ff_wu"), [128, 2, DC, 128], BF16) as wu,
                    nc.sbuf_tensor(uid("ff_st"), [128, 4, FT], BF16) as stmp,
                    nc.psum_tensor(uid("ff_g"), [128, 4, FT], F32) as gps,
                    nc.psum_tensor(uid("ff_u"), [128, 4, FT], F32) as ups,
                    nc.Block() as blk,
                ):
                    ve0, sc0, te0 = sVE.n, sSC.n, sTE.n
                    s0 = [sy[0].n, sy[1].n]
                    g0 = [gp[0].n, gp[1].n]
                    tm = lambda j: te0 + j + 1
                    sim = lambda j: sc0 + j + 1
                    hm = lambda j: ve0 + j + 1
                    ndg = 8
                    wms = lambda oc: (sy[oc % 2].h,
                                      s0[oc % 2] + 16 * ndg * (oc // 2 + 1))
                    wmg = lambda oc: (gp[oc % 2].h,
                                      g0[oc % 2] + 16 * ndg * (oc // 2 + 1))

                    def gu_loads(q, ctrs, sec, dstb):
                        if wait_c2:
                            q.wait_ge(sC2.h, sC2.n)
                        for oc in range(IC):
                            if oc >= 2:
                                q.wait_ge(sTE.h, tm((oc - 2) * FTN + FTN - 1))
                            if RG % 128 == 0:
                                rgc = RG // 128
                                for r in range(8):
                                    src = cag2_out.ap()[
                                        r, sec * SEC:sec * SEC + RG * INTER]
                                    src = src.rearrange("(a p i) -> p a i",
                                                        p=128, i=INTER)
                                    src = src[:, :, oc * 128:(oc + 1) * 128]
                                    dst = dstb[:, oc % 2,
                                               r * rgc:(r + 1) * rgc]
                                    i = q.dma_start(out=dst, in_=src)
                                    ctrs[oc % 2].inc(i, 16)
                            else:
                                for r in range(8):
                                    src = cag2_out.ap()[
                                        r, sec * SEC:sec * SEC + RG * INTER]
                                    src = src.rearrange("(a i) -> a i",
                                                        i=INTER)
                                    src = src[:, oc * 128:(oc + 1) * 128]
                                    row0 = r * RG
                                    dst = dstb[row0 % 128:row0 % 128 + RG,
                                               oc % 2, row0 // 128]
                                    i = q.dma_start(out=dst, in_=src)
                                    ctrs[oc % 2].inc(i, 16)
                        q.wait_ge(ctrs[0].h, ctrs[0].n)
                        q.wait_ge(ctrs[1].h, ctrs[1].n)

                    blk.sync(lambda syn: gu_loads(syn, sy, 0, wg))
                    blk.gpsimd(lambda g: gu_loads(g, gp, 1, wu))

                    def f_te(te):
                        for j, (oc, th) in enumerate(jobs):
                            if th == 0:
                                te.wait_ge(*wms(oc))
                                te.wait_ge(*wmg(oc))
                            if j >= 4:
                                te.wait_ge(sSC.h, sim(j - 4))
                                te.wait_ge(sVE.h, hm(j - 4))
                            for dc in range(DC):
                                te.matmul(gps[:, j % 4], wg[:, oc % 2, dc],
                                          xn[:, dc, fsl(th)],
                                          start=(dc == 0), stop=(dc == DC - 1))
                            last = None
                            for dc in range(DC):
                                last = te.matmul(
                                    ups[:, j % 4], wu[:, oc % 2, dc],
                                    xn[:, dc, fsl(th)],
                                    start=(dc == 0), stop=(dc == DC - 1))
                            sTE.inc(last)
                    blk.tensor(f_te)

                    def f_sc(se):
                        for j, (oc, th) in enumerate(jobs):
                            se.wait_ge(sTE.h, tm(j))
                            i = se.activation(stmp[:, j % 4],
                                              gps[:, j % 4], AF.Silu)
                            sSC.inc(i)
                    blk.scalar(f_sc)

                    def f_ve(ve):
                        for j, (oc, th) in enumerate(jobs):
                            ve.wait_ge(sSC.h, sim(j))
                            i = ve.tensor_mul(hbuf[:, oc, fsl(th)],
                                              stmp[:, j % 4],
                                              ups[:, j % 4])
                            sVE.inc(i)
                    blk.vector(f_ve)

                with (
                    nc.sbuf_tensor(uid("ff_wd"), [128, 2, IC, 128], BF16) as wd,
                    nc.psum_tensor(uid("ff_d"), [128, 4, FT], F32) as dps,
                    nc.Block() as blk,
                ):
                    djobs = [(dj, th) for dj in range(DC) for th in range(FTN)]
                    ve0, te0 = sVE.n, sTE.n
                    s0 = [sy[0].n, sy[1].n]
                    tm = lambda j: te0 + j + 1
                    em = lambda j: ve0 + j + 1

                    def down_dmas(dj, dstb):
                        out = []
                        col0 = dj * 128
                        done = 0
                        while done < 128:
                            c = col0 + done
                            r = c // RDC
                            lc = c % RDC
                            w = min(RDC - lc, 128 - done)
                            src = cag2_out.ap()[r, 2 * SEC:3 * SEC]
                            src = src.rearrange("(a p c) -> p a c",
                                                p=128, c=RDC)
                            src = src[:, :, lc:lc + w]
                            dst = dstb[:, :, done:done + w]
                            out.append((dst, src))
                            done += w
                        return out

                    def f_sy(syn):
                        for dj in range(DC):
                            if dj >= 2:
                                syn.wait_ge(sTE.h,
                                            tm((dj - 2) * FTN + FTN - 1))
                            for dst, src in down_dmas(dj, wd[:, dj % 2]):
                                i = syn.dma_start(out=dst, in_=src)
                                sy[dj % 2].inc(i, 16)
                        syn.wait_ge(sy[0].h, sy[0].n)
                        syn.wait_ge(sy[1].h, sy[1].n)
                    blk.sync(f_sy)

                    wm_pred = []
                    acc = [s0[0], s0[1]]
                    for dj in range(DC):
                        acc[dj % 2] += 16 * len(down_dmas(dj, wd[:, 0]))
                        wm_pred.append((sy[dj % 2].h, acc[dj % 2]))

                    def f_te(te):
                        for j, (dj, th) in enumerate(djobs):
                            if th == 0:
                                te.wait_ge(*wm_pred[dj])
                            if j >= 4:
                                te.wait_ge(sVE.h, em(j - 4))
                            last = None
                            for oc in range(IC):
                                last = te.matmul(
                                    dps[:, j % 4], wd[:, dj % 2, oc],
                                    hbuf[:, oc, fsl(th)],
                                    start=(oc == 0), stop=(oc == IC - 1))
                            sTE.inc(last)
                    blk.tensor(f_te)

                    def f_ve(ve):
                        for j, (dj, th) in enumerate(djobs):
                            ve.wait_ge(sTE.h, tm(j))
                            i = ve.tensor_add(xsb[:, dj, fsl(th)],
                                              xsb[:, dj, fsl(th)],
                                              dps[:, j % 4])
                            sVE.inc(i)
                    blk.vector(f_ve)

                    if store_y:
                        def f_sc(se):
                            for dj in range(DC):
                                se.wait_ge(sVE.h,
                                           em(dj * FTN + FTN - 1))
                                dst = y.ap().rearrange(
                                    "(c p) t -> p c t", p=128)[:, dj]
                                i = se.dma_start(out=dst, in_=xsb[:, dj])
                                sW.inc(i, 16)
                        blk.scalar(f_sc)

        first_ln = [True]

        def ln():
            emit_ln(first=first_ln[0])
            first_ln[0] = False

        need_ffn = ("f1" in stages) or ("f2" in stages)
        if "a" in stages:
            ln()
            with nc.sbuf_tensor("qbuf_a", [128, H, TOK], BF16) as qa:
                jobs_k = [(H + h, h) for h in range(H)]
                emit_proj(wqkvT, jobs_k, "ropek", cag1_in, 0)
                emit_vproj(lambda c0, w: colslice(wqkvT, 2 * H * 128 + c0, w),
                           D, VOFF1, cag1_in, D)
                emit_ag(cag1_in, cag1_out, sC1,
                        [(sy[0].h, sy[0].n), (sy[1].h, sy[1].n)],
                        groups=PAIRS)
                jobs_qk = [(h, h) for h in range(H)]
                emit_proj(wqkvT, jobs_qk, "ropeq", qb=qa)
                emit_attention("a", expansion=need_ffn, qb=qa)
                emit_outproj(woT, qb=qa)
        elif need_ffn:
            exp_waits = emit_expansion()
            emit_ag(cag2_in, cag2_out, sC2, exp_waits)
        if "f1" in stages:
            ln()
            emit_ffn(wait_c2=True)
        if "b" in stages:
            ln()
            with nc.sbuf_tensor("qbuf_b", [128, H, TOK], BF16) as qb_:
                jobs_kb = [(g, g) for g in range(NKV)]
                emit_proj(wkb, jobs_kb, "copyk", cag3_in, 0)
                emit_vproj(lambda c0, w: wvb.ap().rearrange(
                    "(c p) o -> p c o", p=128)[:, :, c0:c0 + w],
                    KVD, VOFF3, cag3_in, KVD)
                emit_ag(cag3_in, cag3_out, sC3,
                        [(sy[0].h, sy[0].n), (sy[1].h, sy[1].n)],
                        groups=PAIRS)
                jobs_qb = [(h, h) for h in range(H)]
                emit_proj(wqb, jobs_qb, "copyq", qb=qb_)
                emit_attention("b", qb=qb_)
                emit_outproj(wob, qb=qb_)
        if "f2" in stages:
            ln()
            emit_ffn(wait_c2=("f1" not in stages),
                     store_y=True)

        with nc.Block() as blk:
            def f_sy(sy):
                if "f2" not in stages:
                    i = sy.dma_start(out=y.ap().rearrange(
                        "(c p) t -> p c t", p=128), in_=xsb[:])
                    sW.inc(i, 16)
                sy.wait_ge(sW.h, sW.n)
            blk.sync(f_sy)

    if verbose:
        print(f"instructions: {nc.n_instructions()}")
    return nc



def _rope_tables(S):
    inv = 1.0 / (10000.0 ** (np.arange(0, 128, 2, dtype=np.float32) / 128.0))
    t = np.arange(S, dtype=np.float32)
    fr = np.einsum("i,j->ij", t, inv)
    emb = np.concatenate([fr, fr], axis=-1)
    return np.cos(emb).astype(np.float32), np.sin(emb).astype(np.float32)


def make_in_maps(cfg, x, w_qkv, w_out, Wq, Wk, Wv, Wo, lut, gate_idx, up_idx,
                 down_idx):
    D, TOK, S = cfg.D, cfg.TOK, cfg.S
    f32 = lambda a: np.ascontiguousarray(np.asarray(a, dtype=np.float32))
    i32 = lambda a: np.ascontiguousarray(np.asarray(a, dtype=np.int32))
    wqkvT = f32(np.asarray(w_qkv).T)
    woT = f32(np.asarray(w_out).T)
    wq, wk, wv, wo = f32(Wq), f32(Wk), f32(Wv), f32(Wo)
    gT = i32(np.asarray(gate_idx).T)
    uT = i32(np.asarray(up_idx).T)
    dT = i32(np.asarray(down_idx).T)
    lutbc = f32(np.tile(np.asarray(lut, dtype=np.float32)[None, :], (128, 1)))
    cos, sin = _rope_tables(S)
    in_maps = []
    for c in range(8):
        b, hh = c // 2, c % 2
        tok = slice(hh * TOK, (hh + 1) * TOK)
        kk = np.arange(S, dtype=np.int64)[:, None]
        qq = hh * TOK + np.arange(TOK, dtype=np.int64)[None, :]
        mask = (kk <= qq).astype(ml_dtypes.bfloat16)
        in_maps.append({
            "xin": f32(np.asarray(x)[b, tok, :].T),
            "wqkvT": wqkvT, "woT": woT,
            "wqb": wq, "wkb": wk, "wvb": wv, "wob": wo,
            "cosq": f32(cos[tok, :].T), "sinq": f32(sin[tok, :].T),
            "maskb": np.ascontiguousarray(mask),
            "lutin": lutbc,
            "gsl": np.ascontiguousarray(gT[c * cfg.RG:(c + 1) * cfg.RG, :]),
            "usl": np.ascontiguousarray(uT[c * cfg.RG:(c + 1) * cfg.RG, :]),
            "dsl": np.ascontiguousarray(dT[:, c * cfg.RDC:(c + 1) * cfg.RDC]),
        })
    return in_maps


def assemble(cfg, results):
    B, S, D, TOK = cfg.B, cfg.S, cfg.D, cfg.TOK
    out = np.empty((B, S, D), dtype=np.float32)
    for c in range(8):
        b, hh = c // 2, c % 2
        out[b, hh * TOK:(hh + 1) * TOK, :] = results[c]["y"].T
    return out


def kernel(L0_attn_a_x, L0_attn_a_w_qkv, L0_attn_a_w_out, L0_attn_b_Wq,
           L0_attn_b_Wk, L0_attn_b_Wv, L0_attn_b_Wo, L0_ffn_lut,
           L0_ffn_gate_idx, L0_ffn_up_idx, L0_ffn_down_idx):
    cfg = FULL
    nc = build(cfg)
    in_maps = make_in_maps(cfg, L0_attn_a_x, L0_attn_a_w_qkv, L0_attn_a_w_out,
                           L0_attn_b_Wq, L0_attn_b_Wk, L0_attn_b_Wv,
                           L0_attn_b_Wo, L0_ffn_lut, L0_ffn_gate_idx,
                           L0_ffn_up_idx, L0_ffn_down_idx)
    res = run_bass_kernel_spmd(nc, in_maps, core_ids=list(range(8)))
    return assemble(cfg, res.results)


# version 08
# speedup vs baseline: 1.3910x; 1.0030x over previous
import math
from contextlib import ExitStack

import numpy as np
import ml_dtypes
import concourse.bass as bass
import concourse.mybir as mybir
from concourse.bass_utils import run_bass_kernel_spmd

F32 = mybir.dt.float32
BF16 = mybir.dt.bfloat16
I32 = mybir.dt.int32
AF = mybir.ActivationFunctionType
ALU = mybir.AluOpType

SCALE = 1.0 / math.sqrt(128.0)
SHIFT = 12.0
EPS = 1e-5
FDMAX = 2304
PAIRS = [[0, 1], [2, 3], [4, 5], [6, 7]]


class Cfg:
    def __init__(self, H, NKV, S, IC, B=4):
        self.H = H
        self.NKV = NKV
        self.G = H // NKV
        self.D = H * 128
        self.DC = self.D // 128
        self.S = S
        self.B = B
        self.TOK = B * S // 8
        assert self.TOK == S // 2
        self.KC = S // 128
        self.KCh = self.TOK // 128
        self.IC = IC
        self.INTER = IC * 128
        self.KVD = NKV * 128
        self.NT = min(512, self.TOK)
        self.TN = self.TOK // self.NT
        self.FT = min(512, self.TOK)
        self.FTN = self.TOK // self.FT
        self.RG = self.D // 8
        self.RDC = self.D // 8
        self.SEC = self.D * self.INTER // 8


FULL = Cfg(H=16, NKV=4, S=2048, IC=43)
SMALL = Cfg(H=2, NKV=1, S=512, IC=2)


def build(cfg, verbose=False, stages=('a', 'f1', 'b', 'f2')):
    H, D, DC, TOK, KC, KCh = cfg.H, cfg.D, cfg.DC, cfg.TOK, cfg.KC, cfg.KCh
    IC, KVD, NT, TN = cfg.IC, cfg.KVD, cfg.NT, cfg.TN
    S, G, NKV, INTER, RG, RDC, SEC = (cfg.S, cfg.G, cfg.NKV, cfg.INTER,
                                      cfg.RG, cfg.RDC, cfg.SEC)
    FT, FTN = cfg.FT, cfg.FTN

    nc = bass.Bass("TRN2", target_bir_lowering=False)

    xin = nc.dram_tensor("xin", [D, TOK], F32, kind="ExternalInput")
    wqkvT = nc.dram_tensor("wqkvT", [D, 3 * D], F32, kind="ExternalInput")
    woT = nc.dram_tensor("woT", [D, D], F32, kind="ExternalInput")
    wqb = nc.dram_tensor("wqb", [D, D], F32, kind="ExternalInput")
    wkb = nc.dram_tensor("wkb", [D, KVD], F32, kind="ExternalInput")
    wvb = nc.dram_tensor("wvb", [D, KVD], F32, kind="ExternalInput")
    wob = nc.dram_tensor("wob", [D, D], F32, kind="ExternalInput")
    cosq = nc.dram_tensor("cosq", [128, TOK], F32, kind="ExternalInput")
    sinq = nc.dram_tensor("sinq", [128, TOK], F32, kind="ExternalInput")
    maskb = nc.dram_tensor("maskb", [S, TOK], BF16, kind="ExternalInput")
    lutin = nc.dram_tensor("lutin", [128, 16], F32, kind="ExternalInput")
    gsl = nc.dram_tensor("gsl", [RG, INTER], I32, kind="ExternalInput")
    usl = nc.dram_tensor("usl", [RG, INTER], I32, kind="ExternalInput")
    dsl = nc.dram_tensor("dsl", [INTER, RDC], I32, kind="ExternalInput")
    y = nc.dram_tensor("y", [D, TOK], F32, kind="ExternalOutput")

    L1 = 2 * D * TOK
    cag1_in = nc.dram_tensor("cag1_in", [L1], BF16)
    cag1_out = nc.dram_tensor("cag1_out", [2, L1], BF16)
    L2 = 3 * SEC
    cag2_in = nc.dram_tensor("cag2_in", [L2], BF16)
    cag2_out = nc.dram_tensor("cag2_out", [8, L2], BF16, addr_space="Shared")
    L3 = 2 * KVD * TOK
    cag3_in = nc.dram_tensor("cag3_in", [L3], BF16)
    cag3_out = nc.dram_tensor("cag3_out", [2, L3], BF16)
    VOFF1 = D * TOK
    VOFF3 = KVD * TOK

    class Ctr:
        def __init__(self, name):
            self.h = nc.alloc_semaphore(name)
            self.n = 0

        def inc(self, instr, k=1):
            instr.then_inc(self.h, k)
            self.n += k
            return self.n

    sW = Ctr("sW")
    gp = [Ctr("gp0"), Ctr("gp1")]
    sy = [Ctr("sy0"), Ctr("sy1")]
    exL = [Ctr("exL0"), Ctr("exL1")]
    exS = [Ctr("exS0"), Ctr("exS1")]
    sEX = Ctr("sEX")
    sMK = Ctr("sMK")
    sC1 = Ctr("sC1")
    sC2 = Ctr("sC2")
    sC3 = Ctr("sC3")
    sTE = Ctr("sTE")
    sSC = Ctr("sSC")
    sVE = Ctr("sVE")

    _uid = [0]

    def uid(s):
        _uid[0] += 1
        return f"{s}{_uid[0]}"

    def nsl(n):
        return slice(n * NT, (n + 1) * NT)

    def colslice(t, j0, w):
        return t.ap().rearrange("(c p) o -> p c o", p=128)[:, :, j0:j0 + w]

    st = ExitStack()
    with st:
        sb = lambda name, shape, dt: st.enter_context(
            nc.sbuf_tensor(name, shape, dt))
        xsb = sb("xsb", [128, DC, TOK], F32)
        xn = sb("xnb", [128, DC, TOK], BF16)
        luts = sb("luts", [128, 16], F32)
        ones_col_bf = sb("ones_col_bf", [128, 1], BF16)
        ones_row_f = sb("ones_row_f", [1, 128], F32)
        ones_row_bf = sb("ones_row_bf", [1, 128], BF16)
        negshift = sb("negshift", [128, 1], F32)

        with nc.Block() as blk:
            def f_sy(sy):
                sy.dma_start(out=xsb[:], in_=xin.ap().rearrange(
                    "(c p) t -> p c t", p=128)).then_inc(sW.h, 16)
                sW.n += 16
                sy.dma_start(out=luts[:], in_=lutin.ap()).then_inc(sW.h, 16)
                sW.n += 16
            blk.sync(f_sy)

            def f_ve(ve):
                ve.memset(ones_col_bf[:], 1.0)
                ve.memset(ones_row_f[:], 1.0)
                ve.memset(ones_row_bf[:], 1.0)
                ve.memset(negshift[:], -SHIFT)
            blk.vector(f_ve)
        p0_w = sW.n

        def emit_ln(first=False):
            with (
                nc.sbuf_tensor(uid("ln_mu"), [1, TOK], F32) as mu,
                nc.sbuf_tensor(uid("ln_es"), [1, TOK], F32) as es,
                nc.sbuf_tensor(uid("ln_rs"), [1, TOK], F32) as rsd,
                nc.sbuf_tensor(uid("ln_mr"), [1, TOK], F32) as mr,
                nc.sbuf_tensor(uid("ln_sq"), [128, 2, TOK], BF16) as sqb,
                nc.sbuf_tensor(uid("ln_nrm"), [128, 2, TOK], F32) as nrm,
                nc.psum_tensor(uid("ln_sum"), [1, TOK], F32) as sumps,
                nc.psum_tensor(uid("ln_sqp"), [1, TOK], F32) as sqps,
                nc.psum_tensor(uid("ln_br"), [128, TOK], F32) as brps,
                nc.psum_tensor(uid("ln_bm"), [128, TOK], F32) as bmps,
                nc.Block() as blk,
            ):
                ve0, sc0, te0 = sVE.n, sSC.n, sTE.n
                copy_m = lambda c: ve0 + c + 1
                sq_m = lambda c: sc0 + c + 1
                sum_m = lambda c: te0 + c + 1
                stats1_m = ve0 + DC + 1
                sqrt_m = sc0 + DC + 1
                stats2_m = ve0 + DC + 2
                bcast_m = te0 + DC + 1

                def f_ve(ve):
                    if first:
                        ve.wait_ge(sW.h, p0_w)
                    for c in range(DC):
                        i = ve.tensor_copy(xn[:, c], xsb[:, c])
                        sVE.inc(i)
                    ve.wait_ge(sTE.h, sum_m(DC - 1))
                    ve.tensor_scalar(mu[:], sumps[:], 1.0 / D, None, ALU.mult)
                    ve.tensor_scalar(es[:], sqps[:], 1.0 / D, None, ALU.mult)
                    ve.drain()
                    ve.tensor_mul(rsd[:], mu[:], mu[:])
                    ve.drain()
                    ve.tensor_sub(es[:], es[:], rsd[:])
                    ve.drain()
                    i = ve.tensor_scalar(es[:], es[:], EPS, None, ALU.add)
                    sVE.inc(i)
                    ve.wait_ge(sSC.h, sqrt_m)
                    ve.reciprocal(rsd[:], es[:])
                    ve.drain()
                    i = ve.tensor_mul(mr[:], mu[:], rsd[:])
                    sVE.inc(i)
                    for c in range(DC):
                        ve.wait_ge(sTE.h, bcast_m)
                        ve.tensor_mul(nrm[:, c % 2], xsb[:, c], brps[:])
                        ve.drain()
                        i = ve.tensor_sub(xn[:, c], nrm[:, c % 2], bmps[:])
                        sVE.inc(i)
                blk.vector(f_ve)

                def f_sc(se):
                    for c in range(DC):
                        se.wait_ge(sVE.h, copy_m(c))
                        if c >= 2:
                            se.wait_ge(sTE.h, sum_m(c - 2))
                        i = se.activation(sqb[:, c % 2], xn[:, c], AF.Square)
                        sSC.inc(i)
                    se.wait_ge(sVE.h, stats1_m)
                    i = se.activation(es[:], es[:], AF.Sqrt)
                    sSC.inc(i)
                blk.scalar(f_sc)

                def f_te(te):
                    for c in range(DC):
                        te.wait_ge(sVE.h, copy_m(c))
                        te.wait_ge(sSC.h, sq_m(c))
                        for n in range(TN):
                            te.matmul(sumps[:, nsl(n)], ones_col_bf[:],
                                      xn[:, c, nsl(n)],
                                      start=(c == 0), stop=(c == DC - 1))
                        last = None
                        for n in range(TN):
                            last = te.matmul(sqps[:, nsl(n)], ones_col_bf[:],
                                             sqb[:, c % 2, nsl(n)],
                                             start=(c == 0), stop=(c == DC - 1))
                        sTE.inc(last)
                    te.wait_ge(sVE.h, stats2_m)
                    for n in range(TN):
                        te.matmul(brps[:, nsl(n)], ones_row_f[:],
                                  rsd[:, nsl(n)], start=True, stop=True)
                    last = None
                    for n in range(TN):
                        last = te.matmul(bmps[:, nsl(n)], ones_row_f[:],
                                         mr[:, nsl(n)], start=True, stop=True)
                    sTE.inc(last)
                blk.tensor(f_te)

        def emit_proj(wt, jobs, kind, cag_in_t=None, koff=0, qb=None):
            rope = kind.startswith("rope")
            to_cag = kind.endswith("k")
            qbuf = qb
            with (
                nc.sbuf_tensor(uid("pj_w"), [128, 2, DC, 128], BF16) as wqk,
                nc.sbuf_tensor(uid("pj_kst"), [128, 2, TOK], BF16) as kst,
                nc.sbuf_tensor(uid("pj_rt"), [128, 2, TOK], F32) as rtp,
                nc.sbuf_tensor(uid("pj_ut"), [128, 2, TOK], F32) as utp,
                nc.sbuf_tensor(uid("pj_cos"), [128, TOK], F32) as cosb,
                nc.sbuf_tensor(uid("pj_sin"), [128, TOK], F32) as sinb,
                nc.psum_tensor(uid("pj_ps"), [128, 2, TOK], F32) as pps,
                nc.Block() as blk,
            ):
                n_j = len(jobs)
                ve0, sc0, te0, w0 = sVE.n, sSC.n, sTE.n, sW.n
                g0 = [gp[0].n, gp[1].n]
                s0 = [sy[0].n, sy[1].n]
                tm = lambda j: te0 + j + 1
                em0 = ve0 if rope else sc0
                em = lambda j: em0 + j + 1
                gm = lambda j: g0[j % 2] + 16 * (j // 2 + 1)
                kdma_m = lambda j: s0[j % 2] + 16 * (j // 2 + 1)

                def f_sy(syn):
                    if rope:
                        syn.dma_start(out=cosb[:], in_=cosq.ap()).then_inc(
                            sW.h, 16)
                        sW.n += 16
                        syn.dma_start(out=sinb[:], in_=sinq.ap()).then_inc(
                            sW.h, 16)
                        sW.n += 16
                    if to_cag:
                        eh = sVE.h if rope else sSC.h
                        for j, (oc, hidx) in enumerate(jobs):
                            syn.wait_ge(eh, em(j))
                            dst = cag_in_t.ap()[
                                (koff + hidx * 128) * TOK:
                                (koff + (hidx + 1) * 128) * TOK]
                            dst = dst.rearrange("(p t) -> p t", p=128)
                            i = syn.dma_start(out=dst, in_=kst[:, j % 2])
                            sy[j % 2].inc(i, 16)
                        syn.wait_ge(sy[0].h, sy[0].n)
                        syn.wait_ge(sy[1].h, sy[1].n)
                blk.sync(f_sy)

                def f_gp(g):
                    for j, (oc, hidx) in enumerate(jobs):
                        if j >= 2:
                            g.wait_ge(sTE.h, tm(j - 2))
                        i = g.dma_start(out=wqk[:, j % 2],
                                        in_=colslice(wt, oc * 128, 128))
                        gp[j % 2].inc(i, 16)
                    g.wait_ge(gp[0].h, gp[0].n)
                    g.wait_ge(gp[1].h, gp[1].n)
                blk.gpsimd(f_gp)

                def f_te(te):
                    for j, (oc, hidx) in enumerate(jobs):
                        te.wait_ge(gp[j % 2].h, gm(j))
                        if j >= 2:
                            te.wait_ge(sVE.h if rope else sSC.h, em(j - 2))
                        last = None
                        for dc in range(DC):
                            for n in range(TN):
                                last = te.matmul(
                                    pps[:, j % 2, nsl(n)], wqk[:, j % 2, dc],
                                    xn[:, dc, nsl(n)],
                                    start=(dc == 0), stop=(dc == DC - 1))
                        sTE.inc(last)
                blk.tensor(f_te)

                if rope:
                    def f_ve(ve):
                        if rope:
                            ve.wait_ge(sW.h, sW.n)
                        for j, (oc, hidx) in enumerate(jobs):
                            ve.wait_ge(sTE.h, tm(j))
                            ps = pps[:, j % 2]
                            if to_cag:
                                if j >= 2:
                                    ve.wait_ge(sy[j % 2].h, kdma_m(j - 2))
                                dst = kst[:, j % 2]
                            else:
                                dst = qbuf[:, hidx]
                            t = rtp[:, j % 2]
                            u = utp[:, j % 2]
                            ve.tensor_mul(t, ps, cosb[:])
                            ve.tensor_mul(u[0:64], ps[64:128], sinb[0:64])
                            ve.tensor_mul(u[64:128], ps[0:64], sinb[64:128])
                            ve.drain()
                            ve.tensor_sub(dst[0:64], t[0:64], u[0:64])
                            i = ve.tensor_add(dst[64:128], t[64:128],
                                              u[64:128])
                            sVE.inc(i)
                    blk.vector(f_ve)
                else:
                    def f_sc(se):
                        for j, (oc, hidx) in enumerate(jobs):
                            se.wait_ge(sTE.h, tm(j))
                            if to_cag:
                                if j >= 2:
                                    se.wait_ge(sy[j % 2].h, kdma_m(j - 2))
                                dst = kst[:, j % 2]
                            else:
                                dst = qbuf[:, hidx]
                            i = se.activation(dst, pps[:, j % 2], AF.Copy)
                            sSC.inc(i)
                    blk.scalar(f_sc)

        def emit_vproj(wsrc_fn, out_cols, voff, cag_in_t, Dv):
            NTv = min(512, out_cols)
            NOS = out_cols // NTv
            TC = TOK // 128
            with (
                nc.sbuf_tensor(uid("vp_w"), [128, 2, DC, NTv], BF16) as wv4,
                nc.sbuf_tensor(uid("vp_st"), [128, 2, NTv], BF16) as vst,
                nc.psum_tensor(uid("vp_ps"), [128, 2, NTv], F32) as vps,
                nc.Block() as blk,
            ):
                jobs = [(o, tc) for o in range(NOS) for tc in range(TC)]
                sc0, te0 = sSC.n, sTE.n
                g0 = [gp[0].n, gp[1].n]
                s0 = [sy[0].n, sy[1].n]
                tm = lambda j: te0 + j + 1
                scm = lambda j: sc0 + j + 1
                gm = lambda o: g0[o % 2] + 16 * (o // 2 + 1)
                vdma_m = lambda j: s0[j % 2] + 16 * (j // 2 + 1)

                def f_gp(g):
                    for o in range(NOS):
                        if o >= 2:
                            g.wait_ge(sTE.h, te0 + (o - 1) * TC)
                        i = g.dma_start(out=wv4[:, o % 2],
                                        in_=wsrc_fn(o * NTv, NTv))
                        gp[o % 2].inc(i, 16)
                    g.wait_ge(gp[0].h, gp[0].n)
                    g.wait_ge(gp[1].h, gp[1].n)
                blk.gpsimd(f_gp)

                def f_te(te):
                    for j, (o, tc) in enumerate(jobs):
                        if tc == 0:
                            te.wait_ge(gp[o % 2].h, gm(o))
                        if j >= 2:
                            te.wait_ge(sSC.h, scm(j - 2))
                        last = None
                        for dc in range(DC):
                            last = te.matmul(
                                vps[:, j % 2], xn[:, dc, tc * 128:(tc + 1) * 128],
                                wv4[:, o % 2, dc],
                                start=(dc == 0), stop=(dc == DC - 1))
                        sTE.inc(last)
                blk.tensor(f_te)

                def f_sc(se):
                    for j, (o, tc) in enumerate(jobs):
                        se.wait_ge(sTE.h, tm(j))
                        if j >= 2:
                            se.wait_ge(sy[j % 2].h, vdma_m(j - 2))
                        i = se.activation(vst[:, j % 2], vps[:, j % 2],
                                          AF.Copy)
                        sSC.inc(i)
                blk.scalar(f_sc)

                def f_sy(syn):
                    for j, (o, tc) in enumerate(jobs):
                        syn.wait_ge(sSC.h, scm(j))
                        dst = cag_in_t.ap()[voff:voff + TOK * Dv]
                        dst = dst.rearrange("(t d) -> t d", d=Dv)
                        dst = dst[tc * 128:(tc + 1) * 128,
                                  o * NTv:(o + 1) * NTv]
                        i = syn.dma_start(out=dst, in_=vst[:, j % 2])
                        sy[j % 2].inc(i, 16)
                    syn.wait_ge(sy[0].h, sy[0].n)
                    syn.wait_ge(sy[1].h, sy[1].n)
                blk.sync(f_sy)

        def emit_ag(cin, cout, ctr, waits, groups=None):
            with nc.Block() as blk:
                def f_gp(g):
                    for h, v in waits:
                        g.wait_ge(h, v)
                    i = g.collective_compute(
                        "AllGather", ALU.bypass,
                        replica_groups=groups or [list(range(8))],
                        ins=[cin.ap().opt()], outs=[cout.ap().opt()])
                    ctr.inc(i)
                blk.gpsimd(f_gp)

        def expansion_tiles():
            tiles = []
            for sec, src_t in ((0, gsl), (1, usl)):
                dstv = cag2_in.ap()[sec * SEC:sec * SEC + RG * INTER]
                dstv = dstv.rearrange("(p i) -> p i", i=INTER)
                r0 = 0
                while r0 < RG:
                    npr = min(128, RG - r0)
                    c0 = 0
                    while c0 < INTER:
                        w = min(FDMAX, INTER - c0)
                        src = src_t.ap()[r0:r0 + npr, c0:c0 + w]
                        dst = dstv[r0:r0 + npr, c0:c0 + w]
                        tiles.append((src, npr, w, dst))
                        c0 += w
                    r0 += npr
            amax = max(1, FDMAX // RDC)
            a0 = 0
            while a0 < IC:
                na = min(amax, IC - a0)
                src = dsl.ap().rearrange("(a p) c -> p a c", p=128)
                src = src[:, a0:a0 + na, :]
                dst = cag2_in.ap()[2 * SEC + a0 * 128 * RDC:
                                   2 * SEC + (a0 + na) * 128 * RDC]
                dst = dst.rearrange("(a p c) -> p a c", p=128, c=RDC)
                tiles.append((src, 128, na * RDC, dst))
                a0 += na
            return tiles

        def expansion_order(nt):
            order = []
            for t in range(nt):
                order.append(("load", t))
                if t >= 1:
                    order.append(("ship", t - 1))
            order.append(("ship", nt - 1))
            return order

        def expansion_gp_ops(g, tiles, eibf, ewb, ex0):
            for op, t in expansion_order(len(tiles)):
                src, npr, fd, dst = tiles[t]
                if op == "load":
                    if t >= 2:
                        g.wait_ge(sEX.h, ex0 + t - 1)
                    dstv = eibf[0:npr, t % 2, 0:fd]
                    if src.ndim == 3:
                        dstv = dstv.rearrange("p (a c) -> p a c",
                                              c=src.shape[2])
                    i = g.dma_start(out=dstv, in_=src)
                    exL[t % 2].inc(i, 16)
                else:
                    g.wait_ge(sEX.h, ex0 + t + 1)
                    srcv = ewb[0:npr, t % 2, 0:fd]
                    if dst.ndim == 3:
                        srcv = srcv.rearrange("p (a c) -> p a c",
                                              c=dst.shape[2])
                    i = g.dma_start(out=dst, in_=srcv)
                    exS[t % 2].inc(i, 16)

        def expansion_ve_tile(ve, tiles, t, eibf, ewb, etb, l0, s0):
            src, npr, fd, dst = tiles[t]
            ve.wait_ge(exL[t % 2].h, l0[t % 2] + 16 * (t // 2 + 1))
            if t >= 2:
                ve.wait_ge(exS[t % 2].h, s0[t % 2] + 16 * (t // 2 - 1 + 1))
            ib = eibf[0:npr, t % 2, 0:fd]
            wb = ewb[0:npr, t % 2, 0:fd]
            tb = etb[0:npr, 0:fd]
            dr = ((lambda: ve.drain()) if (npr < 128 or fd < 512)
                  else (lambda: None))
            ve.tensor_scalar(wb, ib, 0.0, luts[0:npr, 0:1],
                             ALU.is_equal, ALU.mult)
            i = None
            for l in range(1, 16):
                ve.tensor_scalar(tb, ib, float(l), luts[0:npr, l:l + 1],
                                 ALU.is_equal, ALU.mult)
                dr()
                i = ve.tensor_tensor(wb, wb, tb, ALU.add)
                dr()
            sEX.inc(i)

        def emit_expansion():
            tiles = expansion_tiles()
            nt = len(tiles)
            with (
                nc.sbuf_tensor(uid("ex_i"), [128, 2, FDMAX], BF16) as eibf,
                nc.sbuf_tensor(uid("ex_w"), [128, 2, FDMAX], BF16) as ewb,
                nc.sbuf_tensor(uid("ex_t"), [128, FDMAX], BF16) as etb,
                nc.Block() as blk,
            ):
                ex0 = sEX.n
                l0 = [exL[0].n, exL[1].n]
                s0 = [exS[0].n, exS[1].n]

                def f_gp(g):
                    expansion_gp_ops(g, tiles, eibf, ewb, ex0)
                    for c_ in (exL[0], exL[1], exS[0], exS[1]):
                        g.wait_ge(c_.h, c_.n)
                blk.gpsimd(f_gp)

                def f_ve(ve):
                    ve.wait_ge(sW.h, p0_w)
                    for t in range(nt):
                        expansion_ve_tile(ve, tiles, t, eibf, ewb, etb, l0, s0)
                blk.vector(f_ve)
            return [(exS[0].h, exS[0].n), (exS[1].h, exS[1].n)]

        def emit_attention(kind, expansion=False, qb=None):
            masked = kind == "b"
            qbuf = qb
            Hq = H
            cag = cag3_out if masked else cag1_out
            Lr = L3 if masked else L1
            Dv = KVD if masked else D
            voff = VOFF3 if masked else VOFF1
            cc = sC3 if masked else sC1
            groups = NKV if masked else H
            hpg = G if masked else 1
            nbuf = 1 if masked else 2
            with (
                nc.sbuf_tensor(uid("at_kh"), [128, nbuf, S], BF16) as kh,
                nc.sbuf_tensor(uid("at_vh"), [128, nbuf, KC, 128], BF16) as vh,
                nc.sbuf_tensor(uid("at_ot"), [128, TOK], BF16) as otmp,
                nc.psum_tensor(uid("at_st"), [128, 2, TOK], F32) as stp,
                nc.psum_tensor(uid("at_pv"), [128, TOK], F32) as pvp,
                nc.psum_tensor(uid("at_dn"), [1, TOK], F32) as dnp,
                nc.sbuf_tensor(uid("at_pt"), [128, KC, TOK], BF16) as ptb,
                nc.sbuf_tensor(uid("at_rv"), [1, TOK], F32) as rvec,
                ExitStack() as mst,
            ):
                msk = None
                if masked:
                    msk = mst.enter_context(
                        nc.sbuf_tensor(uid("at_mk"), [128, KC, TOK], BF16))
                tiles = None
                if expansion:
                    tiles = expansion_tiles()
                    eibf = mst.enter_context(
                        nc.sbuf_tensor(uid("ex_i"), [128, 2, FDMAX], BF16))
                    ewb = mst.enter_context(
                        nc.sbuf_tensor(uid("ex_w"), [128, 2, FDMAX], BF16))
                    etb = mst.enter_context(
                        nc.sbuf_tensor(uid("ex_t"), [128, FDMAX], BF16))
                    ex0 = sEX.n
                    exl0 = [exL[0].n, exL[1].n]
                    exs0 = [exS[0].n, exS[1].n]
                with nc.Block() as blk:
                    ve0, sc0, te0 = sVE.n, sSC.n, sTE.n
                    s0 = [sy[0].n, sy[1].n]
                    scph = KC + 1
                    veph = (KC if masked else 0) + 2
                    teph = KC + 2
                    exp_m = lambda h, kc: sc0 + h * scph + kc + 1
                    copy_m = lambda h: sc0 + h * scph + scph
                    msk_m = lambda h, kc: ve0 + h * veph + kc + 1
                    recip_m = lambda h: ve0 + h * veph + veph - 1
                    evac_m = lambda h: ve0 + h * veph + veph
                    st_m = lambda h, kc: te0 + h * teph + kc + 1
                    pv_m = lambda h: te0 + h * teph + KC + 1
                    bc_m = lambda h: te0 + h * teph + KC + 2
                    mask_w = sMK.n + (16 if masked else 0)
                    kv_m = lambda gi: (sy[gi % 2].h,
                                       s0[gi % 2] + 64 * (gi // 2 + 1))

                    def f_sy(syn):
                        if masked:
                            i = syn.dma_start(out=msk[:], in_=maskb.ap(
                            ).rearrange("(a p) t -> p a t", p=128))
                            sMK.inc(i, 16)
                        syn.wait_ge(cc.h, cc.n)
                        for gi in range(groups):
                            if gi >= nbuf:
                                syn.wait_ge(sTE.h,
                                            pv_m((gi - nbuf) * hpg + hpg - 1))
                            gb = gi % nbuf
                            for half in range(2):
                                src = cag.ap()[half,
                                               gi * 128 * TOK:
                                               (gi + 1) * 128 * TOK]
                                src = src.rearrange("(p t) -> p t", p=128)
                                i = syn.dma_start(
                                    out=kh[:, gb, half * TOK:(half + 1) * TOK],
                                    in_=src)
                                sy[gi % 2].inc(i, 16)
                            for half in range(2):
                                src = cag.ap()[half, voff:voff + TOK * Dv]
                                src = src.rearrange(
                                    "(a p d) -> p a d", p=128, d=Dv)
                                src = src[:, :, gi * 128:(gi + 1) * 128]
                                i = syn.dma_start(
                                    out=vh[:, gb, half * KCh:(half + 1) * KCh],
                                    in_=src)
                                sy[gi % 2].inc(i, 16)
                        syn.wait_ge(sy[0].h, sy[0].n)
                        syn.wait_ge(sy[1].h, sy[1].n)
                    blk.sync(f_sy)

                    if expansion:
                        def f_gp(g):
                            expansion_gp_ops(g, tiles, eibf, ewb, ex0)
                            for c_ in (exL[0], exL[1], exS[0], exS[1]):
                                g.wait_ge(c_.h, c_.n)
                            i = g.collective_compute(
                                "AllGather", ALU.bypass,
                                replica_groups=[list(range(8))],
                                ins=[cag2_in.ap().opt()],
                                outs=[cag2_out.ap().opt()])
                            sC2.inc(i)
                        blk.gpsimd(f_gp)

                    def f_te(te):
                        for h in range(Hq):
                            gi = h // hpg
                            gb = gi % nbuf
                            te.wait_ge(*kv_m(gi))
                            for kc in range(KC):
                                if kc >= 2:
                                    te.wait_ge(sSC.h, exp_m(h, kc - 2))
                                last = None
                                for n in range(TN):
                                    last = te.matmul(
                                        stp[:, kc % 2, nsl(n)],
                                        kh[:, gb, kc * 128:(kc + 1) * 128],
                                        qbuf[:, h, nsl(n)],
                                        start=True, stop=True)
                                sTE.inc(last)
                            if h >= 1:
                                te.wait_ge(sVE.h, evac_m(h - 1))
                            last = None
                            for kc in range(KC):
                                te.wait_ge(sSC.h, exp_m(h, kc))
                                if masked:
                                    te.wait_ge(sVE.h, msk_m(h, kc))
                                for n in range(TN):
                                    te.matmul(pvp[:, nsl(n)],
                                              vh[:, gb, kc],
                                              ptb[:, kc, nsl(n)],
                                              start=(kc == 0),
                                              stop=(kc == KC - 1))
                                for n in range(TN):
                                    last = te.matmul(dnp[:, nsl(n)],
                                                     ones_col_bf[:],
                                                     ptb[:, kc, nsl(n)],
                                                     start=(kc == 0),
                                                     stop=(kc == KC - 1))
                            sTE.inc(last)
                            te.wait_ge(sVE.h, recip_m(h))
                            te.wait_ge(sSC.h, copy_m(h))
                            last = None
                            for n in range(TN):
                                last = te.matmul(pvp[:, nsl(n)],
                                                 ones_row_f[:],
                                                 rvec[:, nsl(n)],
                                                 start=True, stop=True)
                            sTE.inc(last)
                    blk.tensor(f_te)

                    def f_sc(se):
                        for h in range(Hq):
                            for kc in range(KC):
                                se.wait_ge(sTE.h, st_m(h, kc))
                                i = se.activation(ptb[:, kc], stp[:, kc % 2],
                                                  AF.Exp, bias=negshift[:],
                                                  scale=SCALE)
                                sSC.inc(i)
                            se.wait_ge(sTE.h, pv_m(h))
                            if h >= 1:
                                se.wait_ge(sVE.h, evac_m(h - 1))
                            i = se.activation(otmp[:], pvp[:], AF.Copy)
                            sSC.inc(i)
                    blk.scalar(f_sc)

                    tptr = [0]

                    def f_ve(ve):
                        if masked:
                            ve.wait_ge(sMK.h, mask_w)
                        if expansion:
                            ve.wait_ge(sW.h, p0_w)
                        for h in range(Hq):
                            if masked:
                                for kc in range(KC):
                                    ve.wait_ge(sSC.h, exp_m(h, kc))
                                    i = ve.tensor_mul(ptb[:, kc], ptb[:, kc],
                                                      msk[:, kc])
                                    sVE.inc(i)
                            ve.wait_ge(sTE.h, pv_m(h))
                            i = ve.reciprocal(rvec[:], dnp[:])
                            sVE.inc(i)
                            ve.wait_ge(sTE.h, bc_m(h))
                            i = ve.tensor_mul(qbuf[:, h], otmp[:], pvp[:])
                            sVE.inc(i)
                            if expansion:
                                while tptr[0] < min(len(tiles),
                                                    (3 * (h + 1) + 1) // 2):
                                    expansion_ve_tile(ve, tiles, tptr[0],
                                                      eibf, ewb, etb,
                                                      exl0, exs0)
                                    tptr[0] += 1
                        if expansion:
                            for t in range(tptr[0], len(tiles)):
                                expansion_ve_tile(ve, tiles, t, eibf, ewb,
                                                  etb, exl0, exs0)
                    blk.vector(f_ve)

        def emit_outproj(wt, qb=None):
            qbuf = qb
            with (
                nc.sbuf_tensor(uid("op_w"), [128, 2, DC, 128], BF16) as wti,
                nc.psum_tensor(uid("op_ps"), [128, 2, TOK], F32) as ops,
                nc.Block() as blk,
            ):
                ve0, te0 = sVE.n, sTE.n
                g0 = [gp[0].n, gp[1].n]
                tm = lambda j: te0 + j + 1
                em = lambda j: ve0 + j + 1
                gm = lambda j: g0[j % 2] + 16 * (j // 2 + 1)

                def f_gp(g):
                    for j in range(DC):
                        if j >= 2:
                            g.wait_ge(sTE.h, tm(j - 2))
                        i = g.dma_start(out=wti[:, j % 2],
                                        in_=colslice(wt, j * 128, 128))
                        gp[j % 2].inc(i, 16)
                    g.wait_ge(gp[0].h, gp[0].n)
                    g.wait_ge(gp[1].h, gp[1].n)
                blk.gpsimd(f_gp)

                def f_te(te):
                    for j in range(DC):
                        te.wait_ge(gp[j % 2].h, gm(j))
                        if j >= 2:
                            te.wait_ge(sVE.h, em(j - 2))
                        last = None
                        for oc in range(H):
                            for n in range(TN):
                                last = te.matmul(
                                    ops[:, j % 2, nsl(n)], wti[:, j % 2, oc],
                                    qbuf[:, oc, nsl(n)],
                                    start=(oc == 0), stop=(oc == H - 1))
                        sTE.inc(last)
                blk.tensor(f_te)

                def f_ve(ve):
                    for j in range(DC):
                        ve.wait_ge(sTE.h, tm(j))
                        i = ve.tensor_add(xsb[:, j], xsb[:, j],
                                          ops[:, j % 2])
                        sVE.inc(i)
                blk.vector(f_ve)

        def emit_ffn(wait_c2, store_y=False):
            fsl = lambda n: slice(n * FT, (n + 1) * FT)
            with nc.sbuf_tensor(uid("ff_h"), [128, IC, TOK], BF16) as hbuf:
                jobs = [(oc, th) for oc in range(IC) for th in range(FTN)]
                with (
                    nc.sbuf_tensor(uid("ff_wg"), [128, 2, DC, 128], BF16) as wg,
                    nc.sbuf_tensor(uid("ff_wu"), [128, 2, DC, 128], BF16) as wu,
                    nc.sbuf_tensor(uid("ff_st"), [128, 4, FT], BF16) as stmp,
                    nc.psum_tensor(uid("ff_g"), [128, 4, FT], F32) as gps,
                    nc.psum_tensor(uid("ff_u"), [128, 4, FT], F32) as ups,
                    nc.Block() as blk,
                ):
                    ve0, sc0, te0 = sVE.n, sSC.n, sTE.n
                    s0 = [sy[0].n, sy[1].n]
                    g0 = [gp[0].n, gp[1].n]
                    tm = lambda j: te0 + j + 1
                    sim = lambda j: sc0 + j + 1
                    hm = lambda j: ve0 + j + 1
                    ndg = 8
                    wms = lambda oc: (sy[oc % 2].h,
                                      s0[oc % 2] + 16 * ndg * (oc // 2 + 1))
                    wmg = lambda oc: (gp[oc % 2].h,
                                      g0[oc % 2] + 16 * ndg * (oc // 2 + 1))

                    def gu_loads(q, ctrs, sec, dstb):
                        if wait_c2:
                            q.wait_ge(sC2.h, sC2.n)
                        for oc in range(IC):
                            if oc >= 2:
                                q.wait_ge(sTE.h, tm((oc - 2) * FTN + FTN - 1))
                            if RG % 128 == 0:
                                rgc = RG // 128
                                for r in range(8):
                                    src = cag2_out.ap()[
                                        r, sec * SEC:sec * SEC + RG * INTER]
                                    src = src.rearrange("(a p i) -> p a i",
                                                        p=128, i=INTER)
                                    src = src[:, :, oc * 128:(oc + 1) * 128]
                                    dst = dstb[:, oc % 2,
                                               r * rgc:(r + 1) * rgc]
                                    i = q.dma_start(out=dst, in_=src)
                                    ctrs[oc % 2].inc(i, 16)
                            else:
                                for r in range(8):
                                    src = cag2_out.ap()[
                                        r, sec * SEC:sec * SEC + RG * INTER]
                                    src = src.rearrange("(a i) -> a i",
                                                        i=INTER)
                                    src = src[:, oc * 128:(oc + 1) * 128]
                                    row0 = r * RG
                                    dst = dstb[row0 % 128:row0 % 128 + RG,
                                               oc % 2, row0 // 128]
                                    i = q.dma_start(out=dst, in_=src)
                                    ctrs[oc % 2].inc(i, 16)
                        q.wait_ge(ctrs[0].h, ctrs[0].n)
                        q.wait_ge(ctrs[1].h, ctrs[1].n)

                    blk.sync(lambda syn: gu_loads(syn, sy, 0, wg))
                    blk.gpsimd(lambda g: gu_loads(g, gp, 1, wu))

                    def f_te(te):
                        for j, (oc, th) in enumerate(jobs):
                            if th == 0:
                                te.wait_ge(*wms(oc))
                                te.wait_ge(*wmg(oc))
                            if j >= 4:
                                te.wait_ge(sSC.h, sim(j - 4))
                                te.wait_ge(sVE.h, hm(j - 4))
                            for dc in range(DC):
                                te.matmul(gps[:, j % 4], wg[:, oc % 2, dc],
                                          xn[:, dc, fsl(th)],
                                          start=(dc == 0), stop=(dc == DC - 1))
                            last = None
                            for dc in range(DC):
                                last = te.matmul(
                                    ups[:, j % 4], wu[:, oc % 2, dc],
                                    xn[:, dc, fsl(th)],
                                    start=(dc == 0), stop=(dc == DC - 1))
                            sTE.inc(last)
                    blk.tensor(f_te)

                    def f_sc(se):
                        for j, (oc, th) in enumerate(jobs):
                            se.wait_ge(sTE.h, tm(j))
                            i = se.activation(stmp[:, j % 4],
                                              gps[:, j % 4], AF.Silu)
                            sSC.inc(i)
                    blk.scalar(f_sc)

                    def f_ve(ve):
                        for j, (oc, th) in enumerate(jobs):
                            ve.wait_ge(sSC.h, sim(j))
                            i = ve.tensor_mul(hbuf[:, oc, fsl(th)],
                                              stmp[:, j % 4],
                                              ups[:, j % 4])
                            sVE.inc(i)
                    blk.vector(f_ve)

                with (
                    nc.sbuf_tensor(uid("ff_wd"), [128, 2, IC, 128], BF16) as wd,
                    nc.psum_tensor(uid("ff_d"), [128, 4, FT], F32) as dps,
                    nc.Block() as blk,
                ):
                    djobs = [(dj, th) for dj in range(DC) for th in range(FTN)]
                    ve0, te0 = sVE.n, sTE.n
                    s0 = [sy[0].n, sy[1].n]
                    tm = lambda j: te0 + j + 1
                    em = lambda j: ve0 + j + 1

                    def down_dmas(dj, dstb):
                        out = []
                        col0 = dj * 128
                        done = 0
                        while done < 128:
                            c = col0 + done
                            r = c // RDC
                            lc = c % RDC
                            w = min(RDC - lc, 128 - done)
                            src = cag2_out.ap()[r, 2 * SEC:3 * SEC]
                            src = src.rearrange("(a p c) -> p a c",
                                                p=128, c=RDC)
                            src = src[:, :, lc:lc + w]
                            dst = dstb[:, :, done:done + w]
                            out.append((dst, src))
                            done += w
                        return out

                    def f_sy(syn):
                        for dj in range(DC):
                            if dj >= 2:
                                syn.wait_ge(sTE.h,
                                            tm((dj - 2) * FTN + FTN - 1))
                            for dst, src in down_dmas(dj, wd[:, dj % 2]):
                                i = syn.dma_start(out=dst, in_=src)
                                sy[dj % 2].inc(i, 16)
                        syn.wait_ge(sy[0].h, sy[0].n)
                        syn.wait_ge(sy[1].h, sy[1].n)
                    blk.sync(f_sy)

                    wm_pred = []
                    acc = [s0[0], s0[1]]
                    for dj in range(DC):
                        acc[dj % 2] += 16 * len(down_dmas(dj, wd[:, 0]))
                        wm_pred.append((sy[dj % 2].h, acc[dj % 2]))

                    def f_te(te):
                        for j, (dj, th) in enumerate(djobs):
                            if th == 0:
                                te.wait_ge(*wm_pred[dj])
                            if j >= 4:
                                te.wait_ge(sVE.h, em(j - 4))
                            last = None
                            for oc in range(IC):
                                last = te.matmul(
                                    dps[:, j % 4], wd[:, dj % 2, oc],
                                    hbuf[:, oc, fsl(th)],
                                    start=(oc == 0), stop=(oc == IC - 1))
                            sTE.inc(last)
                    blk.tensor(f_te)

                    def f_ve(ve):
                        for j, (dj, th) in enumerate(djobs):
                            ve.wait_ge(sTE.h, tm(j))
                            i = ve.tensor_add(xsb[:, dj, fsl(th)],
                                              xsb[:, dj, fsl(th)],
                                              dps[:, j % 4])
                            sVE.inc(i)
                    blk.vector(f_ve)

                    if store_y:
                        def f_sc(se):
                            for dj in range(DC):
                                se.wait_ge(sVE.h,
                                           em(dj * FTN + FTN - 1))
                                dst = y.ap().rearrange(
                                    "(c p) t -> p c t", p=128)[:, dj]
                                i = se.dma_start(out=dst, in_=xsb[:, dj])
                                sW.inc(i, 16)
                        blk.scalar(f_sc)

        first_ln = [True]

        def ln():
            emit_ln(first=first_ln[0])
            first_ln[0] = False

        need_ffn = ("f1" in stages) or ("f2" in stages)
        if "a" in stages:
            ln()
            with nc.sbuf_tensor("qbuf_a", [128, H, TOK], BF16) as qa:
                jobs_k = [(H + h, h) for h in range(H)]
                emit_proj(wqkvT, jobs_k, "ropek", cag1_in, 0)
                emit_vproj(lambda c0, w: colslice(wqkvT, 2 * H * 128 + c0, w),
                           D, VOFF1, cag1_in, D)
                emit_ag(cag1_in, cag1_out, sC1,
                        [(sy[0].h, sy[0].n), (sy[1].h, sy[1].n)],
                        groups=PAIRS)
                jobs_qk = [(h, h) for h in range(H)]
                emit_proj(wqkvT, jobs_qk, "ropeq", qb=qa)
                emit_attention("a", expansion=need_ffn, qb=qa)
                emit_outproj(woT, qb=qa)
        elif need_ffn:
            exp_waits = emit_expansion()
            emit_ag(cag2_in, cag2_out, sC2, exp_waits)
        if "f1" in stages:
            ln()
            emit_ffn(wait_c2=True)
        if "b" in stages:
            ln()
            with nc.sbuf_tensor("qbuf_b", [128, H, TOK], BF16) as qb_:
                jobs_kb = [(g, g) for g in range(NKV)]
                emit_proj(wkb, jobs_kb, "copyk", cag3_in, 0)
                emit_vproj(lambda c0, w: wvb.ap().rearrange(
                    "(c p) o -> p c o", p=128)[:, :, c0:c0 + w],
                    KVD, VOFF3, cag3_in, KVD)
                emit_ag(cag3_in, cag3_out, sC3,
                        [(sy[0].h, sy[0].n), (sy[1].h, sy[1].n)],
                        groups=PAIRS)
                jobs_qb = [(h, h) for h in range(H)]
                emit_proj(wqb, jobs_qb, "copyq", qb=qb_)
                emit_attention("b", qb=qb_)
                emit_outproj(wob, qb=qb_)
        if "f2" in stages:
            ln()
            emit_ffn(wait_c2=("f1" not in stages),
                     store_y=True)

        with nc.Block() as blk:
            def f_sy(sy):
                if "f2" not in stages:
                    i = sy.dma_start(out=y.ap().rearrange(
                        "(c p) t -> p c t", p=128), in_=xsb[:])
                    sW.inc(i, 16)
                sy.wait_ge(sW.h, sW.n)
            blk.sync(f_sy)

    if verbose:
        print(f"instructions: {nc.n_instructions()}")
    return nc



def _rope_tables(S):
    inv = 1.0 / (10000.0 ** (np.arange(0, 128, 2, dtype=np.float32) / 128.0))
    t = np.arange(S, dtype=np.float32)
    fr = np.einsum("i,j->ij", t, inv)
    emb = np.concatenate([fr, fr], axis=-1)
    return np.cos(emb).astype(np.float32), np.sin(emb).astype(np.float32)


def make_in_maps(cfg, x, w_qkv, w_out, Wq, Wk, Wv, Wo, lut, gate_idx, up_idx,
                 down_idx):
    D, TOK, S = cfg.D, cfg.TOK, cfg.S
    f32 = lambda a: np.ascontiguousarray(np.asarray(a, dtype=np.float32))
    i32 = lambda a: np.ascontiguousarray(np.asarray(a, dtype=np.int32))
    wqkvT = f32(np.asarray(w_qkv).T)
    woT = f32(np.asarray(w_out).T)
    wq, wk, wv, wo = f32(Wq), f32(Wk), f32(Wv), f32(Wo)
    gT = i32(np.asarray(gate_idx).T)
    uT = i32(np.asarray(up_idx).T)
    dT = i32(np.asarray(down_idx).T)
    lutbc = f32(np.tile(np.asarray(lut, dtype=np.float32)[None, :], (128, 1)))
    cos, sin = _rope_tables(S)
    in_maps = []
    for c in range(8):
        b, hh = c // 2, c % 2
        tok = slice(hh * TOK, (hh + 1) * TOK)
        kk = np.arange(S, dtype=np.int64)[:, None]
        qq = hh * TOK + np.arange(TOK, dtype=np.int64)[None, :]
        mask = (kk <= qq).astype(ml_dtypes.bfloat16)
        in_maps.append({
            "xin": f32(np.asarray(x)[b, tok, :].T),
            "wqkvT": wqkvT, "woT": woT,
            "wqb": wq, "wkb": wk, "wvb": wv, "wob": wo,
            "cosq": f32(cos[tok, :].T), "sinq": f32(sin[tok, :].T),
            "maskb": np.ascontiguousarray(mask),
            "lutin": lutbc,
            "gsl": np.ascontiguousarray(gT[c * cfg.RG:(c + 1) * cfg.RG, :]),
            "usl": np.ascontiguousarray(uT[c * cfg.RG:(c + 1) * cfg.RG, :]),
            "dsl": np.ascontiguousarray(dT[:, c * cfg.RDC:(c + 1) * cfg.RDC]),
        })
    return in_maps


def assemble(cfg, results):
    B, S, D, TOK = cfg.B, cfg.S, cfg.D, cfg.TOK
    out = np.empty((B, S, D), dtype=np.float32)
    for c in range(8):
        b, hh = c // 2, c % 2
        out[b, hh * TOK:(hh + 1) * TOK, :] = results[c]["y"].T
    return out


def kernel(L0_attn_a_x, L0_attn_a_w_qkv, L0_attn_a_w_out, L0_attn_b_Wq,
           L0_attn_b_Wk, L0_attn_b_Wv, L0_attn_b_Wo, L0_ffn_lut,
           L0_ffn_gate_idx, L0_ffn_up_idx, L0_ffn_down_idx):
    cfg = FULL
    nc = build(cfg)
    in_maps = make_in_maps(cfg, L0_attn_a_x, L0_attn_a_w_qkv, L0_attn_a_w_out,
                           L0_attn_b_Wq, L0_attn_b_Wk, L0_attn_b_Wv,
                           L0_attn_b_Wo, L0_ffn_lut, L0_ffn_gate_idx,
                           L0_ffn_up_idx, L0_ffn_down_idx)
    res = run_bass_kernel_spmd(nc, in_maps, core_ids=list(range(8)))
    return assemble(cfg, res.results)
